# version 07
# baseline (speedup 1.0000x reference)
import math
import numpy as np
import concourse.bass as bass
import concourse.mybir as mybir
from concourse.bass_utils import run_bass_kernel_spmd

F32 = mybir.dt.float32
BF16 = mybir.dt.bfloat16
AF = mybir.ActivationFunctionType
ALU = mybir.AluOpType
AX = mybir.AxisListType
I32 = mybir.dt.int32

NCORES = 8
B = 2
T = 8192
W = 2048
D = 1024
PI = math.pi


class Seq:
    ENGS = ("sync", "gpsimd", "vector", "scalar", "tensor")

    def __init__(self):
        self.steps = []
        self.cnt = {e: 0 for e in self.ENGS}
        self.floor = {e: 0 for e in self.ENGS}

    def set_floor(self):
        self.floor = dict(self.cnt)

    def snap(self):
        return dict(self.cnt)

    FORCE_SERIAL = False

    def _rec(self, eng, fn, waits, inc):
        if Seq.FORCE_SERIAL:
            waits = None
        if waits is None:
            waits = {e: c for e, c in self.cnt.items() if c > 0}
        else:
            waits = dict(waits)
            for e2, c2 in self.floor.items():
                if e2 != eng and c2 > waits.get(e2, 0):
                    waits[e2] = c2
            if eng != "tensor" and self.cnt[eng] > waits.get(eng, 0):
                waits[eng] = self.cnt[eng]
            waits = {e: c for e, c in waits.items() if c > 0}
        self.steps.append((eng, fn, waits, inc))
        self.cnt[eng] += inc
        return dict(self.cnt)

    def op(self, eng, fn, waits=None):
        return self._rec(eng, fn, waits, 1)

    def dma(self, eng, fn, waits=None):
        return self._rec(eng, fn, waits, 16)

    def emit(self, blk, sems):
        final = dict(self.cnt)
        by_eng = {}
        for st in self.steps:
            by_eng.setdefault(st[0], []).append(st)

        def make_body(eng_name, eng_steps):
            def body(eng):
                watermark = {e: 0 for e in self.ENGS}
                for (_, fn, waits, inc) in eng_steps:
                    for tgt, val in waits.items():
                        if val > watermark[tgt]:
                            eng.wait_ge(sems[tgt], val)
                            watermark[tgt] = val
                    inst = fn(eng)
                    if inst is None:
                        raise RuntimeError("step fn returned None")
                    inst.then_inc(sems[eng_name], inc)
                for tgt, val in final.items():
                    if tgt != eng_name and val > watermark.get(tgt, 0):
                        eng.wait_ge(sems[tgt], val)
            return body

        for name, eng_steps in by_eng.items():
            getattr(blk, name)(make_body(name, eng_steps))


class Prog:

    def __init__(self, name):
        self.nc = bass.Bass("TRN2", target_bir_lowering=False, debug=False)
        self.q = Seq()
        self._ctx = []
        self._scopes = [[]]
        self._uid = 0

    def sb(self, name, shape, dtype):
        cm = self.nc.sbuf_tensor(name, shape, dtype)
        h = cm.__enter__()
        self._scopes[-1].append(cm)
        return h

    def const(self, val):
        if not hasattr(self, "_consts"):
            self._consts = {}
        if val in self._consts:
            return
        t = self.sb(f"constap_{len(self._consts)}", [128, 1], F32)
        self.q.op("gpsimd", lambda e: e.memset(t[:], float(val)))
        self.nc.const_aps.register(F32, float(val), t[:, 0:1])
        self._consts[val] = t

    def push(self):
        self._scopes.append([])

    def pop(self):
        self.q.set_floor()
        for cm in reversed(self._scopes.pop()):
            cm.__exit__(None, None, None)

    def ps(self, name, shape, dtype=F32):
        cm = self.nc.psum_tensor(name, shape, dtype)
        t = cm.__enter__()
        self._ctx.append(cm)
        return t

    def din(self, name, shape, dtype=F32):
        return self.nc.dram_tensor(name, shape, dtype, kind="ExternalInput")

    def dout(self, name, shape, dtype=F32):
        return self.nc.dram_tensor(name, shape, dtype, kind="ExternalOutput")

    def dtmp(self, name, shape, dtype=F32):
        return self.nc.dram_tensor(name, shape, dtype, kind="Internal")

    def finish(self):
        nc = self.nc
        with (
            nc.semaphore("s_sync") as s0,
            nc.semaphore("s_gpsimd") as s1,
            nc.semaphore("s_vector") as s2,
            nc.semaphore("s_scalar") as s3,
            nc.semaphore("s_tensor") as s4,
            nc.Block() as blk,
        ):
            sems = {"sync": s0, "gpsimd": s1, "vector": s2, "scalar": s3,
                    "tensor": s4}
            self.q.emit(blk, sems)
        while self._scopes:
            for cm in reversed(self._scopes.pop()):
                cm.__exit__(None, None, None)
        for cm in reversed(self._ctx):
            cm.__exit__(None, None, None)
        return nc



def load_w_bf16(p, name, dram, kdim, mdim):
    assert kdim % 128 == 0
    sb = p.sb(name, [128, kdim // 128, mdim], BF16)
    slow = (kdim * mdim * 4) < 65536
    p.q.dma("gpsimd", lambda e: e.dma_start(
        out=sb[:], in_=dram.rearrange("(kc p) m -> p kc m", p=128),
        allow_slow_non_contiguous=slow))
    return sb


def load_fm_f32(p, name, dram, ddim, tdim, cast=None):
    dt = cast or F32
    sb = p.sb(name, [128, ddim // 128, tdim], dt)
    eng = "gpsimd" if cast else "sync"
    p.q.dma(eng, lambda e: e.dma_start(
        out=sb[:], in_=dram.rearrange("(kc p) m -> p kc m", p=128)))
    return sb


def emit_cast(p, dst, src, nchunk, tdim, eng="vector"):
    p.q.op(eng, lambda e: e.tensor_copy(dst[:], src[:]))


def emit_matmul_fm(p, out_sb, w_sb, x_sb, KC, MC, TT, psums, tile_t=512,
                   evac=None, x_chunks=None, w_col_off=0, entry=None):
    q = p.q
    if not isinstance(psums, (list, tuple)):
        psums = [psums]
    if entry is None:
        entry = q.snap()
    ntt = (TT + tile_t - 1) // tile_t
    xch = x_chunks if x_chunks is not None else list(range(KC))
    units = [(mc, it) for mc in range(MC) for it in range(ntt)]
    marks = {}
    for i, (mc, it) in enumerate(units):
        t0, t1 = it * tile_t, min((it + 1) * tile_t, TT)
        psum = psums[i % len(psums)]
        waits = dict(entry)
        if i >= len(psums):
            eng_r, cnt_r = marks[i - len(psums)]
            waits[eng_r] = max(waits.get(eng_r, 0), cnt_r)

        def mm2(e, mc=mc, t0=t0, t1=t1, psum=psum):
            for ki, kc in enumerate(xch):
                inst = e.matmul(
                    psum[:, 0:t1 - t0],
                    w_sb[:, kc, w_col_off + mc * 128:w_col_off + (mc + 1) * 128],
                    x_sb[:, kc, t0:t1],
                    start=(ki == 0), stop=(ki == len(xch) - 1))
            return inst
        q.op("tensor", mm2, waits=waits)
        ew = {"tensor": q.cnt["tensor"]}
        if evac is not None:
            marks[i] = evac(mc, (t0, t1), psum, ew)
        else:
            q.op("vector", lambda e, mc=mc, t0=t0, t1=t1, psum=psum:
                 e.tensor_copy(out_sb[:, mc, t0:t1], psum[:, 0:t1 - t0]),
                 waits=ew)
            marks[i] = ("vector", q.cnt["vector"])


def emit_ones_row(p, name, n=128, dtype=BF16):
    sb = p.sb(name, [1, n], dtype)
    p.q.op("gpsimd", lambda e: e.memset(sb[:], 1.0))
    return sb


def emit_ones_col(p, name, np_=128, dtype=BF16):
    sb = p.sb(name, [np_, 1], dtype)
    p.q.op("gpsimd", lambda e: e.memset(sb[:], 1.0))
    return sb


class LNHelper:

    def __init__(self, p, tdim, tag, psA, ps1):
        self.p = p
        self.tdim = tdim
        self.psA = psA
        self.ps1 = ps1
        self.onesc = p.sb(f"ln_onesc_{tag}", [128, 1], F32)
        p.q.op("gpsimd", lambda e: e.memset(self.onesc[:], 1.0))
        self.ones1 = p.sb(f"ln_ones1_{tag}", [1, 128], F32)
        p.q.op("gpsimd", lambda e: e.memset(self.ones1[:], 1.0))
        self.stats = p.sb(f"ln_stats_{tag}", [1, 3 * tdim], F32)
        self.sq = p.sb(f"ln_sq_{tag}", [128, 512], F32)
        self.mub = p.sb(f"ln_mub_{tag}", [128, tdim], F32)
        self.rstdb = p.sb(f"ln_rstdb_{tag}", [128, tdim], F32)

    def emit_w(self, x_sb, out_sb, DC, w, eps=1e-5):
        saved = self.tdim
        self.emit(x_sb, out_sb, DC, eps=eps, w=w)
        self.tdim = saved

    def emit(self, x_sb, out_sb, DC, eps=1e-5, w=None):
        p, q = self.p, self.p.q
        tdim = w if w is not None else self.tdim
        nd = DC * 128
        p.const(float(eps))
        for it in range(0, tdim, 512):
            t1 = min(it + 512, tdim)

            def mm_mu(e, it=it, t1=t1):
                for kc in range(DC):
                    inst = e.matmul(self.ps1[:, 0:t1 - it], self.onesc[:],
                                    x_sb[:, kc, it:t1],
                                    start=(kc == 0), stop=(kc == DC - 1))
                return inst
            q.op("tensor", mm_mu)
            q.op("vector", lambda e, it=it, t1=t1: e.tensor_scalar_mul(
                self.stats[:, it:t1], self.ps1[:, 0:t1 - it], 1.0 / nd))
        for it in range(0, tdim, 512):
            t1 = min(it + 512, tdim)
            for kc in range(DC):
                q.op("scalar", lambda e, kc=kc, it=it, t1=t1: e.activation(
                    self.sq[:, 0:t1 - it], x_sb[:, kc, it:t1], AF.Square))

                def mm_sq2(e, kc=kc, it=it, t1=t1):
                    return e.matmul(self.ps1[:, 0:t1 - it], self.onesc[:],
                                    self.sq[:, 0:t1 - it],
                                    start=(kc == 0), stop=(kc == DC - 1))
                q.op("tensor", mm_sq2)
            q.op("vector", lambda e, it=it, t1=t1: e.tensor_scalar_mul(
                self.stats[:, tdim + it:tdim + t1], self.ps1[:, 0:t1 - it],
                1.0 / nd))
        q.op("vector", lambda e: e.tensor_tensor(
            self.stats[:, 2 * tdim:3 * tdim], self.stats[:, 0:tdim],
            self.stats[:, 0:tdim], op=ALU.mult))
        q.op("vector", lambda e: e.tensor_sub(
            self.stats[:, tdim:2 * tdim], self.stats[:, tdim:2 * tdim],
            self.stats[:, 2 * tdim:3 * tdim]))
        q.op("scalar", lambda e: e.activation(
            self.stats[:, tdim:2 * tdim], self.stats[:, tdim:2 * tdim],
            AF.Ln, bias=float(eps)))
        q.op("scalar", lambda e: e.activation(
            self.stats[:, tdim:2 * tdim], self.stats[:, tdim:2 * tdim],
            AF.Exp, scale=-0.5))
        for (src_off, dst) in ((0, self.mub), (tdim, self.rstdb)):
            for it in range(0, tdim, 512):
                t1 = min(it + 512, tdim)
                q.op("tensor", lambda e, src_off=src_off, it=it, t1=t1:
                     e.matmul(self.psA[:, 0:t1 - it], self.ones1[:],
                              self.stats[:, src_off + it:src_off + t1],
                              start=True, stop=True))
                q.op("vector", lambda e, dst=dst, it=it, t1=t1:
                     e.tensor_copy(dst[:, it:t1], self.psA[:, 0:t1 - it]))
        for kc in range(DC):
            q.op("vector", lambda e, kc=kc: e.tensor_sub(
                out_sb[:, kc, 0:tdim], x_sb[:, kc, 0:tdim], self.mub[:, 0:tdim]))
            q.op("vector", lambda e, kc=kc: e.tensor_mul(
                out_sb[:, kc, 0:tdim], out_sb[:, kc, 0:tdim], self.rstdb[:, 0:tdim]))


def store_fm(p, dram, sb, DC, tdim, eng="sync"):
    p.q.dma(eng, lambda e: e.dma_start(
        out=dram.rearrange("(kc p) m -> p kc m", p=128), in_=sb[:]))



def build_A():
    p = Prog("A")
    x_d = p.din("xA", [D, W])
    wuv_d = p.din("W_uv", [D, 2 * D])
    uv_d = p.dout("uvA", [2 * D, W])

    xb = load_fm_f32(p, "xb", x_d, D, W, cast=BF16)
    wsb = load_w_bf16(p, "wuv", wuv_d, D, 2 * D)
    stage = p.sb("uv_stage", [128, 2, W], F32)
    psA = p.ps("psA", [128, 512], F32)
    psB = p.ps("psB", [128, 512], F32)

    def evac(mc, ts, psum, waits):
        t0, t1 = ts
        sg = mc % 2
        if t0 == 0 and mc >= 2:
            waits = dict(waits)
            waits["sync"] = p.q.cnt["sync"]
        p.q.op("scalar", lambda e: e.activation(
            stage[:, sg, t0:t1], psum[:, 0:t1 - t0], AF.Silu), waits=waits)
        mark = ("scalar", p.q.cnt["scalar"])
        if t1 == W:
            p.q.dma("sync", lambda e, mc=mc, sg=sg: e.dma_start(
                out=uv_d[mc * 128:(mc + 1) * 128, :], in_=stage[:, sg, :]),
                waits={"scalar": p.q.cnt["scalar"]})
        return mark
    emit_matmul_fm(p, None, wsb, xb, 8, 16, W, [psA, psB], evac=evac)
    return p.finish()


N1 = 128
N2 = 128
NFFT = N1 * N2


def build_fft_consts(p):
    q = p.q
    p.const(-PI)
    iota_row = p.sb("fc_iota_row", [1, 128], F32)
    q.op("gpsimd", lambda e: e.iota(iota_row[:], pattern=[[1, 128]], base=0,
                                    channel_multiplier=0,
                                    allow_small_or_imprecise_dtypes=True))
    iota_rowb = p.sb("fc_iota_rowb", [1, 128], BF16)
    q.op("vector", lambda e: e.tensor_copy(iota_rowb[:], iota_row[:]))
    prod = p.fb["ps1r"]
    grid = p.sb("fc_grid", [128, 128], F32)
    tmp = p.sb("fc_tmp", [128, 128], F32)
    tmpi = p.sb("fc_tmpi", [128, 128], I32)
    out = {}
    q.op("tensor", lambda e: e.matmul(prod[:, 0:128], iota_rowb[:], iota_rowb[:],
                                      start=True, stop=True))
    q.op("vector", lambda e: e.tensor_copy(grid[:], prod[:, 0:128]))

    def make_sin(dst_name, shift, modn, scale_sign, dtype, scale_extra=1.0):
        dst = p.sb(dst_name, [128, 128], dtype)
        q.op("vector", lambda e: e.tensor_scalar(
            tmp[:], grid[:], float(scale_sign), float(shift),
            op0=ALU.mult, op1=ALU.add))
        q.op("vector", lambda e: e.tensor_copy(tmpi[:], tmp[:]))
        q.op("vector", lambda e: e.tensor_single_scalar(
            tmpi[:], tmpi[:], int(modn - 1), op=ALU.bitwise_and))
        q.op("vector", lambda e: e.tensor_copy(tmp[:], tmpi[:]))
        q.op("scalar", lambda e: e.activation(
            tmp[:], tmp[:], AF.Sin, scale=2 * PI / modn, bias=-PI))
        if scale_extra != 1.0:
            q.op("vector", lambda e: e.tensor_scalar_mul(dst[:], tmp[:],
                                                         float(scale_extra)))
        else:
            q.op("vector", lambda e: e.tensor_copy(dst[:], tmp[:]))
        return dst

    out["F_r"] = make_sin("fc_F_r", 32, 128, 1, BF16, scale_extra=-1.0)
    out["F_i"] = make_sin("fc_F_i", 0, 128, 1, BF16, scale_extra=1.0)
    out["G_r"] = make_sin("fc_G_r", 32, 128, 1, BF16, scale_extra=-1.0)
    out["G_i"] = make_sin("fc_G_i", 0, 128, 1, BF16, scale_extra=-1.0)
    out["Gn_r"] = make_sin("fc_Gn_r", 32, 128, 1, BF16, scale_extra=-1.0 / NFFT)
    out["Gn_i"] = make_sin("fc_Gn_i", 0, 128, 1, BF16, scale_extra=-1.0 / NFFT)
    out["TWf_r"] = make_sin("fc_TWf_r", NFFT // 4, NFFT, 1, F32, scale_extra=-1.0)
    out["TWf_i"] = make_sin("fc_TWf_i", 0, NFFT, 1, F32, scale_extra=1.0)
    out["TWi_r"] = make_sin("fc_TWi_r", NFFT // 4, NFFT, 1, F32, scale_extra=-1.0)
    out["TWi_i"] = make_sin("fc_TWi_i", 0, NFFT, 1, F32, scale_extra=-1.0)
    out["NF_i"] = make_sin("fc_NF_i", 0, 128, 1, BF16, scale_extra=-1.0)
    out["NG_i"] = make_sin("fc_NG_i", 0, 128, 1, BF16, scale_extra=1.0)
    out["NGn_i"] = make_sin("fc_NGn_i", 0, 128, 1, BF16, scale_extra=1.0 / NFFT)
    return out


CG = 4


def emit_fft_fwd(p, C, src_sb, spec_r, spec_i, ident, tag):
    q = p.q
    fb = p.fb
    ps1r, ps1i, pst, ps2 = fb["ps1r"], fb["ps1i"], fb["pst"], fb["ps2"]
    y_r, y_i, yt_r, yt_i, scr = fb["y_r"], fb["y_i"], fb["yt_r"], fb["yt_i"], fb["scr"]
    for g0 in range(0, C, CG):
        g1 = min(g0 + CG, C)
        ng = g1 - g0
        q.op("tensor", lambda e, g0=g0, ng=ng: e.matmul(
            ps1r[:, 0:ng * 128],
            p.fc["F_r"][:], src_sb[:, g0:g0 + ng, :].rearrange("p c n -> p (c n)"),
            start=True, stop=True))
        q.op("tensor", lambda e, g0=g0, ng=ng: e.matmul(
            ps1i[:, 0:ng * 128],
            p.fc["F_i"][:], src_sb[:, g0:g0 + ng, :].rearrange("p c n -> p (c n)"),
            start=True, stop=True))
        twr = p.fc["TWf_r"][:].unsqueeze(1).broadcast_to([128, ng, 128])
        twi = p.fc["TWf_i"][:].unsqueeze(1).broadcast_to([128, ng, 128])
        yrv = y_r[:, 0:ng, :]
        yiv = y_i[:, 0:ng, :]
        psr3 = ps1r[:, 0:ng * 128].rearrange("p (c n) -> p c n", n=128)
        psi3 = ps1i[:, 0:ng * 128].rearrange("p (c n) -> p c n", n=128)
        scr3 = scr[:, 0:ng * 128].rearrange("p (c n) -> p c n", n=128)
        q.op("vector", lambda e: e.tensor_tensor(scr3, psr3, twr, op=ALU.mult))
        q.op("vector", lambda e: e.tensor_tensor(yiv, psr3, twi, op=ALU.mult))
        q.op("vector", lambda e: e.tensor_tensor(yrv, psi3, twi, op=ALU.mult))
        q.op("vector", lambda e: e.tensor_sub(yrv, scr3, yrv))
        q.op("vector", lambda e: e.tensor_tensor(scr3, psi3, twr, op=ALU.mult))
        q.op("vector", lambda e: e.tensor_add(yiv, yiv, scr3))
        for c in range(ng):
            for (ysrc, ydst) in ((y_r, yt_r), (y_i, yt_i)):
                q.op("tensor", lambda e, c=c, ysrc=ysrc: e.transpose(
                    pst[:], ysrc[:, c, :], ident[:]))
                q.op("vector", lambda e, c=c, ydst=ydst: e.tensor_copy(
                    ydst[:, c, :], pst[:]))
        ytrv = yt_r[:, 0:ng, :].rearrange("p c n -> p (c n)")
        ytiv = yt_i[:, 0:ng, :].rearrange("p c n -> p (c n)")

        def mmXr(e, ng=ng):
            e.matmul(ps2[:, 0:ng * 128], p.fc["F_r"][:], ytrv, start=True, stop=False)
            return e.matmul(ps2[:, 0:ng * 128], p.fc["NF_i"][:], ytiv, start=False, stop=True)
        q.op("tensor", mmXr)
        q.op("vector", lambda e, g0=g0, ng=ng: e.tensor_copy(
            spec_r[:, g0:g0 + ng, :].rearrange("p c n -> p (c n)"), ps2[:, 0:ng * 128]))

        def mmXi(e, ng=ng):
            e.matmul(ps2[:, 0:ng * 128], p.fc["F_i"][:], ytrv, start=True, stop=False)
            return e.matmul(ps2[:, 0:ng * 128], p.fc["F_r"][:], ytiv, start=False, stop=True)
        q.op("tensor", mmXi)
        q.op("vector", lambda e, g0=g0, ng=ng: e.tensor_copy(
            spec_i[:, g0:g0 + ng, :].rearrange("p c n -> p (c n)"), ps2[:, 0:ng * 128]))


def emit_fft_inv_real(p, C, spec_r, spec_i, out_cb, ident, tag):
    q = p.q
    fb = p.fb
    psZr, psZi, pst = fb["ps1r"], fb["ps1i"], fb["pst"]
    z_r, z_i, zt_r, zt_i, scr = fb["y_r"], fb["y_i"], fb["yt_r"], fb["yt_i"], fb["scr"]
    psX = fb["ps2"][0:64, :]
    for g0 in range(0, C, CG):
        g1 = min(g0 + CG, C)
        ng = g1 - g0
        xrv = spec_r[:, g0:g1, :].rearrange("p c n -> p (c n)")
        xiv = spec_i[:, g0:g1, :].rearrange("p c n -> p (c n)")
        def mmZr(e, ng=ng, xrv=xrv, xiv=xiv):
            e.matmul(psZr[:, 0:ng * 128], p.fc["G_r"][:], xrv, start=True, stop=False)
            return e.matmul(psZr[:, 0:ng * 128], p.fc["NG_i"][:], xiv, start=False, stop=True)
        q.op("tensor", mmZr)

        def mmZi(e, ng=ng, xrv=xrv, xiv=xiv):
            e.matmul(psZi[:, 0:ng * 128], p.fc["G_i"][:], xrv, start=True, stop=False)
            return e.matmul(psZi[:, 0:ng * 128], p.fc["G_r"][:], xiv, start=False, stop=True)
        q.op("tensor", mmZi)
        twr = p.fc["TWi_r"][:].unsqueeze(1).broadcast_to([128, ng, 128])
        twi = p.fc["TWi_i"][:].unsqueeze(1).broadcast_to([128, ng, 128])
        zrv = z_r[:, 0:ng, :]
        ziv = z_i[:, 0:ng, :]
        psr3 = psZr[:, 0:ng * 128].rearrange("p (c n) -> p c n", n=128)
        psi3 = psZi[:, 0:ng * 128].rearrange("p (c n) -> p c n", n=128)
        scr3 = scr[:, 0:ng * 128].rearrange("p (c n) -> p c n", n=128)
        q.op("vector", lambda e: e.tensor_tensor(scr3, psr3, twr, op=ALU.mult))
        q.op("vector", lambda e: e.tensor_tensor(zrv, psi3, twi, op=ALU.mult))
        q.op("vector", lambda e: e.tensor_sub(zrv, scr3, zrv))
        q.op("vector", lambda e: e.tensor_tensor(scr3, psr3, twi, op=ALU.mult))
        q.op("vector", lambda e: e.tensor_tensor(ziv, psi3, twr, op=ALU.mult))
        q.op("vector", lambda e: e.tensor_add(ziv, ziv, scr3))
        for c in range(ng):
            for (zsrc, zdst) in ((z_r, zt_r), (z_i, zt_i)):
                q.op("tensor", lambda e, c=c, zsrc=zsrc: e.transpose(
                    pst[:], zsrc[:, c, :], ident[:]))
                q.op("vector", lambda e, c=c, zdst=zdst: e.tensor_copy(
                    zdst[:, c, :], pst[:]))
        ztrv = zt_r[:, 0:ng, :].rearrange("p c n -> p (c n)")
        ztiv = zt_i[:, 0:ng, :].rearrange("p c n -> p (c n)")

        def mmX(e, ng=ng, ztrv=ztrv, ztiv=ztiv):
            e.matmul(psX[:, 0:ng * 128], p.fc["Gn_r"][:, 0:64], ztrv, start=True, stop=False)
            return e.matmul(psX[:, 0:ng * 128], p.fc["NGn_i"][:, 0:64], ztiv, start=False, stop=True)
        q.op("tensor", mmX)
        out_cb(psX, g0, ng)


def build_B():
    p = Prog("B")
    q = p.q
    u_d = p.din("uB", [B, 128, T])
    v_d = p.din("vB", [B, 128, T])
    rpe_in_w = p.din("rpe_in_w", [1, 32])
    rpe_hid_w = p.din("rpe_hid_w", [3, 32, 32])
    rpe_ln_g = p.din("rpe_ln_g", [3, 32])
    rpe_ln_b = p.din("rpe_ln_b", [3, 32])
    rpe_out_w_sl = p.din("rpe_out_w_sl", [32, 128])
    gamma_sl = p.din("decay_gamma_sl", [128, 1])
    uo_d = p.dout("uoB", [B, 128, T])
    t_dram = p.dtmp("t_dramB", [128, T], F32)

    p.fb = {
        "ps1r": p.ps("B_ps1r", [128, CG * 128], F32),
        "ps1i": p.ps("B_ps1i", [128, CG * 128], F32),
        "pst": p.ps("B_pst", [128, 128], BF16),
        "ps2": p.ps("B_ps2", [128, CG * 128], F32),
    }
    p.fb["y_r"] = p.sb("B_yr", [128, CG, 128], BF16)
    p.fb["y_i"] = p.sb("B_yi", [128, CG, 128], BF16)
    p.fb["yt_r"] = p.sb("B_ytr", [128, CG, 128], BF16)
    p.fb["yt_i"] = p.sb("B_yti", [128, CG, 128], BF16)
    p.fb["scr"] = p.sb("B_scr", [128, CG * 128], F32)
    p.fb["scr2"] = p.sb("B_scr2", [128, CG * 128], F32)
    p.const(1e-5)
    p.fc = build_fft_consts(p)
    ident = p.sb("identB", [128, 128], BF16)
    ones_t = p.sb("onesB", [128, 128], BF16)
    q.op("gpsimd", lambda e: e.memset(ones_t[:], 1.0))
    q.op("gpsimd", lambda e: e.affine_select(
        ident[:], ones_t[:], pattern=[[1, 128]], compare_op=ALU.is_equal,
        fill=0.0, base=0, channel_multiplier=-1))

    p.push()
    riw = p.sb("riw", [1, 32], F32)
    q.dma("sync", lambda e: e.dma_start(out=riw[:], in_=rpe_in_w[:]))
    lng = p.sb("lngB", [32, 3], F32)
    lnb = p.sb("lnbB", [32, 3], F32)
    q.dma("sync", lambda e: e.dma_start(out=lng[:], in_=rpe_ln_g.rearrange("l f -> f l"), allow_slow_non_contiguous=True))
    q.dma("sync", lambda e: e.dma_start(out=lnb[:], in_=rpe_ln_b.rearrange("l f -> f l"), allow_slow_non_contiguous=True))
    whid = p.sb("whidB", [32, 3, 32], BF16)
    q.dma("gpsimd", lambda e: e.dma_start(
        out=whid[:], in_=rpe_hid_w.rearrange("l k m -> k l m")))
    row = p.sb("rowB", [32, 128], BF16)
    q.dma("gpsimd", lambda e: e.dma_start(out=row[:], in_=rpe_out_w_sl[:]))
    gam = p.sb("gamB", [128, 1], F32)
    q.dma("sync", lambda e: e.dma_start(out=gam[:], in_=gamma_sl[:]))
    q.op("scalar", lambda e: e.activation(gam[:], gam[:], AF.Sigmoid))
    q.op("vector", lambda e: e.tensor_scalar(
        gam[:], gam[:], 0.01, 0.99, op0=ALU.mult, op1=ALU.add))
    q.op("scalar", lambda e: e.activation(gam[:], gam[:], AF.Ln))
    ones32 = p.sb("ones32B", [32, 1], F32)
    q.op("gpsimd", lambda e: e.memset(ones32[:], 1.0))
    ones1_32 = p.sb("ones1_32B", [1, 32], F32)
    q.op("gpsimd", lambda e: e.memset(ones1_32[:], 1.0))
    TJ = 2048
    idx_row = p.sb("idx_row", [1, TJ], F32)
    h = p.sb("hB", [32, TJ], F32)
    hb = p.sb("hBb", [32, TJ], BF16)
    hsq = p.sb("hsqB", [32, TJ], F32)
    hn = p.sb("hnB", [32, TJ], F32)
    mub = p.sb("mubB", [32, TJ], F32)
    rstdb = p.sb("rstdbB", [32, TJ], F32)
    stats = p.sb("statsB", [1, 3 * TJ], F32)
    t_ch = p.sb("t_chB", [128, TJ], F32)
    iota_t = p.sb("iota_tB", [128, TJ], F32)
    ps32 = p.fb["ps1r"][0:32, 0:512]
    ps1 = p.fb["ps1r"][0:1, 0:512]
    psb32 = p.fb["ps1i"][0:32, 0:512]
    ps128 = p.fb["ps1r"]
    for j0 in range(0, T, TJ):
        q.op("gpsimd", lambda e, j0=j0: e.iota(
            idx_row[:], pattern=[[1, TJ]], base=j0, channel_multiplier=0,
            allow_small_or_imprecise_dtypes=True))
        for it in range(0, TJ, 512):
            q.op("tensor", lambda e, it=it: e.matmul(
                ps32[:, 0:512], riw[:], idx_row[:, it:it + 512], start=True, stop=True))
            q.op("scalar", lambda e, it=it: e.activation(
                h[:, it:it + 512], ps32[:, 0:512], AF.Silu))
        for layer in range(3):
            for it in range(0, TJ, 512):
                q.op("tensor", lambda e, it=it: e.matmul(
                    ps1[:, 0:512], ones32[:], h[:, it:it + 512], start=True, stop=True))
                q.op("vector", lambda e, it=it: e.tensor_scalar_mul(
                    stats[:, it:it + 512], ps1[:, 0:512], 1.0 / 32))
            q.op("scalar", lambda e: e.activation(hsq[:], h[:], AF.Square))
            for it in range(0, TJ, 512):
                q.op("tensor", lambda e, it=it: e.matmul(
                    ps1[:, 0:512], ones32[:], hsq[:, it:it + 512], start=True, stop=True))
                q.op("vector", lambda e, it=it: e.tensor_scalar_mul(
                    stats[:, TJ + it:TJ + it + 512], ps1[:, 0:512], 1.0 / 32))
            q.op("vector", lambda e: e.tensor_tensor(
                stats[:, 2 * TJ:3 * TJ], stats[:, 0:TJ], stats[:, 0:TJ], op=ALU.mult))
            q.op("vector", lambda e: e.tensor_sub(
                stats[:, TJ:2 * TJ], stats[:, TJ:2 * TJ], stats[:, 2 * TJ:3 * TJ]))
            q.op("scalar", lambda e: e.activation(
                stats[:, TJ:2 * TJ], stats[:, TJ:2 * TJ], AF.Ln, bias=1e-5))
            q.op("scalar", lambda e: e.activation(
                stats[:, TJ:2 * TJ], stats[:, TJ:2 * TJ], AF.Exp, scale=-0.5))
            for it in range(0, TJ, 512):
                q.op("tensor", lambda e, it=it: e.matmul(
                    psb32[:, 0:512], ones1_32[:], stats[:, it:it + 512],
                    start=True, stop=True))
                q.op("vector", lambda e, it=it: e.tensor_copy(
                    mub[:, it:it + 512], psb32[:, 0:512]))
                q.op("tensor", lambda e, it=it: e.matmul(
                    psb32[:, 0:512], ones1_32[:], stats[:, TJ + it:TJ + it + 512],
                    start=True, stop=True))
                q.op("vector", lambda e, it=it: e.tensor_copy(
                    rstdb[:, it:it + 512], psb32[:, 0:512]))
            q.op("vector", lambda e: e.tensor_sub(hn[:], h[:], mub[:]))
            q.op("vector", lambda e: e.tensor_mul(hn[:], hn[:], rstdb[:]))
            q.op("vector", lambda e, layer=layer: e.tensor_scalar(
                hn[:], hn[:], lng[:, layer:layer + 1], lnb[:, layer:layer + 1],
                op0=ALU.mult, op1=ALU.add))
            q.op("vector", lambda e: e.tensor_copy(hb[:], hn[:]))
            for it in range(0, TJ, 512):
                q.op("tensor", lambda e, it=it, layer=layer: e.matmul(
                    ps32[:, 0:512], whid[:, layer, :], hb[:, it:it + 512],
                    start=True, stop=True))
                q.op("scalar", lambda e, it=it: e.activation(
                    h[:, it:it + 512], ps32[:, 0:512], AF.Silu))
        q.op("vector", lambda e: e.tensor_copy(hb[:], h[:]))
        for it in range(0, TJ, 512):
            q.op("tensor", lambda e, it=it: e.matmul(
                ps128[:, 0:512], row[:], hb[:, it:it + 512], start=True, stop=True))
            q.op("vector", lambda e, it=it: e.tensor_copy(
                t_ch[:, it:it + 512], ps128[:, 0:512]))
        q.op("gpsimd", lambda e, j0=j0: e.iota(
            iota_t[:], pattern=[[1, TJ]], base=j0, channel_multiplier=0,
            allow_small_or_imprecise_dtypes=True))
        q.op("vector", lambda e: e.tensor_scalar_mul(iota_t[:], iota_t[:], gam[:, 0:1]))
        q.op("scalar", lambda e: e.activation(iota_t[:], iota_t[:], AF.Exp))
        q.op("vector", lambda e: e.tensor_mul(t_ch[:], t_ch[:], iota_t[:]))
        q.dma("sync", lambda e, j0=j0: e.dma_start(
            out=t_dram[:, j0:j0 + TJ], in_=t_ch[:]))
    p.pop()

    fftin = p.sb("fftinB", [128, 128, 128], BF16)
    tspec_r = p.sb("tspec_r", [128, 128, 128], BF16)
    tspec_i = p.sb("tspec_i", [128, 128, 128], BF16)
    q.op("vector", lambda e: e.memset(fftin[:], 0.0))
    q.dma("gpsimd", lambda e: e.dma_start(
        out=fftin[0:64, :, :], in_=t_dram.rearrange("c (a b) -> a c b", b=128)))
    emit_fft_fwd(p, 128, fftin, tspec_r, tspec_i, ident, "t")

    vspec_r = p.sb("vspec_r", [128, 128, 128], BF16)
    vspec_i = p.sb("vspec_i", [128, 128, 128], BF16)
    u_g = p.sb("u_gB", [64, CG, 128], F32)
    uo_g = p.sb("uo_gB", [64, CG, 128], F32)
    scrp = p.fb["scr"]
    for b in range(B):
        q.op("vector", lambda e: e.memset(fftin[:], 0.0))
        q.dma("gpsimd", lambda e, b=b: e.dma_start(
            out=fftin[0:64, :, :], in_=v_d[b].rearrange("c (a b2) -> a c b2", b2=128)))
        emit_fft_fwd(p, 128, fftin, vspec_r, vspec_i, ident, f"v{b}")
        for g0 in range(0, 128, CG):
            ng = CG
            trv = tspec_r[:, g0:g0 + ng, :].rearrange("p c n -> p (c n)")
            tiv = tspec_i[:, g0:g0 + ng, :].rearrange("p c n -> p (c n)")
            vrv = vspec_r[:, g0:g0 + ng, :].rearrange("p c n -> p (c n)")
            viv = vspec_i[:, g0:g0 + ng, :].rearrange("p c n -> p (c n)")
            q.op("vector", lambda e, trv=trv, vrv=vrv: e.tensor_tensor(
                scrp[:], trv, vrv, op=ALU.mult))
            q.op("vector", lambda e, tiv=tiv, viv=viv: e.tensor_tensor(
                p.fb["scr2"][:], tiv, viv, op=ALU.mult))
            q.op("vector", lambda e, trv=trv, viv=viv: e.tensor_tensor(
                viv, trv, viv, op=ALU.mult))
            q.op("vector", lambda e, tiv=tiv, vrv=vrv: e.tensor_tensor(
                vrv, tiv, vrv, op=ALU.mult))
            q.op("vector", lambda e, viv=viv, vrv=vrv: e.tensor_add(viv, viv, vrv))
            q.op("vector", lambda e, vrv=vrv: e.tensor_sub(
                vrv, scrp[:], p.fb["scr2"][:]))

        def out_cb(psX, g0, ng, b=b):
            q.dma("sync", lambda e, g0=g0, ng=ng, b=b: e.dma_start(
                out=u_g[:, 0:ng, :],
                in_=u_d[b, g0:g0 + ng, :].rearrange("c (a b2) -> a c b2", b2=128)))
            uv_ = u_g[:, 0:ng, :].rearrange("p c n -> p (c n)")
            uov = uo_g[:, 0:ng, :].rearrange("p c n -> p (c n)")
            q.op("vector", lambda e, uv_=uv_, uov=uov, ng=ng: e.tensor_tensor(
                uov, psX[:, 0:ng * 128], uv_, op=ALU.mult))
            q.dma("sync", lambda e, g0=g0, ng=ng, b=b: e.dma_start(
                out=uo_d[b, g0:g0 + ng, :].rearrange("c (a b2) -> a c b2", b2=128),
                in_=uo_g[:, 0:ng, :]))
        emit_fft_inv_real(p, 128, vspec_r, vspec_i, out_cb, ident, f"i{b}")
    return p.finish()



def build_C1():
    p = Prog("C1")
    q = p.q
    uo_d = p.din("uoC", [D, W])
    x_d = p.din("xC", [D, W])
    wo_d = p.din("W_o", [D, D])
    wg_d = p.din("Wg", [D, 8])
    bg_d = p.din("bg", [8, 1])
    wt_d = p.din("Wt", [D, 1])
    bt_d = p.din("bt", [1, 1])
    x1_d = p.dout("x1C", [D, W])
    w_out_d = p.dout("wC", [8, W])
    dbg_g = p.dout("dbgG", [8, W])
    dbg_t = p.dout("dbgT", [1, W])

    psA = p.ps("C1_psA", [128, 512], F32)
    psB = p.ps("C1_psB", [128, 512], F32)
    ps1 = psB[0:1, :]
    ps8 = psB[0:8, :]
    pst = p.ps("C1_pst", [128, 128], F32)

    uo = load_fm_f32(p, "uo_sb", uo_d, D, W, cast=BF16)
    x = load_fm_f32(p, "x_sb", x_d, D, W)
    wo = load_w_bf16(p, "wo_sb", wo_d, D, D)

    def evac_add(mc, ts, psum, waits):
        t0, t1 = ts
        q.op("vector", lambda e: e.tensor_add(
            x[:, mc, t0:t1], x[:, mc, t0:t1], psum[:, 0:t1 - t0]), waits=waits)
        return ("vector", q.cnt["vector"])
    emit_matmul_fm(p, None, wo, uo, 8, 8, W, [psA, psB[0:128, :]], evac=evac_add)
    ln = LNHelper(p, W, "c1", psA, ps1)
    ln.emit(x, x, 8)
    store_fm(p, x1_d, x, 8, W)
    xb = uo
    emit_cast(p, xb, x, 8, W)
    wg = load_w_bf16(p, "wg_sb", wg_d, D, 8)
    wt = load_w_bf16(p, "wt_sb", wt_d, D, 1)
    bg = p.sb("C1_bg", [8, 1], F32)
    q.dma("sync", lambda e: e.dma_start(out=bg[:], in_=bg_d[:]))
    bt = p.sb("C1_bt", [1, 1], F32)
    q.dma("sync", lambda e: e.dma_start(out=bt[:], in_=bt_d[:]))
    gsc = p.sb("C1_gsc", [8, W], F32)
    thr = p.sb("C1_thr", [1, W], F32)
    for it in range(0, W, 512):
        def mmg(e, it=it):
            for kc in range(8):
                inst = e.matmul(ps8[:, 0:512], wg[:, kc, :], xb[:, kc, it:it + 512],
                                start=(kc == 0), stop=(kc == 7))
            return inst
        q.op("tensor", mmg)
        q.op("scalar", lambda e, it=it: e.activation(
            gsc[:, it:it + 512], ps8[:, 0:512], AF.Identity, bias=bg[:, 0:1]))

        def mmt(e, it=it):
            for kc in range(8):
                inst = e.matmul(ps1[:, 0:512], wt[:, kc, :], xb[:, kc, it:it + 512],
                                start=(kc == 0), stop=(kc == 7))
            return inst
        q.op("tensor", mmt)
        q.op("scalar", lambda e, it=it: e.activation(
            thr[:, it:it + 512], ps1[:, 0:512], AF.Sigmoid, bias=bt[:, 0:1]))
    q.op("vector", lambda e: e.tensor_scalar_mul(thr[:], thr[:], 0.125))
    q.dma("sync", lambda e: e.dma_start(out=dbg_g[:], in_=gsc[:]))
    q.dma("sync", lambda e: e.dma_start(out=dbg_t[:], in_=thr[:]))
    gt = p.sb("C1_gt", [128, 16, 8], F32)
    tt_ = p.sb("C1_tt", [128, 16, 1], F32)
    identF = p.sb("C1_identF", [128, 128], F32)
    q.op("gpsimd", lambda e: e.memset(identF[:], 1.0))
    q.op("gpsimd", lambda e: e.affine_select(
        identF[:], identF[:], pattern=[[1, 128]], compare_op=ALU.is_equal,
        fill=0.0, base=0, channel_multiplier=-1))
    for ti in range(16):
        q.op("tensor", lambda e, ti=ti: e.transpose(
            pst[:, 0:8], gsc[:, ti * 128:(ti + 1) * 128], identF[0:8, 0:8]))
        q.op("vector", lambda e, ti=ti: e.tensor_copy(gt[:, ti, :], pst[:, 0:8]))
        q.op("tensor", lambda e, ti=ti: e.transpose(
            pst[:, 0:1], thr[:, ti * 128:(ti + 1) * 128], identF[0:1, 0:1]))
        q.op("vector", lambda e, ti=ti: e.tensor_copy(tt_[:, ti, :], pst[:, 0:1]))
    red = p.sb("C1_red", [128, 16, 1], F32)
    q.op("vector", lambda e: e.tensor_reduce(red[:], gt[:], axis=AX.X, op=ALU.max))
    q.op("vector", lambda e: e.tensor_sub(gt[:], gt[:], red[:].broadcast_to([128, 16, 8])))
    q.op("scalar", lambda e: e.activation(gt[:], gt[:], AF.Exp))
    q.op("vector", lambda e: e.tensor_reduce(red[:], gt[:], axis=AX.X, op=ALU.add))
    q.op("vector", lambda e: e.reciprocal(red[:], red[:]))
    q.op("vector", lambda e: e.tensor_mul(gt[:], gt[:], red[:].broadcast_to([128, 16, 8])))
    q.op("vector", lambda e: e.tensor_sub(gt[:], gt[:], tt_[:].broadcast_to([128, 16, 8])))
    q.op("vector", lambda e: e.tensor_relu(gt[:], gt[:]))
    q.op("vector", lambda e: e.tensor_reduce(red[:], gt[:], axis=AX.X, op=ALU.add))
    zmask = p.sb("C1_zmask", [128, 16, 1], F32)
    q.op("vector", lambda e: e.tensor_single_scalar(zmask[:], red[:], 0.0, op=ALU.is_equal))
    q.op("vector", lambda e: e.tensor_add(red[:], red[:], zmask[:]))
    q.op("vector", lambda e: e.reciprocal(red[:], red[:]))
    q.op("vector", lambda e: e.tensor_mul(gt[:], gt[:], red[:].broadcast_to([128, 16, 8])))
    wfm = p.sb("C1_wfm", [8, W], F32)
    for ti in range(16):
        q.op("tensor", lambda e, ti=ti: e.transpose(
            pst[0:8, :], gt[:, ti, :], identF[:]))
        q.op("vector", lambda e, ti=ti: e.tensor_copy(
            wfm[:, ti * 128:(ti + 1) * 128], pst[0:8, 0:128]))
    q.dma("sync", lambda e: e.dma_start(out=w_out_d[:], in_=wfm[:]))
    return p.finish()



def build_C2():
    p = Prog("C2")
    q = p.q
    x1_d = p.din("x1C2", [D, W])
    w_d = p.din("wC2", [8, W])
    W1_d = p.din("ffW1", [8, D, 4096])
    b1_d = p.din("ffb1", [128, 8, 32])
    W2_d = p.din("ffW2", [8, 4096, D])
    b2_d = p.din("ffb2", [128, 8, 8])
    x15_d = p.dout("x15C", [D, W])

    psA = p.ps("C2_psA", [128, 512], F32)
    psB = p.ps("C2_psB", [128, 512], F32)
    psC = p.ps("C2_psC", [128, 512], F32)
    psD = p.ps("C2_psD", [128, 512], F32)
    ps1 = psC[0:1, :]

    TC = 512
    ones1 = p.sb("C2_ones1", [1, 128], BF16)
    q.op("gpsimd", lambda e: e.memset(ones1[:], 1.0))
    wrow1 = p.sb("C2_wrow1", [1, 512], F32)
    wrow1b = p.sb("C2_wrow1b", [1, 512], BF16)
    b1 = p.sb("C2_b1", [128, 8, 32], F32)
    q.dma("sync", lambda e: e.dma_start(out=b1[:], in_=b1_d[:]))
    b2 = p.sb("C2_b2", [128, 8, 8], F32)
    q.dma("sync", lambda e: e.dma_start(out=b2[:], in_=b2_d[:]))
    ln = LNHelper(p, TC, "c2", psA, ps1)

    x1bf = p.sb("C2_x1bf", [128, 8, TC], BF16)
    h = p.sb("C2_h", [128, 32, TC], BF16)
    acc = p.sb("C2_acc", [128, 8, TC], F32)
    wb = p.sb("C2_wb", [128, TC], F32)
    scr = p.sb("C2_scr", [128, 2, 512], F32)
    w1b = [p.sb(f"C2_w1_{i}", [128, 8, 2048], BF16) for i in range(2)]
    w2b2 = [p.sb(f"C2_w2_{i}", [128, 32, 512], BF16) for i in range(2)]

    w1_rel = {0: {}, 1: {}}
    w2_rel = [{}, {}]
    nexp = 0
    for tc0 in range(0, W, TC):
        e0 = q.dma("gpsimd", lambda e, tc0=tc0: e.dma_start(
            out=x1bf[:], in_=x1_d[:, tc0:tc0 + TC].rearrange("(kc p) m -> p kc m", p=128)))
        q.op("vector", lambda e: e.memset(acc[:], 0.0),
             waits={"tensor": q.cnt["tensor"], "sync": q.cnt["sync"],
                    "scalar": q.cnt["scalar"]})
        memset_mark = q.cnt["vector"]
        for ex in range(8):
            w1_loaded = {}
            for hf in range(2):
                q.dma("gpsimd", lambda e, ex=ex, hf=hf: e.dma_start(
                    out=w1b[hf][:],
                    in_=W1_d[ex, :, hf * 2048:(hf + 1) * 2048].rearrange(
                        "(kc p) m -> p kc m", p=128)),
                    waits=dict(w1_rel[hf]))
                w1_loaded[hf] = q.cnt["gpsimd"]
            marks = {}
            for i in range(32):
                mc = i
                hf = mc // 16
                waits = {"gpsimd": w1_loaded[hf]}
                if i >= 2:
                    waits["scalar"] = marks[i - 2]
                def mmh(e, mc=mc, hf=hf):
                    for kc in range(8):
                        inst = e.matmul(
                            (psA if mc % 2 == 0 else psB)[:, 0:512],
                            w1b[hf][:, kc, (mc % 16) * 128:(mc % 16 + 1) * 128],
                            x1bf[:, kc, :], start=(kc == 0), stop=(kc == 7))
                    return inst
                q.op("tensor", mmh, waits=waits)
                q.op("scalar", lambda e, mc=mc, ex=ex: e.activation(
                    h[:, mc, :], (psA if mc % 2 == 0 else psB)[:, 0:512],
                    AF.Gelu_apprx_tanh, bias=b1[:, ex, mc:mc + 1]),
                    waits={"tensor": q.cnt["tensor"]})
                marks[i] = q.cnt["scalar"]
                if i == 15:
                    w1_rel[0] = {"tensor": q.cnt["tensor"]}
            h_done = q.cnt["scalar"]
            w1_rel[1] = {"tensor": q.cnt["tensor"]}
            q.dma("sync", lambda e, ex=ex, tc0=tc0: e.dma_start(
                out=wrow1[:], in_=w_d[ex:ex + 1, tc0:tc0 + TC]),
                waits={"vector": q.cnt["vector"]})
            q.op("vector", lambda e: e.tensor_copy(wrow1b[:], wrow1[:]),
                 waits={"sync": q.cnt["sync"]})
            q.op("tensor", lambda e: e.matmul(
                psD[:, 0:512], ones1[:], wrow1b[:], start=True, stop=True),
                waits={"vector": q.cnt["vector"]})
            q.op("vector", lambda e: e.tensor_copy(wb[:], psD[:, 0:512]),
                 waits={"tensor": q.cnt["tensor"]})
            wb_done = q.cnt["vector"]
            w2_loaded = {}
            for hf in range(2):
                q.dma("gpsimd", lambda e, ex=ex, hf=hf: e.dma_start(
                    out=w2b2[hf][:],
                    in_=W2_d[ex, :, hf * 512:(hf + 1) * 512].rearrange(
                        "(kc p) m -> p kc m", p=128)),
                    waits=dict(w2_rel[hf]))
                w2_loaded[hf] = q.cnt["gpsimd"]
            vmarks = {}
            for i in range(8):
                dc = i
                hf = dc // 4
                waits = {"gpsimd": w2_loaded[hf], "scalar": h_done}
                if i >= 2:
                    waits["scalar"] = max(h_done, vmarks[i - 2][0])
                def mmo(e, dc=dc, hf=hf):
                    for kc in range(32):
                        inst = e.matmul(
                            (psA if dc % 2 == 0 else psB)[:, 0:512],
                            w2b2[hf][:, kc, (dc % 4) * 128:(dc % 4 + 1) * 128],
                            h[:, kc, :], start=(kc == 0), stop=(kc == 31))
                    return inst
                q.op("tensor", mmo, waits=waits)
                sw = {"tensor": q.cnt["tensor"]}
                if i >= 2:
                    sw["vector"] = vmarks[i - 2][1]
                q.op("scalar", lambda e, dc=dc, ex=ex: e.activation(
                    scr[:, dc % 2, :], (psA if dc % 2 == 0 else psB)[:, 0:512],
                    AF.Identity, bias=b2[:, ex, dc:dc + 1]), waits=sw)
                smark = q.cnt["scalar"]
                vw = {"scalar": smark, "vector": max(wb_done, memset_mark)}
                q.op("vector", lambda e, dc=dc: e.tensor_mul(
                    scr[:, dc % 2, :], scr[:, dc % 2, :], wb[:]), waits=vw)
                q.op("vector", lambda e, dc=dc: e.tensor_add(
                    acc[:, dc, :], acc[:, dc, :], scr[:, dc % 2, :]), waits={})
                vmarks[i] = (smark, q.cnt["vector"])
                if i == 3:
                    w2_rel[0] = {"tensor": q.cnt["tensor"]}
            w2_rel[1] = {"tensor": q.cnt["tensor"]}
            nexp += 1
        for kc in range(8):
            q.dma("sync", lambda e, kc=kc, tc0=tc0: e.dma_start(
                out=scr[:, 0, :],
                in_=x1_d[kc * 128:(kc + 1) * 128, tc0:tc0 + TC]))
            q.op("vector", lambda e, kc=kc: e.tensor_add(
                acc[:, kc, :], acc[:, kc, :], scr[:, 0, :]))
        ln.emit(acc, acc, 8)
        q.dma("sync", lambda e, tc0=tc0: e.dma_start(
            out=x15_d[:, tc0:tc0 + TC].rearrange("(kc p) m -> p kc m", p=128),
            in_=acc[:]))
    return p.finish()


EXT = 2688
QOFF = 637
NQ = 2051


def build_D():
    p = Prog("D")
    q = p.q
    x_d = p.din("x15D", [D, EXT])
    valid_d = p.din("validD", [1, EXT])
    wq_d = p.din("WqT", [D, D])
    wk_d = p.din("WkT", [D, D])
    wv_d = p.din("WvT", [D, D])
    wo_d = p.din("WoT", [D, D])
    w1blk_d = p.din("W1blk", [4, 8, 128, 128])
    w2blk_d = p.din("W2blk", [4, 8, 128, 512])
    wcp_d = p.din("WcpT", [16384, D])
    x25_d = p.dout("x25D", [D, NQ])
    x2_dram = p.dtmp("D_x2dram", [D, NQ], F32)

    psS = p.ps("D_psS", [128, 512], F32)
    psC = p.ps("D_psC", [128, 512], F32)
    psY = p.ps("D_psY", [128, 512], F32)
    psZ = p.ps("D_psZ", [128, 512], F32)
    psO = [p.ps(f"D_psO{i}", [128, 512], F32) for i in range(4)]
    ps1 = psY[0:1, :]
    psT = psZ[:, 0:64].bitcast(BF16)

    QT = [(QOFF, 3)] + [(QOFF + 3 + 512 * k, 512) for k in range(4)]
    TT5 = [(0, 3)] + [(3 + 512 * k, 512) for k in range(4)]
    NKT = EXT // 128

    ident = p.sb("D_ident", [128, 128], BF16)
    onesb = p.sb("D_onesb", [128, 128], BF16)
    q.op("gpsimd", lambda e: e.memset(onesb[:], 1.0))
    q.op("gpsimd", lambda e: e.affine_select(
        ident[:], onesb[:], pattern=[[1, 128]], compare_op=ALU.is_equal,
        fill=0.0, base=0, channel_multiplier=-1))
    ones1 = p.sb("D_ones1", [1, 128], F32)
    q.op("gpsimd", lambda e: e.memset(ones1[:], 1.0))
    onescF = p.sb("D_onescF", [128, 1], F32)
    q.op("gpsimd", lambda e: e.memset(onescF[:], 1.0))
    zeros = p.sb("D_zeros", [128, 512], F32)
    q.op("gpsimd", lambda e: e.memset(zeros[:], 0.0))
    ln = LNHelper(p, 512, "d", psS, ps1)
    attnout = p.sb("D_attnout", [128, 8, NQ], BF16)

    wcp_bf = p.dtmp("D_wcpbf", [16384, D], BF16)
    p.push()
    wstage = p.sb("D_wstage", [128, 8, 1024], BF16)
    for r in range(16):
        q.dma("gpsimd", lambda e, r=r: e.dma_start(
            out=wstage[:],
            in_=wcp_d[r * 1024:(r + 1) * 1024, :].rearrange(
                "(kc p) m -> p kc m", p=128)),
            waits={"sync": q.cnt["sync"]})
        q.dma("sync", lambda e, r=r: e.dma_start(
            out=wcp_bf[r * 1024:(r + 1) * 1024, :].rearrange(
                "(kc p) m -> p kc m", p=128),
            in_=wstage[:]), waits={"gpsimd": q.cnt["gpsimd"]})
    p.pop()

    p.push()
    xb = p.sb("D_xb", [128, 8, EXT], BF16)
    q.dma("gpsimd", lambda e: e.dma_start(
        out=xb[:], in_=x_d.rearrange("(kc p) m -> p kc m", p=128)))
    kvc = p.sb("D_kvc", [128, NKT], F32)
    vrow = p.sb("D_vrow", [1, EXT], F32)
    q.dma("sync", lambda e: e.dma_start(out=vrow[:], in_=valid_d[:]))
    q.op("vector", lambda e: e.tensor_scalar(
        vrow[:], vrow[:], 8e9, -8e9, op0=ALU.mult, op1=ALU.add))
    for kt in range(NKT):
        q.op("tensor", lambda e, kt=kt: e.transpose(
            psS[:, 0:1], vrow[:, kt * 128:(kt + 1) * 128], ones1[0:1, 0:1]))
        q.op("vector", lambda e, kt=kt: e.tensor_copy(kvc[:, kt:kt + 1], psS[:, 0:1]))

    whq = p.sb("D_whq", [128, 8, 64], BF16)
    whk = p.sb("D_whk", [128, 8, 64], BF16)
    whv = p.sb("D_whv", [128, 8, 64], BF16)
    qh = p.sb("D_qh", [64, EXT], BF16)
    kh = p.sb("D_kh", [64, EXT], BF16)
    vh = p.sb("D_vh", [64, EXT], BF16)
    vaug = p.sb("D_vaug", [128, NKT, 65], BF16)
    sm = p.sb("D_sm", [128, 512], F32)
    pt = p.sb("D_pt", [128, 512], BF16)
    msk = p.sb("D_msk", [128, 512], F32)
    msk2 = p.sb("D_msk2", [128, 512], F32)
    smB = p.sb("D_smB", [128, 512], F32)
    ptB = p.sb("D_ptB", [128, 512], BF16)
    mskB = p.sb("D_mskB", [128, 512], F32)
    msk2B = p.sb("D_msk2B", [128, 512], F32)
    pt_rel = {}
    ctx = p.sb("D_ctx", [65, 512], F32)
    rb = p.sb("D_rb", [64, 512], F32)
    rrow = p.sb("D_rrow", [1, 512], F32)
    negb = p.sb("D_negb", [128, 1], F32)
    nrm = p.sb("D_nrm", [1, 8], F32)
    sq64 = p.sb("D_sq64", [64, 512], F32)
    ETT = [(512 * k, 512) for k in range(5)] + [(2560, 128)]

    msk512 = p.sb("D_msk512", [128, 8, 512], F32)
    for m in range(-3, 5):
        q.op("gpsimd", lambda e, m=m: e.iota(
            msk[:, 0:512], pattern=[[1, 512]], base=128 * m,
            channel_multiplier=-1, allow_small_or_imprecise_dtypes=True))
        q.op("vector", lambda e: e.tensor_single_scalar(
            msk2[:, 0:512], msk[:, 0:512], 0.0, op=ALU.is_ge))
        q.op("vector", lambda e: e.tensor_single_scalar(
            msk[:, 0:512], msk[:, 0:512], 512.0, op=ALU.is_lt))
        q.op("vector", lambda e: e.tensor_mul(
            msk[:, 0:512], msk[:, 0:512], msk2[:, 0:512]))
        q.op("vector", lambda e, m=m: e.tensor_scalar(
            msk512[:, m + 3, :], msk[:, 0:512], 8e9, -8e9,
            op0=ALU.mult, op1=ALU.add))

    for h in range(16):
        for (wd, wh) in ((wq_d, whq), (wk_d, whk), (wv_d, whv)):
            q.dma("gpsimd", lambda e, wd=wd, wh=wh, h=h: e.dma_start(
                out=wh[:], in_=wd[:, 64 * h:64 * (h + 1)].rearrange(
                    "(kc p) m -> p kc m", p=128)))
        for (wh, dst) in ((whq, qh), (whk, kh), (whv, vh)):
            for (t0, w) in ETT:
                def mmp(e, wh=wh, t0=t0, w=w):
                    for kc in range(8):
                        inst = e.matmul(psS[0:64, 0:w], wh[:, kc, :],
                                        xb[:, kc, t0:t0 + w],
                                        start=(kc == 0), stop=(kc == 7))
                    return inst
                q.op("tensor", mmp)
                q.op("vector", lambda e, dst=dst, t0=t0, w=w: e.tensor_copy(
                    dst[:, t0:t0 + w], psS[0:64, 0:w]))
        for (src, col) in ((qh, 0), (kh, 1)):
            for i, (t0, w) in enumerate(ETT):
                q.op("scalar", lambda e, src=src, t0=t0, w=w: e.activation(
                    sq64[:, 0:w], src[:, t0:t0 + w], AF.Square))
                q.op("tensor", lambda e, w=w: e.matmul(
                    ps1[:, 0:w], onescF[0:64, :], sq64[:, 0:w],
                    start=True, stop=True))
                tgt = col if i == 0 else 2
                q.op("vector", lambda e, tgt=tgt, w=w: e.reduce_max(
                    nrm[:, tgt:tgt + 1], ps1[:, 0:w], axis=AX.X))
                if i > 0:
                    q.op("vector", lambda e, col=col: e.tensor_max(
                        nrm[:, col:col + 1], nrm[:, col:col + 1], nrm[:, 2:3]))
        q.op("scalar", lambda e: e.activation(nrm[:, 0:1], nrm[:, 0:1], AF.Sqrt))
        q.op("scalar", lambda e: e.activation(nrm[:, 1:2], nrm[:, 1:2], AF.Sqrt))
        q.op("vector", lambda e: e.tensor_mul(nrm[:, 3:4], nrm[:, 0:1], nrm[:, 1:2]))
        q.op("vector", lambda e: e.tensor_scalar_mul(nrm[:, 3:4], nrm[:, 3:4], -0.125))
        q.op("tensor", lambda e: e.matmul(
            psY[:, 0:1], ones1[:], nrm[:, 3:4], start=True, stop=True))
        q.op("vector", lambda e: e.tensor_copy(negb[:], psY[:, 0:1]))
        for kt in range(NKT):
            q.op("tensor", lambda e, kt=kt: e.transpose(
                psT[:, 0:64], vh[:, kt * 128:(kt + 1) * 128], ident[0:64, 0:64]))
            q.op("vector", lambda e, kt=kt: e.tensor_copy(vaug[:, kt, 0:64], psT[:, 0:64]))
        q.op("gpsimd", lambda e: e.memset(vaug[:, :, 64:65], 1.0))
        hc, hb = h // 2, (h % 2) * 64
        entry = q.snap()
        for (t0, w) in QT:
            kt0 = max((t0 - 512) // 128, 0)
            kt1 = (t0 + w + 127) // 128
            nkt = kt1 - kt0
            vm = {}
            sm_marks = {}
            for ki, kt in enumerate(range(kt0, kt1)):
                bank = psS if ki % 2 == 0 else psZ
                tw = dict(entry)
                if ki >= 2:
                    tw["vector"] = vm[ki - 2]
                q.op("tensor", lambda e, kt=kt, t0=t0, w=w, bank=bank: e.matmul(
                    bank[:, 0:w], kh[:, kt * 128:(kt + 1) * 128],
                    qh[:, t0:t0 + w], start=True, stop=True), waits=tw)
                smm = q.cnt["tensor"]
                if w == 512:
                    mi = (t0 - 128 * kt) // 128 + 3
                    assert 0 <= mi < 8 and t0 - 128 * kt == 128 * (mi - 3)
                    mk = msk512[:, mi, :]
                else:
                    gw = dict(entry)
                    if ki >= 2:
                        gw["vector"] = max(gw.get("vector", 0), vm[ki - 2])
                    q.op("gpsimd", lambda e, t0=t0, kt=kt, w=w, bank=bank: e.iota(
                        (msk if bank is psS else mskB)[:, 0:w], pattern=[[1, w]],
                        base=t0 - 128 * kt,
                        channel_multiplier=-1, allow_small_or_imprecise_dtypes=True),
                        waits=gw)
                    gm = q.cnt["gpsimd"]
                    mk0 = msk if ki % 2 == 0 else mskB
                    mk2 = msk2 if ki % 2 == 0 else msk2B
                    q.op("vector", lambda e, w=w, mk0=mk0, mk2=mk2: e.tensor_single_scalar(
                        mk2[:, 0:w], mk0[:, 0:w], 0.0, op=ALU.is_ge),
                        waits={"gpsimd": gm})
                    q.op("vector", lambda e, w=w, mk0=mk0: e.tensor_single_scalar(
                        mk0[:, 0:w], mk0[:, 0:w], 512.0, op=ALU.is_lt), waits={})
                    q.op("vector", lambda e, w=w, mk0=mk0, mk2=mk2: e.tensor_mul(
                        mk0[:, 0:w], mk0[:, 0:w], mk2[:, 0:w]), waits={})
                    q.op("vector", lambda e, w=w, mk0=mk0: e.tensor_scalar(
                        mk0[:, 0:w], mk0[:, 0:w], 8e9, -8e9, op0=ALU.mult, op1=ALU.add),
                        waits={})
                    mk = mk0[:, 0:w]
                sw = {"tensor": smm}
                if ki >= 2:
                    sw["scalar"] = sm_marks[ki - 2]
                q.op("vector", lambda e, kt=kt, w=w, bank=bank, mk=mk: e.scalar_tensor_tensor(
                    (sm if bank is psS else smB)[:, 0:w], bank[:, 0:w],
                    kvc[:, kt:kt + 1], mk,
                    op0=ALU.add, op1=ALU.add), waits=sw)
                vm[ki] = q.cnt["vector"]
                ptw = dict(entry)
                ptw["vector"] = vm[ki]
                if ki >= 2:
                    ptw["tensor"] = max(ptw.get("tensor", 0), pt_rel[ki - 2])
                q.op("scalar", lambda e, w=w, bank=bank: e.activation(
                    (pt if bank is psS else ptB)[:, 0:w],
                    (sm if bank is psS else smB)[:, 0:w],
                    AF.Exp, bias=negb[:, 0:1], scale=0.125), waits=ptw)
                sm_marks[ki] = q.cnt["scalar"]
                q.op("tensor", lambda e, kt=kt, ki=ki, nkt=nkt, w=w, bank=bank: e.matmul(
                    psC[0:65, 0:w], vaug[:, kt, :],
                    (pt if bank is psS else ptB)[:, 0:w],
                    start=(ki == 0), stop=(ki == nkt - 1)),
                    waits={"scalar": sm_marks[ki]})
                pt_rel[ki] = q.cnt["tensor"]
            q.op("vector", lambda e, w=w: e.tensor_copy(ctx[:, 0:w], psC[0:65, 0:w]),
                 waits={"tensor": q.cnt["tensor"]})
            q.op("vector", lambda e, w=w: e.reciprocal(rrow[:, 0:w], ctx[64:65, 0:w]),
                 waits={})
            q.op("tensor", lambda e, w=w: e.matmul(
                psY[0:64, 0:w], ones1[0:1, 0:64], rrow[:, 0:w],
                start=True, stop=True), waits={"vector": q.cnt["vector"]})
            q.op("vector", lambda e, w=w: e.tensor_copy(rb[:, 0:w], psY[0:64, 0:w]),
                 waits={"tensor": q.cnt["tensor"]})
            q.op("vector", lambda e, w=w: e.tensor_mul(
                ctx[0:64, 0:w], ctx[0:64, 0:w], rb[:, 0:w]), waits={})
            q.op("vector", lambda e, t0=t0, w=w, hc=hc, hb=hb: e.tensor_copy(
                attnout[hb:hb + 64, hc, t0 - QOFF:t0 - QOFF + w], ctx[0:64, 0:w]),
                waits={})
            entry = q.snap()
    p.pop()

    p.push()
    wsb = load_w_bf16(p, "D_wo", wo_d, D, D)
    x2t = p.sb("D_x2t", [128, 8, 512], F32)
    for (t0, w) in TT5:
        q.dma("sync", lambda e, t0=t0, w=w: e.dma_start(
            out=x2t[:, :, 0:w],
            in_=x_d[:, QOFF + t0:QOFF + t0 + w].rearrange("(kc p) m -> p kc m", p=128)))
        for mc in range(8):
            def mmwo(e, mc=mc, t0=t0, w=w):
                for kc in range(8):
                    inst = e.matmul(psS[:, 0:w],
                                    wsb[:, kc, mc * 128:(mc + 1) * 128],
                                    attnout[:, kc, t0:t0 + w],
                                    start=(kc == 0), stop=(kc == 7))
                return inst
            q.op("tensor", mmwo)
            q.op("vector", lambda e, mc=mc, w=w: e.tensor_add(
                x2t[:, mc, 0:w], x2t[:, mc, 0:w], psS[:, 0:w]))
        ln.emit_w(x2t, x2t, 8, w)
        q.dma("sync", lambda e, t0=t0, w=w: e.dma_start(
            out=x2_dram[:, t0:t0 + w].rearrange("(kc p) m -> p kc m", p=128),
            in_=x2t[:, :, 0:w]))
    p.pop()

    p.push()
    w1blk = p.sb("D_w1blk", [128, 4, 8, 128], BF16)
    q.dma("gpsimd", lambda e: e.dma_start(
        out=w1blk[:], in_=w1blk_d.rearrange("h g p m -> p h g m")))
    w2blk = p.sb("D_w2blk", [128, 4, 8, 512], BF16)
    q.dma("gpsimd", lambda e: e.dma_start(
        out=w2blk[:], in_=w2blk_d.rearrange("h g p m -> p h g m")))
    wcpb = p.sb("D_wcpb", [128, 4, 512], BF16)
    x2bt = p.sb("D_x2bt", [128, 8, 512], BF16)
    x2rt = p.sb("D_x2rt", [128, 8, 512], F32)
    ybt = p.sb("D_ybt", [128, 2, 512], BF16)
    hgt = p.sb("D_hgt", [128, 4, 512], BF16)
    for (t0, w) in TT5:
        q.dma("gpsimd", lambda e, t0=t0, w=w: e.dma_start(
            out=x2bt[:, :, 0:w],
            in_=x2_dram[:, t0:t0 + w].rearrange("(kc p) m -> p kc m", p=128)))
        q.dma("sync", lambda e, t0=t0, w=w: e.dma_start(
            out=x2rt[:, :, 0:w],
            in_=x2_dram[:, t0:t0 + w].rearrange("(kc p) m -> p kc m", p=128)))
        for dh in range(2):
            entry = q.snap()
            ycopy = {}
            gelu_m = {}
            cp_rel = {}
            ld_m = {}
            first = True
            u = 0
            for h4 in range(4):
                for g8 in range(8):
                    ybank = psY if u % 2 == 0 else psS
                    yslot = u % 2
                    tw = dict(entry)
                    if u >= 2:
                        tw["vector"] = max(tw.get("vector", 0), ycopy[u - 2])
                    q.op("tensor", lambda e, h4=h4, g8=g8, w=w, ybank=ybank: e.matmul(
                        ybank[:, 0:w], w1blk[:, h4, g8, :], x2bt[:, g8, 0:w],
                        start=True, stop=True), waits=tw)
                    q.op("vector", lambda e, w=w, ybank=ybank, yslot=yslot:
                         e.tensor_copy(ybt[:, yslot, 0:w], ybank[:, 0:w]),
                         waits={"tensor": q.cnt["tensor"]})
                    ycopy[u] = q.cnt["vector"]
                    for mi in range(4):
                        fc = h4 * 32 + g8 * 4 + mi
                        zbank = psZ if mi % 2 == 0 else psC
                        zw = dict(entry)
                        zw["vector"] = ycopy[u]
                        if (u, mi) in gelu_m or u >= 1:
                            prev = gelu_m.get((u, mi - 2)) or gelu_m.get((u - 1, mi + 2))
                            if prev:
                                zw["scalar"] = max(zw.get("scalar", 0), prev)
                        q.op("tensor", lambda e, h4=h4, g8=g8, mi=mi, w=w,
                             zbank=zbank, yslot=yslot: e.matmul(
                            zbank[:, 0:w], w2blk[:, h4, g8, mi * 128:(mi + 1) * 128],
                            ybt[:, yslot, 0:w], start=True, stop=True), waits=zw)
                        gw = {"tensor": q.cnt["tensor"]}
                        if (u - 1, mi) in cp_rel:
                            gw["tensor"] = q.cnt["tensor"]
                        q.op("scalar", lambda e, w=w, mi=mi, zbank=zbank: e.activation(
                            hgt[:, mi, 0:w], zbank[:, 0:w], AF.Gelu), waits=gw)
                        gelu_m[(u, mi)] = q.cnt["scalar"]
                        lw = dict(entry)
                        if (u - 1, mi) in cp_rel:
                            lw["tensor"] = max(lw.get("tensor", 0), cp_rel[(u - 1, mi)])
                        q.dma("gpsimd", lambda e, fc=fc, dh=dh, mi=mi: e.dma_start(
                            out=wcpb[:, mi, :],
                            in_=wcp_bf[fc * 128:(fc + 1) * 128, dh * 512:(dh + 1) * 512]),
                            waits=lw)
                        ld_m[(u, mi)] = q.cnt["gpsimd"]
                        cpw = dict(entry)
                        cpw["scalar"] = gelu_m[(u, mi)]
                        cpw["gpsimd"] = max(cpw.get("gpsimd", 0), ld_m[(u, mi)])
                        for oc in range(4):
                            q.op("tensor", lambda e, oc=oc, w=w, mi=mi,
                                 first=first: e.matmul(
                                psO[oc][:, 0:w], wcpb[:, mi, oc * 128:(oc + 1) * 128],
                                hgt[:, mi, 0:w], start=first,
                                stop=(fc == 127)),
                                waits=cpw)
                        cp_rel[(u, mi)] = q.cnt["tensor"]
                        first = False
                    u += 1
            for oc in range(4):
                dc = dh * 4 + oc
                q.op("vector", lambda e, oc=oc, dc=dc, w=w: e.tensor_add(
                    x2rt[:, dc, 0:w], x2rt[:, dc, 0:w], psO[oc][:, 0:w]),
                    waits={"tensor": q.cnt["tensor"], "sync": q.cnt["sync"]})
        ln.emit_w(x2rt, x2rt, 8, w)
        q.dma("sync", lambda e, t0=t0, w=w: e.dma_start(
            out=x25_d[:, t0:t0 + w].rearrange("(kc p) m -> p kc m", p=128),
            in_=x2rt[:, :, 0:w]))
    p.pop()
    return p.finish()


DI = 2048


def build_E():
    p = Prog("E")
    q = p.q
    x_d = p.din("x25E", [D, 3 + W])
    win_d = p.din("W_in", [D, 2 * DI])
    cw_d = p.din("conv_w_h", [128, 16, 4])
    cb_d = p.din("conv_b_h", [128, 16])
    alog_d = p.din("A_log_h", [128, 16, 16])
    uc_d = p.dout("ucE", [DI, W])
    zs_d = p.dout("zsE", [DI, W])
    S_d = p.dout("SE", [128, 16, 16])

    psA = p.ps("E_psA", [128, 512], F32)
    psB = p.ps("E_psB", [128, 512], F32)

    xb = load_fm_f32(p, "E_xb", x_d, D, 3 + W, cast=BF16)
    wsb = load_w_bf16(p, "E_win", win_d, D, 2 * DI)
    cw = p.sb("E_cw", [128, 16, 4], F32)
    q.dma("sync", lambda e: e.dma_start(out=cw[:], in_=cw_d[:]))
    cb = p.sb("E_cb", [128, 16], F32)
    q.dma("sync", lambda e: e.dma_start(out=cb[:], in_=cb_d[:]))
    alog = p.sb("E_alog", [128, 16, 16], F32)
    q.dma("sync", lambda e: e.dma_start(out=alog[:], in_=alog_d[:]))
    A = p.sb("E_A", [128, 16, 16], F32)
    q.op("scalar", lambda e: e.activation(A[:], alog[:], AF.Exp))
    q.op("vector", lambda e: e.tensor_scalar_mul(A[:], A[:], -1.0))

    uraw = p.sb("E_uraw", [128, 3 + W], F32)
    ucv = p.sb("E_ucv", [128, W], F32)
    ucvb = p.sb("E_ucvb", [128, W], F32)
    scr = p.sb("E_scr", [128, W], F32)
    S = p.sb("E_S", [128, 16, 16], F32)
    zbuf = p.sb("E_zbuf", [128, 2, W], F32)

    pre = q.snap()
    gi = 0
    copymark = {}
    for mc in range(16):
        for t0 in range(0, 3 + W, 512):
            t1 = min(t0 + 512, 3 + W)
            bank = psA if gi % 2 == 0 else psB
            tw = dict(pre)
            if gi >= 2:
                tw["vector"] = max(tw.get("vector", 0), copymark[gi - 2])

            def mmu(e, mc=mc, t0=t0, t1=t1, bank=bank):
                for kc in range(8):
                    inst = e.matmul(bank[:, 0:t1 - t0],
                                    wsb[:, kc, mc * 128:(mc + 1) * 128],
                                    xb[:, kc, t0:t1], start=(kc == 0), stop=(kc == 7))
                return inst
            q.op("tensor", mmu, waits=tw)
            q.op("vector", lambda e, t0=t0, t1=t1, bank=bank: e.tensor_copy(
                uraw[:, t0:t1], bank[:, 0:t1 - t0]),
                waits={"tensor": q.cnt["tensor"]})
            copymark[gi] = q.cnt["vector"]
            gi += 1
        q.op("vector", lambda e, mc=mc: e.tensor_scalar_mul(
            ucv[:], uraw[:, 0:W], cw[:, mc, 0:1]))
        for k in range(1, 4):
            q.op("vector", lambda e, mc=mc, k=k: e.scalar_tensor_tensor(
                ucv[:], uraw[:, k:k + W], cw[:, mc, k:k + 1], ucv[:],
                op0=ALU.mult, op1=ALU.add))
        q.op("vector", lambda e, mc=mc: e.tensor_scalar_add(
            ucv[:], ucv[:], cb[:, mc:mc + 1]))
        q.dma("sync", lambda e, mc=mc: e.dma_start(
            out=uc_d[mc * 128:(mc + 1) * 128, :], in_=ucv[:]))
        for s in range(16):
            q.op("vector", lambda e, mc=mc, s=s: e.tensor_tensor_scan(
                scr[:], A[:, mc, s:s + 1].broadcast_to([128, W]), ucv[:],
                initial=0.0, op0=ALU.mult, op1=ALU.add))
            q.op("vector", lambda e, mc=mc, s=s: e.tensor_copy(
                S[:, mc, s:s + 1], scr[:, W - 1:W]))
    q.dma("sync", lambda e: e.dma_start(out=S_d[:], in_=S[:]))
    ucopy_done = copymark[gi - 1]
    gz = 0
    silumark = {}
    zdma = {}
    for mc in range(16):
        for t0 in range(0, W, 512):
            bank = psA if gz % 2 == 0 else psB
            tw = dict(pre)
            tw["vector"] = max(tw.get("vector", 0), ucopy_done)
            if gz >= 2:
                tw["scalar"] = max(tw.get("scalar", 0), silumark[gz - 2])

            def mmz(e, mc=mc, t0=t0, bank=bank):
                for kc in range(8):
                    inst = e.matmul(bank[:, 0:512],
                                    wsb[:, kc, 2048 + mc * 128:2048 + (mc + 1) * 128],
                                    xb[:, kc, 3 + t0:3 + t0 + 512],
                                    start=(kc == 0), stop=(kc == 7))
                return inst
            q.op("tensor", mmz, waits=tw)
            sw = {"tensor": q.cnt["tensor"]}
            if mc >= 2 and t0 == 0:
                sw["sync"] = zdma[mc - 2]
            q.op("scalar", lambda e, t0=t0, mc=mc, bank=bank: e.activation(
                zbuf[:, mc % 2, t0:t0 + 512], bank[:, 0:512], AF.Silu), waits=sw)
            silumark[gz] = q.cnt["scalar"]
            gz += 1
        q.dma("sync", lambda e, mc=mc: e.dma_start(
            out=zs_d[mc * 128:(mc + 1) * 128, :], in_=zbuf[:, mc % 2, :]),
            waits={"scalar": q.cnt["scalar"]})
        zdma[mc] = q.cnt["sync"]
    return p.finish()



def build_F():
    p = Prog("F")
    q = p.q
    uc_d = p.din("ucF", [DI, W])
    zs_d = p.din("zsF", [DI, W])
    x25_d = p.din("x25F", [D, W])
    alog_d = p.din("A_log_h", [128, 16, 16])
    dprm_d = p.din("D_param_h", [128, 16])
    wout_d = p.din("W_out", [DI, D])
    sprev_d = p.din("SprevF", [3, 128, 16, 16])
    wg_d = p.din("wgF", [D, 8])
    x3_d = p.dout("x3F", [D, W])
    g_d = p.dout("gF", [8, W])

    psA = p.ps("F_psA", [128, 512], F32)
    psB = p.ps("F_psB", [128, 512], F32)
    ps1 = psB[0:1, :]
    ps8 = psB[0:8, :]
    pst = p.ps("F_pst", [128, 128], F32)

    alog = p.sb("F_alog", [128, 16, 16], F32)
    q.dma("sync", lambda e: e.dma_start(out=alog[:], in_=alog_d[:]))
    A = p.sb("F_A", [128, 16, 16], F32)
    q.op("scalar", lambda e: e.activation(A[:], alog[:], AF.Exp))
    q.op("vector", lambda e: e.tensor_scalar_mul(A[:], A[:], -1.0))
    dprm = p.sb("F_dprm", [128, 16], F32)
    q.dma("sync", lambda e: e.dma_start(out=dprm[:], in_=dprm_d[:]))
    carry = p.sb("F_carry", [128, 16, 16], F32)
    mid = p.sb("F_mid", [128, 16, 16], F32)
    x3 = p.sb("F_x3", [128, 8, W], F32)
    q.dma("sync", lambda e: e.dma_start(
        out=x3[:], in_=x25_d.rearrange("(kc p) m -> p kc m", p=128)))
    ln = LNHelper(p, 512, "f", psA, ps1)

    p.push()
    sprev = p.sb("F_sprev", [128, 3, 16, 16], F32)
    q.dma("sync", lambda e: e.dma_start(
        out=sprev[:], in_=sprev_d.rearrange("k p c s -> p k c s")))
    dec = p.sb("F_dec", [128, 16, 16], F32)
    q.op("vector", lambda e: e.memset(carry[:], 0.0))
    for k in range(1, 4):
        q.op("scalar", lambda e, k=k: e.activation(
            dec[:], alog[:], AF.Exp, scale=float(2048.0 * (k - 1))))
        q.op("vector", lambda e, k=k: e.tensor_mul(
            dec[:], dec[:], sprev[:, k - 1, :, :]))
        q.op("vector", lambda e: e.tensor_add(carry[:], carry[:], dec[:]))
    p.pop()

    TH = 1024
    p.push()
    wsb = load_w_bf16(p, "F_wout", wout_d, DI, D)
    ucv = p.sb("F_ucv", [128, TH], F32)
    zsv = p.sb("F_zsv", [128, TH], F32)
    scr = p.sb("F_scr", [128, TH], F32)
    yv = p.sb("F_yv", [128, TH], F32)
    ybf = p.sb("F_ybf", [128, 16, TH], BF16)
    for th in range(2):
        h0 = th * TH
        for mc in range(16):
            q.dma("sync", lambda e, mc=mc, h0=h0: e.dma_start(
                out=ucv[:], in_=uc_d[mc * 128:(mc + 1) * 128, h0:h0 + TH]))
            q.dma("sync", lambda e, mc=mc, h0=h0: e.dma_start(
                out=zsv[:], in_=zs_d[mc * 128:(mc + 1) * 128, h0:h0 + TH]))
            for s in range(16):
                init = carry[:, mc, s:s + 1] if th == 0 else mid[:, mc, s:s + 1]
                q.op("vector", lambda e, mc=mc, s=s, init=init: e.tensor_tensor_scan(
                    scr[:], A[:, mc, s:s + 1].broadcast_to([128, TH]), ucv[:],
                    initial=init, op0=ALU.mult, op1=ALU.add))
                if th == 0:
                    q.op("vector", lambda e, mc=mc, s=s: e.tensor_copy(
                        mid[:, mc, s:s + 1], scr[:, TH - 1:TH]))
                if s == 0:
                    q.op("vector", lambda e: e.tensor_copy(yv[:], scr[:]))
                else:
                    q.op("vector", lambda e: e.tensor_add(yv[:], yv[:], scr[:]))
            q.op("vector", lambda e, mc=mc: e.scalar_tensor_tensor(
                yv[:], ucv[:], dprm[:, mc:mc + 1], yv[:], op0=ALU.mult, op1=ALU.add))
            q.op("vector", lambda e: e.tensor_mul(yv[:], yv[:], zsv[:]))
            q.op("vector", lambda e, mc=mc: e.tensor_copy(ybf[:, mc, :], yv[:]))
        for mc in range(8):
            for t0 in range(0, TH, 512):
                def mmwo(e, mc=mc, t0=t0):
                    for kc in range(16):
                        inst = e.matmul(psA[:, 0:512],
                                        wsb[:, kc, mc * 128:(mc + 1) * 128],
                                        ybf[:, kc, t0:t0 + 512],
                                        start=(kc == 0), stop=(kc == 15))
                    return inst
                q.op("tensor", mmwo)
                q.op("vector", lambda e, mc=mc, t0=t0, h0=h0: e.tensor_add(
                    x3[:, mc, h0 + t0:h0 + t0 + 512], x3[:, mc, h0 + t0:h0 + t0 + 512],
                    psA[:, 0:512]))
    p.pop()
    x3v = x3[:].rearrange("p kc (n t) -> p kc n t", t=512)
    for ti in range(4):
        ln.emit(x3v[:, :, ti, :], x3v[:, :, ti, :], 8)
    store_fm(p, x3_d, x3, 8, W)
    p.push()
    x3b = p.sb("F_x3b", [128, 8, W], BF16)
    emit_cast(p, x3b, x3, 8, W)
    wg = load_w_bf16(p, "F_wg", wg_d, D, 8)
    gsc = p.sb("F_gsc", [8, W], F32)
    for it in range(0, W, 512):
        def mmg(e, it=it):
            for kc in range(8):
                inst = e.matmul(ps8[:, 0:512], wg[:, kc, :], x3b[:, kc, it:it + 512],
                                start=(kc == 0), stop=(kc == 7))
            return inst
        q.op("tensor", mmg)
        q.op("vector", lambda e, it=it: e.tensor_copy(
            gsc[:, it:it + 512], ps8[:, 0:512]))
    identF = p.sb("F_identF", [128, 128], F32)
    q.op("gpsimd", lambda e: e.memset(identF[:], 1.0))
    q.op("gpsimd", lambda e: e.affine_select(
        identF[:], identF[:], pattern=[[1, 128]], compare_op=ALU.is_equal,
        fill=0.0, base=0, channel_multiplier=-1))
    gt = p.sb("F_gt", [128, 16, 8], F32)
    red = p.sb("F_red", [128, 16, 1], F32)
    for ti in range(16):
        q.op("tensor", lambda e, ti=ti: e.transpose(
            pst[:, 0:8], gsc[:, ti * 128:(ti + 1) * 128], identF[0:8, 0:8]))
        q.op("vector", lambda e, ti=ti: e.tensor_copy(gt[:, ti, :], pst[:, 0:8]))
    q.op("vector", lambda e: e.tensor_reduce(red[:], gt[:], axis=AX.X, op=ALU.max))
    q.op("vector", lambda e: e.tensor_sub(gt[:], gt[:], red[:].broadcast_to([128, 16, 8])))
    q.op("scalar", lambda e: e.activation(gt[:], gt[:], AF.Exp))
    q.op("vector", lambda e: e.tensor_reduce(red[:], gt[:], axis=AX.X, op=ALU.add))
    q.op("vector", lambda e: e.reciprocal(red[:], red[:]))
    q.op("vector", lambda e: e.tensor_mul(gt[:], gt[:], red[:].broadcast_to([128, 16, 8])))
    for ti in range(16):
        q.op("tensor", lambda e, ti=ti: e.transpose(
            pst[0:8, :], gt[:, ti, :], identF[:]))
        q.op("vector", lambda e, ti=ti: e.tensor_copy(
            gsc[:, ti * 128:(ti + 1) * 128], pst[0:8, 0:128]))
    q.dma("sync", lambda e: e.dma_start(out=g_d[:], in_=gsc[:]))
    p.pop()
    return p.finish()



def build_G():
    p = Prog("G")
    q = p.q
    x3_d = p.din("x3G", [D, W])
    gown_d = p.din("gownG", [8, W])
    gall_d = p.din("gallG", [8, 8 * W])
    w1_d = p.din("moW1", [8, D, DI])
    w3_d = p.din("moW3", [8, D, DI])
    w2_d = p.din("moW2", [8, DI, D])
    wdqt_d = p.din("WDQT", [64, 512])
    wuqt_d = p.din("WUQT", [512, 64])
    wdkvt_d = p.din("WDKVT", [64, 1024])
    wukvt_d = p.din("WUKVT", [1024, 128])
    wqrt_d = p.din("WQRT", [64, 64])
    wkrt_d = p.din("WKRT", [64, 64])
    qnw_d = p.din("qnw", [64, 1])
    knw_d = p.din("knw", [64, 1])
    x35_d = p.dout("x35G", [D, W])
    qH_d = p.dout("qHG", [16, 128, W], BF16)
    kH_d = p.dout("kHG", [16, 128, W], BF16)
    vH_d = p.dout("vHG", [16, 64, W], BF16)
    wsel_dram = p.dtmp("G_wsel", [8, W], F32)

    psA = p.ps("G_psA", [128, 512], F32)
    psB = p.ps("G_psB", [128, 512], F32)
    ps1 = psB[0:1, :]

    ones1 = p.sb("G_ones1", [1, 128], BF16)
    q.op("gpsimd", lambda e: e.memset(ones1[:], 1.0))
    ones1f = p.sb("G_ones1f", [1, 128], F32)
    q.op("gpsimd", lambda e: e.memset(ones1f[:], 1.0))
    onescF = p.sb("G_onescF", [128, 1], F32)
    q.op("gpsimd", lambda e: e.memset(onescF[:], 1.0))
    ln = LNHelper(p, 512, "g", psA, ps1)

    p.push()
    gall = p.sb("G_gall", [8, 8 * W], F32)
    q.dma("sync", lambda e: e.dma_start(out=gall[:], in_=gall_d[:]))
    cmp_ = p.sb("G_cmp", [8, 8 * W], F32)
    sc = p.sb("G_sc", [8, 8], F32)
    q.op("gpsimd", lambda e: e.memset(sc[:, 0:1], 0.0))
    q.op("gpsimd", lambda e: e.memset(sc[:, 1:2], 1.1))
    for it in range(48):
        q.op("vector", lambda e: e.tensor_add(sc[:, 2:3], sc[:, 0:1], sc[:, 1:2]))
        q.op("vector", lambda e: e.tensor_scalar_mul(sc[:, 2:3], sc[:, 2:3], 0.5))
        q.op("vector", lambda e: e.tensor_scalar(
            cmp_[:], gall[:], sc[:, 2:3], None, op0=ALU.is_ge))
        q.op("vector", lambda e: e.reduce_sum(sc[:, 3:4], cmp_[:], axis=AX.X))
        q.op("vector", lambda e: e.tensor_single_scalar(
            sc[:, 4:5], sc[:, 3:4], 4096.0, op=ALU.is_ge))
        q.op("vector", lambda e: e.tensor_scalar(
            sc[:, 5:6], sc[:, 4:5], -1.0, 1.0, op0=ALU.mult, op1=ALU.add))
        q.op("vector", lambda e: e.tensor_sub(sc[:, 6:7], sc[:, 2:3], sc[:, 0:1]))
        q.op("vector", lambda e: e.tensor_sub(sc[:, 7:8], sc[:, 2:3], sc[:, 1:2]))
        q.op("vector", lambda e: e.scalar_tensor_tensor(
            sc[:, 0:1], sc[:, 6:7], sc[:, 4:5], sc[:, 0:1], op0=ALU.mult, op1=ALU.add))
        q.op("vector", lambda e: e.scalar_tensor_tensor(
            sc[:, 1:2], sc[:, 7:8], sc[:, 5:6], sc[:, 1:2], op0=ALU.mult, op1=ALU.add))
    gown = p.sb("G_gown", [8, W], F32)
    q.dma("sync", lambda e: e.dma_start(out=gown[:], in_=gown_d[:]))
    q.op("vector", lambda e: e.tensor_scalar(
        cmp_[:, 0:W], gown[:], sc[:, 0:1], None, op0=ALU.is_ge))
    q.op("vector", lambda e: e.tensor_mul(gown[:], gown[:], cmp_[:, 0:W]))
    q.dma("sync", lambda e: e.dma_start(out=wsel_dram[:], in_=gown[:]))
    wsel_done = q.cnt["sync"]
    p.pop()

    x35bf = p.sb("G_x35bf", [128, 8, W], BF16)
    p.push()
    TC = 512
    x3bf = p.sb("G_x3bf", [128, 8, TC], BF16)
    acc = p.sb("G_acc", [128, 8, TC], F32)
    hsb = p.sb("G_h", [128, 16, TC], BF16)
    wb = p.sb("G_wb", [128, TC], F32)
    scr = p.sb("G_scr", [128, 2, TC], F32)
    wrow1 = p.sb("G_wrow1", [1, TC], F32)
    wrow1b = p.sb("G_wrow1b", [1, TC], BF16)
    xres = p.sb("G_xres", [128, 8, TC], F32)
    w1sb = p.sb("G_w1", [128, 8, DI], BF16)
    w3sb = p.sb("G_w3", [128, 8, DI], BF16)
    w2sb = [p.sb(f"G_w2_{i}", [128, 16, 512], BF16) for i in range(2)]
    psD = p.ps("G_psD", [128, 512], F32)
    w1_rel = {}
    w3_rel = {}
    w2_rel = [{}, {}]
    for tc0 in range(0, W, TC):
        q.dma("sync", lambda e, tc0=tc0: e.dma_start(
            out=xres[:], in_=x3_d[:, tc0:tc0 + TC].rearrange("(kc p) m -> p kc m", p=128)))
        q.op("vector", lambda e: e.memset(acc[:], 0.0),
             waits={"tensor": q.cnt["tensor"], "scalar": q.cnt["scalar"],
                    "sync": q.cnt["sync"]})
        memset_mark = q.cnt["vector"]
        for ex in range(8):
            q.dma("sync", lambda e, ex=ex, tc0=tc0: e.dma_start(
                out=wrow1[:], in_=wsel_dram[ex:ex + 1, tc0:tc0 + TC]),
                waits={"vector": q.cnt["vector"], "sync": wsel_done})
            q.op("vector", lambda e: e.tensor_copy(wrow1b[:], wrow1[:]),
                 waits={"sync": q.cnt["sync"]})
            q.op("tensor", lambda e: e.matmul(
                psD[:, 0:TC], ones1[:], wrow1b[:], start=True, stop=True),
                waits={"vector": q.cnt["vector"]})
            q.op("vector", lambda e: e.tensor_copy(wb[:], psD[:, 0:TC]),
                 waits={"tensor": q.cnt["tensor"]})
            for kc in range(8):
                q.op("vector", lambda e, kc=kc: e.tensor_tensor(
                    x3bf[:, kc, :], xres[:, kc, :], wb[:], op=ALU.mult),
                    waits=({"tensor": q.cnt["tensor"]} if kc == 0 else {}))
            route_done = q.cnt["vector"]
            q.dma("gpsimd", lambda e, ex=ex: e.dma_start(
                out=w1sb[:], in_=w1_d[ex].rearrange("(kc p) m -> p kc m", p=128)),
                waits=dict(w1_rel))
            w1_loaded = q.cnt["gpsimd"]
            q.dma("gpsimd", lambda e, ex=ex: e.dma_start(
                out=w3sb[:], in_=w3_d[ex].rearrange("(kc p) m -> p kc m", p=128)),
                waits=dict(w3_rel))
            w3_loaded = q.cnt["gpsimd"]
            vmarks = {}
            smarks = {}
            for i in range(16):
                mc = i
                w_ = {"gpsimd": w1_loaded, "vector": route_done}
                if i >= 1:
                    w_["scalar"] = smarks[i - 1]
                def mm1(e, mc=mc):
                    for kc in range(8):
                        inst = e.matmul(psA[:, 0:TC], w1sb[:, kc, mc * 128:(mc + 1) * 128],
                                        x3bf[:, kc, :], start=(kc == 0), stop=(kc == 7))
                    return inst
                q.op("tensor", mm1, waits=w_)
                mm1_done = q.cnt["tensor"]
                w_ = {"gpsimd": w3_loaded}
                if i >= 1:
                    w_["vector"] = vmarks[i - 1]
                def mm3(e, mc=mc):
                    for kc in range(8):
                        inst = e.matmul(psB[:, 0:TC], w3sb[:, kc, mc * 128:(mc + 1) * 128],
                                        x3bf[:, kc, :], start=(kc == 0), stop=(kc == 7))
                    return inst
                q.op("tensor", mm3, waits=w_)
                sw = {"tensor": mm1_done}
                if i >= 2:
                    sw["vector"] = vmarks[i - 2]
                q.op("scalar", lambda e, mc=mc: e.activation(
                    scr[:, mc % 2, :], psA[:, 0:TC], AF.Silu), waits=sw)
                smarks[i] = q.cnt["scalar"]
                q.op("vector", lambda e, mc=mc: e.tensor_tensor(
                    hsb[:, mc, :], scr[:, mc % 2, :], psB[:, 0:TC], op=ALU.mult),
                    waits={"tensor": q.cnt["tensor"], "scalar": smarks[i]})
                vmarks[i] = q.cnt["vector"]
            h_done = q.cnt["vector"]
            w1_rel = {"tensor": q.cnt["tensor"]}
            w3_rel = {"tensor": q.cnt["tensor"]}
            eomarks = {}
            for i in range(8):
                dc = i
                hf = dc // 4
                if i < 2:
                    q.dma("gpsimd", lambda e, ex=ex, hf2=i: e.dma_start(
                        out=w2sb[hf2][:],
                        in_=w2_d[ex, :, hf2 * 512:(hf2 + 1) * 512].rearrange(
                            "(kc p) m -> p kc m", p=128)),
                        waits=dict(w2_rel[i]))
                    w2_rel[i] = {}
                    w2_rel[i]["loaded"] = q.cnt["gpsimd"]
                w_ = {"gpsimd": w2_rel[hf]["loaded"], "vector": h_done,
                      "scalar": smarks[15]}
                if i >= 2:
                    w_["vector"] = max(h_done, eomarks[i - 2])
                def mmo(e, dc=dc, hf=hf):
                    for kc in range(16):
                        inst = e.matmul(
                            (psA if dc % 2 == 0 else psB)[:, 0:TC],
                            w2sb[hf][:, kc, (dc % 4) * 128:(dc % 4 + 1) * 128],
                            hsb[:, kc, :], start=(kc == 0), stop=(kc == 15))
                    return inst
                q.op("tensor", mmo, waits=w_)
                q.op("vector", lambda e, dc=dc: e.tensor_add(
                    acc[:, dc, :], acc[:, dc, :], (psA if dc % 2 == 0 else psB)[:, 0:TC]),
                    waits={"tensor": q.cnt["tensor"], "vector": memset_mark})
                eomarks[i] = q.cnt["vector"]
            w2_rel = [{"tensor": q.cnt["tensor"], "loaded": w2_rel[0]["loaded"]},
                      {"tensor": q.cnt["tensor"], "loaded": w2_rel[1]["loaded"]}]
            for d_ in w2_rel:
                d_.pop("loaded", None)
        for kc in range(8):
            q.op("vector", lambda e, kc=kc: e.tensor_add(
                acc[:, kc, :], acc[:, kc, :], xres[:, kc, :]))
        ln.emit(acc, acc, 8)
        q.dma("sync", lambda e, tc0=tc0: e.dma_start(
            out=x35_d[:, tc0:tc0 + TC].rearrange("(kc p) m -> p kc m", p=128),
            in_=acc[:]))
        for kc in range(8):
            q.op("vector", lambda e, kc=kc, tc0=tc0: e.tensor_copy(
                x35bf[:, kc, tc0:tc0 + TC], acc[:, kc, :]))
    p.pop()

    p.push()
    EPS3 = 1.1920929e-07
    p.const(EPS3)
    wdqt = p.sb("G_wdqt", [64, 512], BF16)
    q.dma("gpsimd", lambda e: e.dma_start(out=wdqt[:], in_=wdqt_d[:]))
    wuqt = p.sb("G_wuqt", [128, 4, 64], BF16)
    q.dma("gpsimd", lambda e: e.dma_start(
        out=wuqt[:], in_=wuqt_d.rearrange("(kc p) m -> p kc m", p=128)))
    wdkvt = p.sb("G_wdkvt", [64, 1024], BF16)
    q.dma("gpsimd", lambda e: e.dma_start(out=wdkvt[:], in_=wdkvt_d[:]))
    wukvt = p.sb("G_wukvt", [128, 8, 128], BF16)
    q.dma("gpsimd", lambda e: e.dma_start(
        out=wukvt[:], in_=wukvt_d.rearrange("(kc p) m -> p kc m", p=128)))
    wqrt = p.sb("G_wqrt", [64, 64], BF16)
    q.dma("gpsimd", lambda e: e.dma_start(out=wqrt[:], in_=wqrt_d[:]))
    wkrt = p.sb("G_wkrt", [64, 64], BF16)
    q.dma("gpsimd", lambda e: e.dma_start(out=wkrt[:], in_=wkrt_d[:]))
    qnw = p.sb("G_qnw", [64, 1], F32)
    q.dma("sync", lambda e: e.dma_start(out=qnw[:], in_=qnw_d[:]))
    knw = p.sb("G_knw", [64, 1], F32)
    q.dma("sync", lambda e: e.dma_start(out=knw[:], in_=knw_d[:]))

    xh0 = p.sb("G_xh0", [64, W], BF16)
    cq = p.sb("G_cq", [128, 4, W], BF16)
    ckv = p.sb("G_ckv", [128, 8, W], BF16)
    qc = p.sb("G_qc", [64, W], F32)
    kvf = p.sb("G_kvf", [128, W], F32)
    sq64 = p.sb("G_sq64", [64, W], F32)
    row = p.sb("G_row", [1, W], F32)
    rb64 = p.sb("G_rb64", [64, W], F32)
    outb = p.sb("G_outb", [64, W], BF16)

    def rmsnorm64(src_ap, w_col, dst_bf):
        q.op("scalar", lambda e: e.activation(sq64[:], src_ap, AF.Square))
        for it in range(0, W, 512):
            q.op("tensor", lambda e, it=it: e.matmul(
                ps1[:, 0:512], onescF[0:64, :], sq64[:, it:it + 512],
                start=True, stop=True))
            q.op("vector", lambda e, it=it: e.tensor_scalar_mul(
                row[:, it:it + 512], ps1[:, 0:512], 1.0 / 64))
        q.op("scalar", lambda e: e.activation(row[:], row[:], AF.Ln, bias=EPS3))
        q.op("scalar", lambda e: e.activation(row[:], row[:], AF.Exp, scale=-0.5))
        for it in range(0, W, 512):
            q.op("tensor", lambda e, it=it: e.matmul(
                psB[0:64, 0:512], ones1f[0:1, 0:64], row[:, it:it + 512],
                start=True, stop=True))
            q.op("vector", lambda e, it=it: e.tensor_copy(
                rb64[:, it:it + 512], psB[0:64, 0:512]))
        q.op("vector", lambda e: e.tensor_mul(rb64[:], src_ap, rb64[:]))
        q.op("vector", lambda e: e.tensor_scalar_mul(rb64[:], rb64[:], w_col))
        q.op("vector", lambda e: e.tensor_copy(dst_bf, rb64[:]))

    def pgroup(units):
        entry = q.snap()
        marks = {}
        for i, (mmf, dstf) in enumerate(units):
            bank = psA if i % 2 == 0 else psB
            tw = dict(entry)
            if i >= 2:
                tw["vector"] = max(tw.get("vector", 0), marks[i - 2])
            q.op("tensor", lambda e, mmf=mmf, bank=bank: mmf(e, bank), waits=tw)
            q.op("vector", lambda e, dstf=dstf, bank=bank: dstf(e, bank),
                 waits={"tensor": q.cnt["tensor"]})
            marks[i] = q.cnt["vector"]

    for h in range(16):
        hc, hb = h // 2, (h % 2) * 64
        q.op("vector", lambda e, hc=hc, hb=hb: e.tensor_copy(
            xh0[:], x35bf[hb:hb + 64, hc, :]))
        pgroup([
            (lambda e, bank, mc=mc, it=it: e.matmul(
                bank[:, 0:512], wdqt[:, mc * 128:(mc + 1) * 128],
                xh0[:, it:it + 512], start=True, stop=True),
             lambda e, bank, mc=mc, it=it: e.tensor_copy(
                cq[:, mc, it:it + 512], bank[:, 0:512]))
            for mc in range(4) for it in range(0, W, 512)])

        def mmq_u(e, bank, it):
            for kc in range(4):
                inst = e.matmul(bank[0:64, 0:512], wuqt[:, kc, :],
                                cq[:, kc, it:it + 512],
                                start=(kc == 0), stop=(kc == 3))
            return inst
        pgroup([
            (lambda e, bank, it=it: mmq_u(e, bank, it),
             lambda e, bank, it=it: e.tensor_copy(
                qc[:, it:it + 512], bank[0:64, 0:512]))
            for it in range(0, W, 512)])
        rmsnorm64(qc[:], qnw[:, 0:1], outb[:])
        q.dma("sync", lambda e, h=h: e.dma_start(out=qH_d[h, 0:64, :], in_=outb[:]))
        pgroup([
            (lambda e, bank, it=it: e.matmul(
                bank[0:64, 0:512], wqrt[:], outb[:, it:it + 512],
                start=True, stop=True),
             lambda e, bank, it=it: e.tensor_copy(
                sq64[:, it:it + 512], bank[0:64, 0:512]))
            for it in range(0, W, 512)])
        q.op("vector", lambda e: e.tensor_copy(outb[:], sq64[:]))
        q.dma("sync", lambda e, h=h: e.dma_start(out=qH_d[h, 64:128, :], in_=outb[:]))
        pgroup([
            (lambda e, bank, mc=mc, it=it: e.matmul(
                bank[:, 0:512], wdkvt[:, mc * 128:(mc + 1) * 128],
                xh0[:, it:it + 512], start=True, stop=True),
             lambda e, bank, mc=mc, it=it: e.tensor_copy(
                ckv[:, mc, it:it + 512], bank[:, 0:512]))
            for mc in range(8) for it in range(0, W, 512)])

        def mmkv_u(e, bank, it):
            for kc in range(8):
                inst = e.matmul(bank[:, 0:512], wukvt[:, kc, :],
                                ckv[:, kc, it:it + 512],
                                start=(kc == 0), stop=(kc == 7))
            return inst
        pgroup([
            (lambda e, bank, it=it: mmkv_u(e, bank, it),
             lambda e, bank, it=it: e.tensor_copy(
                kvf[:, it:it + 512], bank[:, 0:512]))
            for it in range(0, W, 512)])
        rmsnorm64(kvf[0:64, :], knw[:, 0:1], outb[:])
        q.dma("sync", lambda e, h=h: e.dma_start(out=kH_d[h, 0:64, :], in_=outb[:]))
        q.op("vector", lambda e: e.tensor_copy(outb[:], kvf[64:128, :]))
        q.dma("sync", lambda e, h=h: e.dma_start(out=vH_d[h, :, :], in_=outb[:]))
        pgroup([
            (lambda e, bank, it=it: e.matmul(
                bank[0:64, 0:512], wkrt[:], xh0[:, it:it + 512],
                start=True, stop=True),
             lambda e, bank, it=it: e.tensor_copy(
                sq64[:, it:it + 512], bank[0:64, 0:512]))
            for it in range(0, W, 512)])
        q.op("vector", lambda e: e.tensor_copy(outb[:], sq64[:]))
        q.dma("sync", lambda e, h=h: e.dma_start(out=kH_d[h, 64:128, :], in_=outb[:]))
    p.pop()
    return p.finish()



def build_H1():
    p = Prog("H1")
    q = p.q
    q_d = p.din("qH1", [4, 128, T], BF16)
    k_d = p.din("kH1", [4, 128, T], BF16)
    v_d = p.din("vH1", [4, 64, T], BF16)
    ctx_d = p.dout("ctxH1", [4, 64, T], BF16)

    psS = p.ps("H1_psS", [128, 512], F32)
    psC = p.ps("H1_psC", [128, 512], F32)
    psB_ = p.ps("H1_psB", [128, 512], F32)
    psD1 = p.ps("H1_psD", [128, 512], F32)
    psE = p.ps("H1_psE", [128, 512], F32)
    psF = p.ps("H1_psF", [128, 512], F32)
    ps1 = psB_[0:1, :]
    psT = p.ps("H1_psT", [128, 128], BF16)

    ident = p.sb("H1_ident", [128, 128], BF16)
    q.op("gpsimd", lambda e: e.memset(ident[:], 1.0))
    q.op("gpsimd", lambda e: e.affine_select(
        ident[:], ident[:], pattern=[[1, 128]], compare_op=ALU.is_equal,
        fill=0.0, base=0, channel_multiplier=-1))
    ones1 = p.sb("H1_ones1", [1, 128], F32)
    q.op("gpsimd", lambda e: e.memset(ones1[:], 1.0))
    onescF = p.sb("H1_onescF", [128, 1], F32)
    q.op("gpsimd", lambda e: e.memset(onescF[:], 1.0))
    m4 = p.sb("H1_m4", [128, 4, 512], F32)
    for dlt in range(4):
        q.op("gpsimd", lambda e, dlt=dlt: e.iota(
            m4[:, dlt, :], pattern=[[1, 512]], base=-128 * dlt,
            channel_multiplier=-1, allow_small_or_imprecise_dtypes=True))
    q.op("vector", lambda e: e.tensor_single_scalar(m4[:], m4[:], 0.0, op=ALU.is_ge))
    q.op("vector", lambda e: e.tensor_scalar(
        m4[:], m4[:], 8e9, -8e9, op0=ALU.mult, op1=ALU.add))

    qsb = p.sb("H1_q", [128, T], BF16)
    ksb = p.sb("H1_k", [128, T], BF16)
    vsb = p.sb("H1_v", [64, T], BF16)
    vaug = p.sb("H1_vaug", [128, 64, 65], BF16)
    ctxout = p.sb("H1_ctx", [64, T], BF16)
    sm = p.sb("H1_sm", [128, 512], F32)
    pt = p.sb("H1_pt", [128, 512], BF16)
    smB = p.sb("H1_smB", [128, 512], F32)
    ptB = p.sb("H1_ptB", [128, 512], BF16)
    sm4 = p.sb("H1_sm4", [128, 4, 512], F32)
    pt4 = p.sb("H1_pt4", [128, 4, 512], BF16)
    ctx = p.sb("H1_ctxf", [65, 512], F32)
    rb = p.sb("H1_rb", [64, 512], F32)
    rrow = p.sb("H1_rrow", [1, 512], F32)
    negb = p.sb("H1_negb", [128, 1], F32)
    nrm = p.sb("H1_nrm", [1, 8], F32)
    sqs = p.sb("H1_sqs", [128, 512], F32)

    for pr in range(4):
        q.dma("sync", lambda e, pr=pr: e.dma_start(out=qsb[:], in_=q_d[pr]))
        q.dma("sync", lambda e, pr=pr: e.dma_start(out=ksb[:], in_=k_d[pr]))
        q.dma("sync", lambda e, pr=pr: e.dma_start(out=vsb[:], in_=v_d[pr]))
        q.op("gpsimd", lambda e: e.memset(vaug[:], 0.0))
        for kt in range(64):
            q.op("tensor", lambda e, kt=kt: e.transpose(
                psT[:, 0:64], vsb[:, kt * 128:(kt + 1) * 128], ident[0:64, 0:64]))
            q.op("vector", lambda e, kt=kt: e.tensor_copy(
                vaug[:, kt, 0:64], psT[:, 0:64]))
        q.op("gpsimd", lambda e: e.memset(vaug[:, :, 64:65], 1.0))
        for (src_, col) in ((qsb, 0), (ksb, 1)):
            for i, t0 in enumerate(range(0, T, 512)):
                q.op("scalar", lambda e, src_=src_, t0=t0: e.activation(
                    sqs[:], src_[:, t0:t0 + 512], AF.Square))
                q.op("tensor", lambda e: e.matmul(
                    ps1[:, 0:512], onescF[:], sqs[:], start=True, stop=True))
                tgt = col if i == 0 else 2
                q.op("vector", lambda e, tgt=tgt: e.reduce_max(
                    nrm[:, tgt:tgt + 1], ps1[:, 0:512], axis=AX.X))
                if i > 0:
                    q.op("vector", lambda e, col=col: e.tensor_max(
                        nrm[:, col:col + 1], nrm[:, col:col + 1], nrm[:, 2:3]))
        q.op("scalar", lambda e: e.activation(nrm[:, 0:1], nrm[:, 0:1], AF.Sqrt))
        q.op("scalar", lambda e: e.activation(nrm[:, 1:2], nrm[:, 1:2], AF.Sqrt))
        q.op("vector", lambda e: e.tensor_mul(nrm[:, 3:4], nrm[:, 0:1], nrm[:, 1:2]))
        q.op("vector", lambda e: e.tensor_scalar_mul(nrm[:, 3:4], nrm[:, 3:4], -0.125))
        q.op("tensor", lambda e: e.matmul(
            psB_[:, 0:1], ones1[:], nrm[:, 3:4], start=True, stop=True))
        q.op("vector", lambda e: e.tensor_copy(negb[:], psB_[:, 0:1]))
        entry = q.snap()
        for qt in range(16):
            qt0 = qt * 512
            nkt = 4 * (qt + 1)
            freemk = {}
            expp = {}
            pvp = {}
            for j in range(nkt // 2):
                kts = (2 * j, 2 * j + 1)
                BP = (psS, psD1) if j % 2 == 0 else (psE, psF)
                sl = (0, 1) if j % 2 == 0 else (2, 3)
                tw = dict(entry)
                if j >= 2:
                    for eng_r, cnt_r in freemk[j - 2].items():
                        tw[eng_r] = max(tw.get(eng_r, 0), cnt_r)

                def mmqk(e, kts=kts, qt0=qt0, BP=BP):
                    for m in range(2):
                        inst = e.matmul(
                            BP[m][:, 0:512],
                            ksb[:, kts[m] * 128:(kts[m] + 1) * 128],
                            qsb[:, qt0:qt0 + 512], start=True, stop=True)
                    return inst
                q.op("tensor", mmqk, waits=tw)
                qk_cnt = q.cnt["tensor"]
                dlts = [kts[m] - 4 * qt for m in range(2)]
                diag = [m for m in range(2) if dlts[m] >= 0]
                vcnt = None
                if diag:
                    vw = {"tensor": qk_cnt}
                    if j >= 2:
                        vw["scalar"] = expp[j - 2]

                    def mmadd(e, diag=diag, dlts=dlts, BP=BP, sl=sl):
                        for m in diag:
                            inst = e.tensor_add(
                                sm4[:, sl[m], :], BP[m][:, 0:512], m4[:, dlts[m], :])
                        return inst
                    q.op("vector", mmadd, waits=vw)
                    vcnt = q.cnt["vector"]
                ew = {"tensor": qk_cnt}
                if j >= 2:
                    ew["tensor"] = max(ew["tensor"], pvp[j - 2])
                if vcnt is not None:
                    ew["vector"] = vcnt

                def mmexp(e, diag=diag, BP=BP, sl=sl):
                    for m in range(2):
                        src = sm4[:, sl[m], :] if m in diag else BP[m][:, 0:512]
                        inst = e.activation(
                            pt4[:, sl[m], :], src, AF.Exp,
                            bias=negb[:, 0:1], scale=0.125)
                    return inst
                q.op("scalar", mmexp, waits=ew)
                expp[j] = q.cnt["scalar"]
                freemk[j] = {"scalar": expp[j]}
                if vcnt is not None:
                    freemk[j]["vector"] = vcnt

                def mmpv(e, kts=kts, nkt=nkt, sl=sl):
                    for m in range(2):
                        inst = e.matmul(
                            psC[0:65, 0:512], vaug[:, kts[m], :], pt4[:, sl[m], :],
                            start=(kts[m] == 0), stop=(kts[m] == nkt - 1))
                    return inst
                q.op("tensor", mmpv, waits={"scalar": expp[j]})
                pvp[j] = q.cnt["tensor"]
            q.op("vector", lambda e: e.tensor_copy(ctx[:], psC[0:65, 0:512]),
                 waits={"tensor": q.cnt["tensor"]})
            q.op("vector", lambda e: e.reciprocal(rrow[:], ctx[64:65, :]), waits={})
            q.op("tensor", lambda e: e.matmul(
                psB_[0:64, 0:512], ones1[0:1, 0:64], rrow[:],
                start=True, stop=True), waits={"vector": q.cnt["vector"]})
            q.op("vector", lambda e: e.tensor_copy(rb[:], psB_[0:64, 0:512]),
                 waits={"tensor": q.cnt["tensor"]})
            q.op("vector", lambda e: e.tensor_mul(ctx[0:64, :], ctx[0:64, :], rb[:]),
                 waits={})
            q.op("vector", lambda e, qt0=qt0: e.tensor_copy(
                ctxout[:, qt0:qt0 + 512], ctx[0:64, :]), waits={})
            entry = q.snap()
        q.dma("sync", lambda e, pr=pr: e.dma_start(out=ctx_d[pr], in_=ctxout[:]))
    return p.finish()



def build_H2():
    p = Prog("H2")
    q = p.q
    x35_d = p.din("x35H", [D, W])
    ctx_d = p.din("ctxH2", [D, W], BF16)
    wot_d = p.din("WOT", [64, 64])
    rwt_d = p.din("routerW", [2, D, 6])
    rb_d = p.din("routerB", [2, 6])
    gb_d = p.din("gateB", [2, 6])
    wup_d = p.din("expWup", [6, D, D])
    bup_d = p.din("expBup", [128, 6, 8])
    wgg_d = p.din("expWg", [6, D, D])
    bgg_d = p.din("expBg", [128, 6, 8])
    wdn_d = p.din("expWdown", [6, D, D])
    bdn_d = p.din("expBdown", [128, 6, 8])
    swup_d = p.din("shWup", [D, 2 * D])
    sbup_d = p.din("shBup", [128, 16])
    swg_d = p.din("shWg", [D, D])
    sbg_d = p.din("shBg", [128, 8])
    swdn_d = p.din("shWdown", [D, D])
    sbdn_d = p.din("shBdown", [128, 8])
    nw_d = p.din("normW", [128, 8, 2])
    out_d = p.dout("outH2", [D, W])
    x4_dram = p.dtmp("H2_x4", [D, W], F32)
    xf_dram = p.dtmp("H2_xf", [D, W], F32)
    ws_dram = p.dtmp("H2_ws", [6, W], F32)

    psA = p.ps("H2_psA", [128, 512], F32)
    psB = p.ps("H2_psB", [128, 512], F32)
    psD = p.ps("H2_psD", [128, 512], F32)
    ps1 = psD[0:1, :]
    ps6 = psD[0:6, :]
    pst = p.ps("H2_pst", [128, 128], F32)
    ws_done = {}

    ones1 = p.sb("H2_ones1", [1, 128], BF16)
    q.op("gpsimd", lambda e: e.memset(ones1[:], 1.0))
    ones1f = p.sb("H2_ones1f", [1, 128], F32)
    q.op("gpsimd", lambda e: e.memset(ones1f[:], 1.0))
    onescF = p.sb("H2_onescF", [128, 1], F32)
    q.op("gpsimd", lambda e: e.memset(onescF[:], 1.0))
    identF = p.sb("H2_identF", [128, 128], F32)
    q.op("gpsimd", lambda e: e.memset(identF[:], 1.0))
    q.op("gpsimd", lambda e: e.affine_select(
        identF[:], identF[:], pattern=[[1, 128]], compare_op=ALU.is_equal,
        fill=0.0, base=0, channel_multiplier=-1))
    ln = LNHelper(p, 512, "h2", psA, ps1)
    p.const(1e-6)

    p.push()
    wot = p.sb("H2_wot", [64, 64], BF16)
    q.dma("gpsimd", lambda e: e.dma_start(out=wot[:], in_=wot_d[:]))
    ctxb = p.sb("H2_ctxb", [128, 8, W], BF16)
    q.dma("sync", lambda e: e.dma_start(
        out=ctxb[:], in_=ctx_d.rearrange("(kc p) m -> p kc m", p=128)))
    ctx0 = p.sb("H2_ctx0", [64, W], BF16)
    x4t = p.sb("H2_x4t", [128, 8, 512], F32)
    att = p.sb("H2_att", [128, 8, W], F32)
    for h in range(16):
        hc, hb = h // 2, (h % 2) * 64
        q.op("vector", lambda e, hc=hc, hb=hb: e.tensor_copy(
            ctx0[:], ctxb[hb:hb + 64, hc, :]))
        for it in range(0, W, 512):
            q.op("tensor", lambda e, it=it: e.matmul(
                psA[0:64, 0:512], wot[:], ctx0[:, it:it + 512],
                start=True, stop=True))
            q.op("vector", lambda e, it=it, hc=hc, hb=hb: e.tensor_copy(
                att[hb:hb + 64, hc, it:it + 512], psA[0:64, 0:512]))
    for (t0, w) in [(i * 512, 512) for i in range(4)]:
        q.dma("sync", lambda e, t0=t0: e.dma_start(
            out=x4t[:], in_=x35_d[:, t0:t0 + 512].rearrange("(kc p) m -> p kc m", p=128)))
        for kc in range(8):
            q.op("vector", lambda e, kc=kc, t0=t0: e.tensor_add(
                x4t[:, kc, :], x4t[:, kc, :], att[:, kc, t0:t0 + 512]))
        ln.emit(x4t, x4t, 8)
        q.dma("sync", lambda e, t0=t0: e.dma_start(
            out=x4_dram[:, t0:t0 + 512].rearrange("(kc p) m -> p kc m", p=128),
            in_=x4t[:]))
        q.dma("sync", lambda e, t0=t0: e.dma_start(
            out=xf_dram[:, t0:t0 + 512].rearrange("(kc p) m -> p kc m", p=128),
            in_=x4t[:]))
    p.pop()

    rbv = p.sb("H2_rbv", [6, 2], F32)
    q.dma("sync", lambda e: e.dma_start(out=rbv[:], in_=rb_d.rearrange("c e -> e c"),
                                        allow_slow_non_contiguous=True))
    gbv = p.sb("H2_gbv", [6, 2], F32)
    q.dma("sync", lambda e: e.dma_start(out=gbv[:], in_=gb_d.rearrange("c e -> e c"),
                                        allow_slow_non_contiguous=True))
    bup = p.sb("H2_bup", [128, 6, 8], F32)
    q.dma("sync", lambda e: e.dma_start(out=bup[:], in_=bup_d[:]))
    bgg = p.sb("H2_bgg", [128, 6, 8], F32)
    q.dma("sync", lambda e: e.dma_start(out=bgg[:], in_=bgg_d[:]))
    bdn = p.sb("H2_bdn", [128, 6, 8], F32)
    q.dma("sync", lambda e: e.dma_start(out=bdn[:], in_=bdn_d[:]))
    sbup = p.sb("H2_sbup", [128, 16], F32)
    q.dma("sync", lambda e: e.dma_start(out=sbup[:], in_=sbup_d[:]))
    sbg = p.sb("H2_sbg", [128, 8], F32)
    q.dma("sync", lambda e: e.dma_start(out=sbg[:], in_=sbg_d[:]))
    sbdn = p.sb("H2_sbdn", [128, 8], F32)
    q.dma("sync", lambda e: e.dma_start(out=sbdn[:], in_=sbdn_d[:]))
    nw = p.sb("H2_nw", [128, 8, 2], F32)
    q.dma("sync", lambda e: e.dma_start(out=nw[:], in_=nw_d[:]))

    TC = 512
    for cyc in range(2):
        p.push()
        xfbr = p.sb(f"H2_xfbr_{cyc}", [128, 8, W], BF16)
        q.dma("gpsimd", lambda e, xfbr=xfbr: e.dma_start(
            out=xfbr[:], in_=xf_dram.rearrange("(kc p) m -> p kc m", p=128)))
        rwt = p.sb(f"H2_rwt_{cyc}", [128, 8, 6], BF16)
        q.dma("gpsimd", lambda e, cyc=cyc: e.dma_start(
            out=rwt[:], in_=rwt_d[cyc].rearrange("(kc p) m -> p kc m", p=128),
            allow_slow_non_contiguous=True))
        gate = p.sb(f"H2_gate_{cyc}", [6, W], F32)
        for it in range(0, W, 512):
            def mmr(e, it=it, rwt=rwt, xfbr=xfbr):
                for kc in range(8):
                    inst = e.matmul(ps6[:, 0:512], rwt[:, kc, :],
                                    xfbr[:, kc, it:it + 512], start=(kc == 0), stop=(kc == 7))
                return inst
            q.op("tensor", mmr)
            q.op("scalar", lambda e, it=it, cyc=cyc: e.activation(
                gate[:, it:it + 512], ps6[:, 0:512], AF.Identity,
                bias=rbv[:, cyc:cyc + 1]))
        gt = p.sb(f"H2_gt_{cyc}", [128, 16, 6], F32)
        bt = p.sb(f"H2_bt_{cyc}", [128, 16, 6], F32)
        red = p.sb(f"H2_red_{cyc}", [128, 16, 1], F32)
        red2 = p.sb(f"H2_red2_{cyc}", [128, 16, 1], F32)
        eqm = p.sb(f"H2_eqm_{cyc}", [128, 16, 6], F32)
        for ti in range(16):
            q.op("tensor", lambda e, ti=ti: e.transpose(
                pst[:, 0:6], gate[:, ti * 128:(ti + 1) * 128], identF[0:6, 0:6]))
            q.op("vector", lambda e, ti=ti: e.tensor_copy(gt[:, ti, :], pst[:, 0:6]))
        q.op("vector", lambda e, cyc=cyc: e.tensor_scalar_add(
            gate[:], gate[:], gbv[:, cyc:cyc + 1]))
        for ti in range(16):
            q.op("tensor", lambda e, ti=ti: e.transpose(
                pst[:, 0:6], gate[:, ti * 128:(ti + 1) * 128], identF[0:6, 0:6]))
            q.op("vector", lambda e, ti=ti: e.tensor_copy(bt[:, ti, :], pst[:, 0:6]))
        q.op("vector", lambda e: e.tensor_reduce(red[:], bt[:], axis=AX.X, op=ALU.max))
        q.op("vector", lambda e: e.tensor_tensor(
            eqm[:], bt[:], red[:].broadcast_to([128, 16, 6]), op=ALU.is_ge))
        q.op("vector", lambda e: e.scalar_tensor_tensor(
            eqm[:], eqm[:], -1e30, bt[:], op0=ALU.mult, op1=ALU.add))
        q.op("vector", lambda e: e.tensor_reduce(red2[:], eqm[:], axis=AX.X, op=ALU.max))
        q.op("vector", lambda e: e.tensor_tensor(
            eqm[:], bt[:], red2[:].broadcast_to([128, 16, 6]), op=ALU.is_ge))
        q.op("vector", lambda e: e.tensor_mul(gt[:], gt[:], eqm[:]))
        q.op("vector", lambda e: e.tensor_scalar(
            eqm[:], eqm[:], 1e30, -1e30, op0=ALU.mult, op1=ALU.add))
        q.op("vector", lambda e: e.tensor_add(gt[:], gt[:], eqm[:]))
        q.op("vector", lambda e: e.tensor_reduce(red[:], gt[:], axis=AX.X, op=ALU.max))
        q.op("vector", lambda e: e.tensor_sub(gt[:], gt[:], red[:].broadcast_to([128, 16, 6])))
        q.op("scalar", lambda e: e.activation(gt[:], gt[:], AF.Exp))
        q.op("vector", lambda e: e.tensor_reduce(red[:], gt[:], axis=AX.X, op=ALU.add))
        q.op("vector", lambda e: e.reciprocal(red[:], red[:]))
        q.op("vector", lambda e: e.tensor_mul(gt[:], gt[:], red[:].broadcast_to([128, 16, 6])))
        for ti in range(16):
            q.op("tensor", lambda e, ti=ti: e.transpose(
                pst[0:6, :], gt[:, ti, :], identF[:]))
            q.op("vector", lambda e, ti=ti: e.tensor_copy(
                gate[:, ti * 128:(ti + 1) * 128], pst[0:6, 0:128]))
        q.dma("sync", lambda e: e.dma_start(out=ws_dram[:], in_=gate[:]))
        ws_done[cyc] = q.cnt["sync"]
        p.pop()

        p.push()
        TCH = 512
        xfb = p.sb(f"H2_xfb2_{cyc}", [128, 8, TCH], BF16)
        xfres = p.sb(f"H2_xfres_{cyc}", [128, 8, TCH], F32)
        acc = p.sb(f"H2_acc_{cyc}", [128, 8, TCH], F32)
        h1 = p.sb(f"H2_h1_{cyc}", [128, 8, TCH], BF16)
        h2t = p.sb(f"H2_h2t_{cyc}", [128, 8, TCH], BF16)
        wb = p.sb(f"H2_wb_{cyc}", [128, TCH], F32)
        scr = p.sb(f"H2_scr_{cyc}", [128, 2, TCH], F32)
        wrow1 = p.sb(f"H2_wrow1_{cyc}", [1, TCH], F32)
        wrow1b = p.sb(f"H2_wrow1b_{cyc}", [1, TCH], BF16)
        sgrp = p.sb(f"H2_sgrp_{cyc}", [128, 16, TCH], BF16)
        sg2 = p.sb(f"H2_sg2_{cyc}", [128, 16, TCH], BF16)
        poolA = p.sb(f"H2_poolA_{cyc}", [128, 8, 2048], BF16)
        poolB = p.sb(f"H2_poolB_{cyc}", [128, 8, 2048], BF16)
        relA = {}
        relB = {}
        for tc0 in range(0, W, TCH):
            q.dma("gpsimd", lambda e, tc0=tc0: e.dma_start(
                out=xfb[:], in_=xf_dram[:, tc0:tc0 + TCH].rearrange("(kc p) m -> p kc m", p=128)))
            q.dma("sync", lambda e, tc0=tc0: e.dma_start(
                out=xfres[:], in_=xf_dram[:, tc0:tc0 + TCH].rearrange("(kc p) m -> p kc m", p=128)))
            q.op("vector", lambda e: e.memset(acc[:], 0.0))
            entry = q.snap()
            accbase = q.cnt["vector"]

            def units8(wview, xview, evacs, loaded, entry, extra=None):
                marks = {}
                for i in range(8):
                    bank = psA if i % 2 == 0 else psB
                    tw = dict(entry)
                    tw["gpsimd"] = max(tw.get("gpsimd", 0), loaded)
                    if extra:
                        for k2, v2 in extra.items():
                            tw[k2] = max(tw.get(k2, 0), v2)
                    if i >= 2:
                        eng_r, cnt_r = marks[i - 2]
                        tw[eng_r] = max(tw.get(eng_r, 0), cnt_r)

                    def mmx(e, i=i, bank=bank, wview=wview, xview=xview):
                        for kc in range(8):
                            inst = e.matmul(bank[:, 0:TCH],
                                            wview(kc, i), xview(kc),
                                            start=(kc == 0), stop=(kc == 7))
                        return inst
                    q.op("tensor", mmx, waits=tw)
                    marks[i] = evacs(i, bank, {"tensor": q.cnt["tensor"]})
                return marks

            for ex in range(6):
                q.dma("gpsimd", lambda e, ex=ex: e.dma_start(
                    out=poolA[:, :, 0:1024],
                    in_=wup_d[ex].rearrange("(kc p) m -> p kc m", p=128)),
                    waits=dict(relA))
                ldA = q.cnt["gpsimd"]

                def ev1(i, bank, tw, ex=ex):
                    q.op("scalar", lambda e, i=i, bank=bank: e.activation(
                        h1[:, i, :], bank[:, 0:TCH], AF.Identity,
                        bias=bup[:, ex, i:i + 1]), waits=tw)
                    return ("scalar", q.cnt["scalar"])
                units8(lambda kc, i: poolA[:, kc, i * 128:(i + 1) * 128],
                       lambda kc: xfb[:, kc, :], ev1, ldA, entry)
                relA = {"tensor": q.cnt["tensor"]}
                h1_done = q.cnt["scalar"]
                q.dma("gpsimd", lambda e, ex=ex: e.dma_start(
                    out=poolB[:, :, 0:1024],
                    in_=wgg_d[ex].rearrange("(kc p) m -> p kc m", p=128)),
                    waits=dict(relB))
                ldB = q.cnt["gpsimd"]

                def ev2(i, bank, tw, ex=ex):
                    q.op("scalar", lambda e, i=i, bank=bank: e.activation(
                        scr[:, i % 2, :], bank[:, 0:TCH], AF.Silu,
                        bias=bgg[:, ex, i:i + 1]), waits=tw)
                    q.op("vector", lambda e, i=i: e.tensor_tensor(
                        h2t[:, i, :], scr[:, i % 2, :], h1[:, i, :], op=ALU.mult),
                        waits={"scalar": q.cnt["scalar"]})
                    return ("vector", q.cnt["vector"])
                units8(lambda kc, i: poolB[:, kc, i * 128:(i + 1) * 128],
                       lambda kc: h1[:, kc, :], ev2, ldB, entry,
                       extra={"scalar": h1_done})
                relB = {"tensor": q.cnt["tensor"]}
                h2_done = q.cnt["vector"]
                q.dma("sync", lambda e, ex=ex, tc0=tc0: e.dma_start(
                    out=wrow1[:], in_=ws_dram[ex:ex + 1, tc0:tc0 + TCH]),
                    waits={"vector": q.cnt["vector"], "sync": ws_done[cyc]})
                q.op("vector", lambda e: e.tensor_copy(wrow1b[:], wrow1[:]),
                     waits={"sync": q.cnt["sync"]})
                q.op("tensor", lambda e: e.matmul(
                    psD[:, 0:TCH], ones1[:], wrow1b[:], start=True, stop=True),
                    waits={"vector": q.cnt["vector"]})
                q.op("vector", lambda e: e.tensor_copy(wb[:], psD[:, 0:TCH]),
                     waits={"tensor": q.cnt["tensor"]})
                wb_done = q.cnt["vector"]
                q.dma("gpsimd", lambda e, ex=ex: e.dma_start(
                    out=poolA[:, :, 0:1024],
                    in_=wdn_d[ex].rearrange("(kc p) m -> p kc m", p=128)),
                    waits=dict(relA))
                ldA = q.cnt["gpsimd"]

                def ev3(i, bank, tw, ex=ex, wb_done=wb_done, accbase=accbase):
                    q.op("scalar", lambda e, i=i, bank=bank: e.activation(
                        scr[:, i % 2, :], bank[:, 0:TCH], AF.Identity,
                        bias=bdn[:, ex, i:i + 1]), waits=tw)
                    sm_ = q.cnt["scalar"]
                    q.op("vector", lambda e, i=i: e.tensor_mul(
                        scr[:, i % 2, :], scr[:, i % 2, :], wb[:]),
                        waits={"scalar": sm_, "vector": max(wb_done, accbase)})
                    q.op("vector", lambda e, i=i: e.tensor_add(
                        acc[:, i, :], acc[:, i, :], scr[:, i % 2, :]), waits={})
                    return ("scalar", sm_)
                units8(lambda kc, i: poolA[:, kc, i * 128:(i + 1) * 128],
                       lambda kc: h2t[:, kc, :], ev3, ldA, entry,
                       extra={"vector": h2_done})
                relA = {"tensor": q.cnt["tensor"]}
            q.dma("gpsimd", lambda e: e.dma_start(
                out=poolB[:],
                in_=swup_d.rearrange("(kc p) m -> p kc m", p=128)),
                waits=dict(relB))
            ldB = q.cnt["gpsimd"]
            smarks = {}
            for i in range(16):
                bank = psA if i % 2 == 0 else psB
                tw = dict(entry)
                tw["gpsimd"] = max(tw.get("gpsimd", 0), ldB)
                if i >= 2:
                    tw["scalar"] = max(tw.get("scalar", 0), smarks[i - 2])

                def mms(e, i=i, bank=bank):
                    for kc in range(8):
                        inst = e.matmul(bank[:, 0:TCH],
                                        poolB[:, kc, i * 128:(i + 1) * 128],
                                        xfb[:, kc, :], start=(kc == 0), stop=(kc == 7))
                    return inst
                q.op("tensor", mms, waits=tw)
                q.op("scalar", lambda e, i=i, bank=bank: e.activation(
                    sgrp[:, i, :], bank[:, 0:TCH], AF.Identity,
                    bias=sbup[:, i:i + 1]), waits={"tensor": q.cnt["tensor"]})
                smarks[i] = q.cnt["scalar"]
            relB = {"tensor": q.cnt["tensor"]}
            sgrp_done = q.cnt["scalar"]
            q.dma("gpsimd", lambda e: e.dma_start(
                out=poolA[:, :, 0:1024],
                in_=swg_d.rearrange("(kc p) m -> p kc m", p=128)),
                waits=dict(relA))
            ldA = q.cnt["gpsimd"]
            vmk = {}
            for i in range(16):
                g = i // 8
                mc = i % 8
                bank = psA if i % 2 == 0 else psB
                tw = dict(entry)
                tw["gpsimd"] = max(tw.get("gpsimd", 0), ldA)
                tw["scalar"] = max(tw.get("scalar", 0), sgrp_done)
                if i >= 2:
                    tw["vector"] = max(tw.get("vector", 0), vmk[i - 2])

                def mmsg(e, g=g, mc=mc, bank=bank):
                    for kc in range(8):
                        inst = e.matmul(bank[:, 0:TCH],
                                        poolA[:, kc, mc * 128:(mc + 1) * 128],
                                        sgrp[:, g * 8 + kc, :],
                                        start=(kc == 0), stop=(kc == 7))
                    return inst
                q.op("tensor", mmsg, waits=tw)
                q.op("scalar", lambda e, i=i, mc=mc, bank=bank: e.activation(
                    scr[:, i % 2, :], bank[:, 0:TCH], AF.Silu,
                    bias=sbg[:, mc:mc + 1]), waits={"tensor": q.cnt["tensor"]})
                q.op("vector", lambda e, i=i: e.tensor_tensor(
                    sg2[:, i, :], scr[:, i % 2, :], sgrp[:, i, :], op=ALU.mult),
                    waits={"scalar": q.cnt["scalar"]})
                vmk[i] = q.cnt["vector"]
            relA = {"tensor": q.cnt["tensor"]}
            sg2_done = q.cnt["vector"]
            q.dma("gpsimd", lambda e: e.dma_start(
                out=poolB[:, :, 0:1024],
                in_=swdn_d.rearrange("(kc p) m -> p kc m", p=128)),
                waits=dict(relB))
            ldB = q.cnt["gpsimd"]
            admk = {}
            for i in range(16):
                g = i // 8
                mc = i % 8
                bank = psA if i % 2 == 0 else psB
                tw = dict(entry)
                tw["gpsimd"] = max(tw.get("gpsimd", 0), ldB)
                tw["vector"] = max(tw.get("vector", 0), sg2_done)
                if i >= 2:
                    tw["scalar"] = max(tw.get("scalar", 0), admk[i - 2])

                def mmsd(e, g=g, mc=mc, bank=bank):
                    for kc in range(8):
                        inst = e.matmul(bank[:, 0:TCH],
                                        poolB[:, kc, mc * 128:(mc + 1) * 128],
                                        sg2[:, g * 8 + kc, :],
                                        start=(kc == 0), stop=(kc == 7))
                    return inst
                q.op("tensor", mmsd, waits=tw)
                q.op("scalar", lambda e, i=i, mc=mc, bank=bank: e.activation(
                    scr[:, i % 2, :], bank[:, 0:TCH], AF.Identity,
                    bias=sbdn[:, mc:mc + 1]), waits={"tensor": q.cnt["tensor"]})
                admk[i] = q.cnt["scalar"]
                q.op("vector", lambda e, i=i, mc=mc: e.tensor_add(
                    acc[:, mc, :], acc[:, mc, :], scr[:, i % 2, :]),
                    waits={"scalar": admk[i]})
            relB = {"tensor": q.cnt["tensor"]}
            for kc in range(8):
                q.op("vector", lambda e, kc=kc: e.tensor_add(
                    xfres[:, kc, :], xfres[:, kc, :], acc[:, kc, :]))
            for kc in range(8):
                q.op("scalar", lambda e, kc=kc: e.activation(
                    scr[:, 0, :], xfres[:, kc, :], AF.Square))

                def mmss(e, kc=kc):
                    return e.matmul(ps1[:, 0:TCH], onescF[:], scr[:, 0, :],
                                    start=(kc == 0), stop=(kc == 7))
                q.op("tensor", mmss)
            q.op("vector", lambda e: e.tensor_scalar_mul(
                ln.stats[:, 0:TCH], ps1[:, 0:TCH], 1.0 / 1024))
            q.op("scalar", lambda e: e.activation(
                ln.stats[:, 0:TCH], ln.stats[:, 0:TCH], AF.Ln, bias=1e-6))
            q.op("scalar", lambda e: e.activation(
                ln.stats[:, 0:TCH], ln.stats[:, 0:TCH], AF.Exp, scale=-0.5))
            q.op("tensor", lambda e: e.matmul(
                psA[:, 0:TCH], ones1f[:], ln.stats[:, 0:TCH], start=True, stop=True))
            q.op("vector", lambda e: e.tensor_copy(ln.mub[:, 0:TCH], psA[:, 0:TCH]))
            for kc in range(8):
                q.op("vector", lambda e, kc=kc: e.tensor_mul(
                    xfres[:, kc, :], xfres[:, kc, :], ln.mub[:, 0:TCH]))
                q.op("vector", lambda e, kc=kc, cyc=cyc: e.tensor_scalar_mul(
                    xfres[:, kc, :], xfres[:, kc, :], nw[:, kc, cyc:cyc + 1]))
            q.dma("sync", lambda e, tc0=tc0: e.dma_start(
                out=xf_dram[:, tc0:tc0 + TCH].rearrange("(kc p) m -> p kc m", p=128),
                in_=xfres[:]))
        p.pop()

    p.push()
    x4t = p.sb("H2_fx4", [128, 8, 512], F32)
    xft = p.sb("H2_fxf", [128, 8, 512], F32)
    for t0 in range(0, W, 512):
        q.dma("sync", lambda e, t0=t0: e.dma_start(
            out=x4t[:], in_=x4_dram[:, t0:t0 + 512].rearrange("(kc p) m -> p kc m", p=128)))
        q.dma("sync", lambda e, t0=t0: e.dma_start(
            out=xft[:], in_=xf_dram[:, t0:t0 + 512].rearrange("(kc p) m -> p kc m", p=128)))
        for kc in range(8):
            q.op("vector", lambda e, kc=kc: e.tensor_add(
                x4t[:, kc, :], x4t[:, kc, :], xft[:, kc, :]))
        ln.emit(x4t, x4t, 8)
        q.dma("sync", lambda e, t0=t0: e.dma_start(
            out=out_d[:, t0:t0 + 512].rearrange("(kc p) m -> p kc m", p=128),
            in_=x4t[:]))
    p.pop()
    return p.finish()



def _fm(a):
    return np.ascontiguousarray(a.T)


def kernel(**inputs):
    import ml_dtypes
    inp = inputs
    x0 = np.asarray(inp["L0_attn_x"], np.float32)

    def run(nc, in_maps):
        res = run_bass_kernel_spmd(nc, in_maps, core_ids=list(range(NCORES)))
        return res.results

    def win(c):
        b, q_ = divmod(c, 4)
        return b, q_ * W

    maps = []
    for c in range(NCORES):
        b, w0 = win(c)
        maps.append({"xA": _fm(x0[b, w0:w0 + W, :]),
                     "W_uv": np.asarray(inp["L0_attn_W_uv"], np.float32)})
    rA = run(build_A(), maps)
    uv = np.zeros((2, T, 2 * D), np.float32)
    for c in range(NCORES):
        b, w0 = win(c)
        uv[b, w0:w0 + W, :] = rA[c]["uvA"].T
    u_full, v_full = uv[..., :D], uv[..., D:]

    maps = []
    for c in range(NCORES):
        sl = slice(c * 128, (c + 1) * 128)
        maps.append({
            "uB": np.ascontiguousarray(u_full[:, :, sl].transpose(0, 2, 1)),
            "vB": np.ascontiguousarray(v_full[:, :, sl].transpose(0, 2, 1)),
            "rpe_in_w": np.asarray(inp["L0_attn_rpe_in_w"], np.float32),
            "rpe_hid_w": np.asarray(inp["L0_attn_rpe_hid_w"], np.float32),
            "rpe_ln_g": np.asarray(inp["L0_attn_rpe_ln_g"], np.float32),
            "rpe_ln_b": np.asarray(inp["L0_attn_rpe_ln_b"], np.float32),
            "rpe_out_w_sl": np.ascontiguousarray(np.asarray(inp["L0_attn_rpe_out_w"], np.float32)[:, sl]),
            "decay_gamma_sl": np.ascontiguousarray(np.asarray(inp["L0_attn_decay_gamma"], np.float32)[sl, None]),
        })
    rB = run(build_B(), maps)
    uo = np.zeros((2, T, D), np.float32)
    for c in range(NCORES):
        sl = slice(c * 128, (c + 1) * 128)
        uo[:, :, sl] = rB[c]["uoB"].transpose(0, 2, 1)

    maps = []
    for c in range(NCORES):
        b, w0 = win(c)
        maps.append({
            "uoC": _fm(uo[b, w0:w0 + W, :]),
            "xC": _fm(x0[b, w0:w0 + W, :]),
            "W_o": np.asarray(inp["L0_attn_W_o"], np.float32),
            "Wg": np.asarray(inp["L0_ffn_Wg"], np.float32),
            "bg": np.asarray(inp["L0_ffn_bg"], np.float32)[:, None],
            "Wt": np.asarray(inp["L0_ffn_Wt"], np.float32),
            "bt": np.asarray(inp["L0_ffn_bt"], np.float32)[:, None],
        })
    rC1 = run(build_C1(), maps)
    x1 = np.zeros((2, T, D), np.float32)
    wgt = np.zeros((2, T, 8), np.float32)
    for c in range(NCORES):
        b, w0 = win(c)
        x1[b, w0:w0 + W, :] = rC1[c]["x1C"].T
        wgt[b, w0:w0 + W, :] = rC1[c]["wC"].T

    b1h = np.ascontiguousarray(np.asarray(inp["L0_ffn_b1"], np.float32).reshape(8, 32, 128).transpose(2, 0, 1))
    b2h = np.ascontiguousarray(np.asarray(inp["L0_ffn_b2"], np.float32).reshape(8, 8, 128).transpose(2, 0, 1))
    maps = []
    for c in range(NCORES):
        b, w0 = win(c)
        maps.append({
            "x1C2": _fm(x1[b, w0:w0 + W, :]),
            "wC2": _fm(wgt[b, w0:w0 + W, :]),
            "ffW1": np.asarray(inp["L0_ffn_W1"], np.float32),
            "ffb1": b1h,
            "ffW2": np.asarray(inp["L0_ffn_W2"], np.float32),
            "ffb2": b2h,
        })
    rC2 = run(build_C2(), maps)
    x15 = np.zeros((2, T, D), np.float32)
    for c in range(NCORES):
        b, w0 = win(c)
        x15[b, w0:w0 + W, :] = rC2[c]["x15C"].T

    W1blk = np.zeros((4, 8, 128, 128), np.float32)
    W2blk = np.zeros((4, 8, 128, 512), np.float32)
    rm32 = np.asarray(inp["L1_ffn_rail_maker"], np.float32)
    rup32 = np.asarray(inp["L1_ffn_rail_up_proj"], np.float32)
    for h in range(4):
        for g in range(8):
            for j in range(8):
                W1blk[h, g, 16 * j:16 * j + 16, 16 * j:16 * j + 16] = rm32[h, 8 * g + j]
                W2blk[h, g, 16 * j:16 * j + 16, 64 * j:64 * j + 64] = rup32[h, 8 * g + j].T
    maps = []
    for c in range(NCORES):
        b, w0 = win(c)
        lo = w0 - (EXT - W)
        xe = np.zeros((D, EXT), np.float32)
        ve = np.zeros((1, EXT), np.float32)
        s0 = max(lo, 0)
        xe[:, s0 - lo:EXT] = x15[b, s0:w0 + W, :].T
        ve[0, s0 - lo:EXT] = 1.0
        maps.append({
            "x15D": xe, "validD": ve,
            "WqT": np.ascontiguousarray(np.asarray(inp["L1_attn_Wq"], np.float32).T),
            "WkT": np.ascontiguousarray(np.asarray(inp["L1_attn_Wk"], np.float32).T),
            "WvT": np.ascontiguousarray(np.asarray(inp["L1_attn_Wv"], np.float32).T),
            "WoT": np.ascontiguousarray(np.asarray(inp["L1_attn_Wo"], np.float32).T),
            "W1blk": W1blk, "W2blk": W2blk,
            "WcpT": np.ascontiguousarray(np.asarray(inp["L1_ffn_w_cproj"], np.float32).T),
        })
    rD = run(build_D(), maps)
    x25 = np.zeros((2, T, D), np.float32)
    for c in range(NCORES):
        b, w0 = win(c)
        x25[b, w0:w0 + W, :] = rD[c]["x25D"][:, 3:].T

    cwh = np.ascontiguousarray(np.asarray(inp["L2_attn_conv_w"], np.float32)[:, 0, :].reshape(16, 128, 4).transpose(1, 0, 2))
    cbh = np.ascontiguousarray(np.asarray(inp["L2_attn_conv_b"], np.float32).reshape(16, 128).T)
    alogh = np.ascontiguousarray(np.asarray(inp["L2_attn_A_log"], np.float32).reshape(16, 16, 128).transpose(2, 1, 0))
    dprmh = np.ascontiguousarray(np.asarray(inp["L2_attn_D_param"], np.float32).reshape(16, 128).T)
    maps = []
    for c in range(NCORES):
        b, w0 = win(c)
        xe = np.zeros((D, 3 + W), np.float32)
        lo = w0 - 3
        s0 = max(lo, 0)
        xe[:, s0 - lo:] = x25[b, s0:w0 + W, :].T
        maps.append({"x25E": xe, "W_in": np.asarray(inp["L2_attn_W_in"], np.float32),
                     "conv_w_h": cwh, "conv_b_h": cbh, "A_log_h": alogh})
    rE = run(build_E(), maps)

    maps = []
    for c in range(NCORES):
        b, w0 = win(c)
        q_ = c % 4
        sprev = np.zeros((3, 128, 16, 16), np.float32)
        for k in range(1, 4):
            if q_ - k >= 0:
                sprev[k - 1] = rE[c - k]["SE"]
        maps.append({
            "ucF": rE[c]["ucE"], "zsF": rE[c]["zsE"],
            "x25F": _fm(x25[b, w0:w0 + W, :]),
            "A_log_h": alogh, "D_param_h": dprmh,
            "W_out": np.asarray(inp["L2_attn_W_out"], np.float32),
            "SprevF": sprev,
            "wgF": np.asarray(inp["L2_ffn_wg"], np.float32),
        })
    rF = run(build_F(), maps)
    x3 = np.zeros((2, T, D), np.float32)
    gsc = np.zeros((2, T, 8), np.float32)
    for c in range(NCORES):
        b, w0 = win(c)
        x3[b, w0:w0 + W, :] = rF[c]["x3F"].T
        gsc[b, w0:w0 + W, :] = rF[c]["gF"].T

    gall = np.ascontiguousarray(gsc.reshape(-1, 8).T)
    maps = []
    for c in range(NCORES):
        b, w0 = win(c)
        maps.append({
            "x3G": _fm(x3[b, w0:w0 + W, :]),
            "gownG": _fm(gsc[b, w0:w0 + W, :]),
            "gallG": gall,
            "moW1": np.asarray(inp["L2_ffn_w1"], np.float32),
            "moW3": np.asarray(inp["L2_ffn_w3"], np.float32),
            "moW2": np.asarray(inp["L2_ffn_w2"], np.float32),
            "WDQT": np.ascontiguousarray(np.asarray(inp["L3_attn_W_DQ"], np.float32).T),
            "WUQT": np.ascontiguousarray(np.asarray(inp["L3_attn_W_UQ"], np.float32).T),
            "WDKVT": np.ascontiguousarray(np.asarray(inp["L3_attn_W_DKV"], np.float32).T),
            "WUKVT": np.ascontiguousarray(np.asarray(inp["L3_attn_W_UKV"], np.float32).T),
            "WQRT": np.ascontiguousarray(np.asarray(inp["L3_attn_W_QR"], np.float32).T),
            "WKRT": np.ascontiguousarray(np.asarray(inp["L3_attn_W_KR"], np.float32).T),
            "qnw": np.asarray(inp["L3_attn_q_norm_w"], np.float32)[:, None],
            "knw": np.asarray(inp["L3_attn_k_norm_w"], np.float32)[:, None],
        })
    rG = run(build_G(), maps)
    x35 = np.zeros((2, T, D), np.float32)
    for c in range(NCORES):
        b, w0 = win(c)
        x35[b, w0:w0 + W, :] = rG[c]["x35G"].T

    bf16 = ml_dtypes.bfloat16
    maps = []
    for c in range(NCORES):
        qs, ks, vs = [], [], []
        for j in range(4):
            pr = 4 * c + j
            b, h = pr // 16, pr % 16
            qs.append(np.concatenate([np.asarray(rG[4 * b + qq]["qHG"][h]) for qq in range(4)], axis=1))
            ks.append(np.concatenate([np.asarray(rG[4 * b + qq]["kHG"][h]) for qq in range(4)], axis=1))
            vs.append(np.concatenate([np.asarray(rG[4 * b + qq]["vHG"][h]) for qq in range(4)], axis=1))
        maps.append({"qH1": np.stack(qs), "kH1": np.stack(ks), "vH1": np.stack(vs)})
    rH1 = run(build_H1(), maps)
    ctx = np.zeros((2, 16, 64, T), bf16)
    for c in range(NCORES):
        for j in range(4):
            pr = 4 * c + j
            b, h = pr // 16, pr % 16
            ctx[b, h] = np.asarray(rH1[c]["ctxH1"][j])

    f32 = np.float32
    bup_h = np.ascontiguousarray(np.asarray(inp["L3_ffn_exp_bup"], f32).reshape(6, 8, 128).transpose(2, 0, 1))
    bgg_h = np.ascontiguousarray(np.asarray(inp["L3_ffn_exp_bg"], f32).reshape(6, 8, 128).transpose(2, 0, 1))
    bdn_h = np.ascontiguousarray(np.asarray(inp["L3_ffn_exp_bdown"], f32).reshape(6, 8, 128).transpose(2, 0, 1))
    sbup_h = np.ascontiguousarray(np.asarray(inp["L3_ffn_share_bup"], f32).reshape(16, 128).T)
    sbg_h = np.ascontiguousarray(np.asarray(inp["L3_ffn_share_bg"], f32).reshape(8, 128).T)
    sbdn_h = np.ascontiguousarray(np.asarray(inp["L3_ffn_share_bdown"], f32).reshape(8, 128).T)
    nw_h = np.ascontiguousarray(np.asarray(inp["L3_ffn_norm_w"], f32).reshape(2, 8, 128).transpose(2, 1, 0))
    maps = []
    for c in range(NCORES):
        b, w0 = win(c)
        ctxfm = np.ascontiguousarray(ctx[b, :, :, w0:w0 + W]).reshape(D, W)
        maps.append({
            "x35H": _fm(x35[b, w0:w0 + W, :]),
            "ctxH2": ctxfm,
            "WOT": np.ascontiguousarray(np.asarray(inp["L3_attn_W_O"], f32).T),
            "routerW": np.asarray(inp["L3_ffn_router_W"], f32),
            "routerB": np.asarray(inp["L3_ffn_router_b"], f32),
            "gateB": np.asarray(inp["L3_ffn_gate_bias"], f32),
            "expWup": np.asarray(inp["L3_ffn_exp_Wup"], f32),
            "expBup": bup_h,
            "expWg": np.asarray(inp["L3_ffn_exp_Wg"], f32),
            "expBg": bgg_h,
            "expWdown": np.asarray(inp["L3_ffn_exp_Wdown"], f32),
            "expBdown": bdn_h,
            "shWup": np.asarray(inp["L3_ffn_share_Wup"], f32),
            "shBup": sbup_h,
            "shWg": np.asarray(inp["L3_ffn_share_Wg"], f32),
            "shBg": sbg_h,
            "shWdown": np.asarray(inp["L3_ffn_share_Wdown"], f32),
            "shBdown": sbdn_h,
            "normW": nw_h,
        })
    rH2 = run(build_H2(), maps)
    out = np.zeros((2, T, D), np.float32)
    for c in range(NCORES):
        b, w0 = win(c)
        out[b, w0:w0 + W, :] = rH2[c]["outH2"].T
    return out


# version 08 (final)
# speedup vs baseline: 1.1891x; 1.1891x over previous
import math
import numpy as np
import concourse.bass as bass
import concourse.mybir as mybir
from concourse.bass_utils import run_bass_kernel_spmd

F32 = mybir.dt.float32
BF16 = mybir.dt.bfloat16
AF = mybir.ActivationFunctionType
ALU = mybir.AluOpType
AX = mybir.AxisListType
I32 = mybir.dt.int32

NCORES = 8
B = 2
T = 8192
W = 2048
D = 1024
PI = math.pi


class Seq:
    ENGS = ("sync", "gpsimd", "vector", "scalar", "tensor")

    def __init__(self):
        self.steps = []
        self.cnt = {e: 0 for e in self.ENGS}
        self.floor = {e: 0 for e in self.ENGS}

    def set_floor(self):
        self.floor = dict(self.cnt)

    def snap(self):
        return dict(self.cnt)

    FORCE_SERIAL = False

    def _rec(self, eng, fn, waits, inc):
        if Seq.FORCE_SERIAL:
            waits = None
        if waits is None:
            waits = {e: c for e, c in self.cnt.items() if c > 0}
        else:
            waits = dict(waits)
            for e2, c2 in self.floor.items():
                if e2 != eng and c2 > waits.get(e2, 0):
                    waits[e2] = c2
            if eng != "tensor" and self.cnt[eng] > waits.get(eng, 0):
                waits[eng] = self.cnt[eng]
            waits = {e: c for e, c in waits.items() if c > 0}
        self.steps.append((eng, fn, waits, inc))
        self.cnt[eng] += inc
        return dict(self.cnt)

    def op(self, eng, fn, waits=None):
        return self._rec(eng, fn, waits, 1)

    def dma(self, eng, fn, waits=None):
        return self._rec(eng, fn, waits, 16)

    def emit(self, blk, sems):
        final = dict(self.cnt)
        by_eng = {}
        for st in self.steps:
            by_eng.setdefault(st[0], []).append(st)

        def make_body(eng_name, eng_steps):
            def body(eng):
                watermark = {e: 0 for e in self.ENGS}
                for (_, fn, waits, inc) in eng_steps:
                    for tgt, val in waits.items():
                        if val > watermark[tgt]:
                            eng.wait_ge(sems[tgt], val)
                            watermark[tgt] = val
                    inst = fn(eng)
                    if inst is None:
                        raise RuntimeError("step fn returned None")
                    inst.then_inc(sems[eng_name], inc)
                for tgt, val in final.items():
                    if tgt != eng_name and val > watermark.get(tgt, 0):
                        eng.wait_ge(sems[tgt], val)
            return body

        for name, eng_steps in by_eng.items():
            getattr(blk, name)(make_body(name, eng_steps))


class Prog:

    def __init__(self, name):
        self.nc = bass.Bass("TRN2", target_bir_lowering=False, debug=False)
        self.q = Seq()
        self._ctx = []
        self._scopes = [[]]
        self._uid = 0

    def sb(self, name, shape, dtype):
        cm = self.nc.sbuf_tensor(name, shape, dtype)
        h = cm.__enter__()
        self._scopes[-1].append(cm)
        return h

    def const(self, val):
        if not hasattr(self, "_consts"):
            self._consts = {}
        if val in self._consts:
            return
        t = self.sb(f"constap_{len(self._consts)}", [128, 1], F32)
        self.q.op("gpsimd", lambda e: e.memset(t[:], float(val)))
        self.nc.const_aps.register(F32, float(val), t[:, 0:1])
        self._consts[val] = t

    def push(self):
        self._scopes.append([])

    def pop(self):
        self.q.set_floor()
        for cm in reversed(self._scopes.pop()):
            cm.__exit__(None, None, None)

    def ps(self, name, shape, dtype=F32):
        cm = self.nc.psum_tensor(name, shape, dtype)
        t = cm.__enter__()
        self._ctx.append(cm)
        return t

    def din(self, name, shape, dtype=F32):
        return self.nc.dram_tensor(name, shape, dtype, kind="ExternalInput")

    def dout(self, name, shape, dtype=F32):
        return self.nc.dram_tensor(name, shape, dtype, kind="ExternalOutput")

    def dtmp(self, name, shape, dtype=F32):
        return self.nc.dram_tensor(name, shape, dtype, kind="Internal")

    def finish(self):
        nc = self.nc
        with (
            nc.semaphore("s_sync") as s0,
            nc.semaphore("s_gpsimd") as s1,
            nc.semaphore("s_vector") as s2,
            nc.semaphore("s_scalar") as s3,
            nc.semaphore("s_tensor") as s4,
            nc.Block() as blk,
        ):
            sems = {"sync": s0, "gpsimd": s1, "vector": s2, "scalar": s3,
                    "tensor": s4}
            self.q.emit(blk, sems)
        while self._scopes:
            for cm in reversed(self._scopes.pop()):
                cm.__exit__(None, None, None)
        for cm in reversed(self._ctx):
            cm.__exit__(None, None, None)
        return nc



def load_w_bf16(p, name, dram, kdim, mdim):
    assert kdim % 128 == 0
    sb = p.sb(name, [128, kdim // 128, mdim], BF16)
    slow = (kdim * mdim * 4) < 65536
    p.q.dma("gpsimd", lambda e: e.dma_start(
        out=sb[:], in_=dram.rearrange("(kc p) m -> p kc m", p=128),
        allow_slow_non_contiguous=slow))
    return sb


def load_fm_f32(p, name, dram, ddim, tdim, cast=None):
    dt = cast or F32
    sb = p.sb(name, [128, ddim // 128, tdim], dt)
    eng = "gpsimd" if cast else "sync"
    p.q.dma(eng, lambda e: e.dma_start(
        out=sb[:], in_=dram.rearrange("(kc p) m -> p kc m", p=128)))
    return sb


def emit_cast(p, dst, src, nchunk, tdim, eng="vector"):
    p.q.op(eng, lambda e: e.tensor_copy(dst[:], src[:]))


def emit_matmul_fm(p, out_sb, w_sb, x_sb, KC, MC, TT, psums, tile_t=512,
                   evac=None, x_chunks=None, w_col_off=0, entry=None):
    q = p.q
    if not isinstance(psums, (list, tuple)):
        psums = [psums]
    if entry is None:
        entry = q.snap()
    ntt = (TT + tile_t - 1) // tile_t
    xch = x_chunks if x_chunks is not None else list(range(KC))
    units = [(mc, it) for mc in range(MC) for it in range(ntt)]
    marks = {}
    for i, (mc, it) in enumerate(units):
        t0, t1 = it * tile_t, min((it + 1) * tile_t, TT)
        psum = psums[i % len(psums)]
        waits = dict(entry)
        if i >= len(psums):
            eng_r, cnt_r = marks[i - len(psums)]
            waits[eng_r] = max(waits.get(eng_r, 0), cnt_r)

        def mm2(e, mc=mc, t0=t0, t1=t1, psum=psum):
            for ki, kc in enumerate(xch):
                inst = e.matmul(
                    psum[:, 0:t1 - t0],
                    w_sb[:, kc, w_col_off + mc * 128:w_col_off + (mc + 1) * 128],
                    x_sb[:, kc, t0:t1],
                    start=(ki == 0), stop=(ki == len(xch) - 1))
            return inst
        q.op("tensor", mm2, waits=waits)
        ew = {"tensor": q.cnt["tensor"]}
        if evac is not None:
            marks[i] = evac(mc, (t0, t1), psum, ew)
        else:
            q.op("vector", lambda e, mc=mc, t0=t0, t1=t1, psum=psum:
                 e.tensor_copy(out_sb[:, mc, t0:t1], psum[:, 0:t1 - t0]),
                 waits=ew)
            marks[i] = ("vector", q.cnt["vector"])


def emit_ones_row(p, name, n=128, dtype=BF16):
    sb = p.sb(name, [1, n], dtype)
    p.q.op("gpsimd", lambda e: e.memset(sb[:], 1.0))
    return sb


def emit_ones_col(p, name, np_=128, dtype=BF16):
    sb = p.sb(name, [np_, 1], dtype)
    p.q.op("gpsimd", lambda e: e.memset(sb[:], 1.0))
    return sb


class LNHelper:

    def __init__(self, p, tdim, tag, psA, ps1):
        self.p = p
        self.tdim = tdim
        self.psA = psA
        self.ps1 = ps1
        self.onesc = p.sb(f"ln_onesc_{tag}", [128, 1], F32)
        p.q.op("gpsimd", lambda e: e.memset(self.onesc[:], 1.0))
        self.ones1 = p.sb(f"ln_ones1_{tag}", [1, 128], F32)
        p.q.op("gpsimd", lambda e: e.memset(self.ones1[:], 1.0))
        self.stats = p.sb(f"ln_stats_{tag}", [1, 3 * tdim], F32)
        self.sq = p.sb(f"ln_sq_{tag}", [128, 512], F32)
        self.mub = p.sb(f"ln_mub_{tag}", [128, tdim], F32)
        self.rstdb = p.sb(f"ln_rstdb_{tag}", [128, tdim], F32)

    def emit_w(self, x_sb, out_sb, DC, w, eps=1e-5):
        saved = self.tdim
        self.emit(x_sb, out_sb, DC, eps=eps, w=w)
        self.tdim = saved

    def emit(self, x_sb, out_sb, DC, eps=1e-5, w=None):
        p, q = self.p, self.p.q
        tdim = w if w is not None else self.tdim
        nd = DC * 128
        p.const(float(eps))
        for it in range(0, tdim, 512):
            t1 = min(it + 512, tdim)

            def mm_mu(e, it=it, t1=t1):
                for kc in range(DC):
                    inst = e.matmul(self.ps1[:, 0:t1 - it], self.onesc[:],
                                    x_sb[:, kc, it:t1],
                                    start=(kc == 0), stop=(kc == DC - 1))
                return inst
            q.op("tensor", mm_mu)
            q.op("vector", lambda e, it=it, t1=t1: e.tensor_scalar_mul(
                self.stats[:, it:t1], self.ps1[:, 0:t1 - it], 1.0 / nd))
        for it in range(0, tdim, 512):
            t1 = min(it + 512, tdim)
            for kc in range(DC):
                q.op("scalar", lambda e, kc=kc, it=it, t1=t1: e.activation(
                    self.sq[:, 0:t1 - it], x_sb[:, kc, it:t1], AF.Square))

                def mm_sq2(e, kc=kc, it=it, t1=t1):
                    return e.matmul(self.ps1[:, 0:t1 - it], self.onesc[:],
                                    self.sq[:, 0:t1 - it],
                                    start=(kc == 0), stop=(kc == DC - 1))
                q.op("tensor", mm_sq2)
            q.op("vector", lambda e, it=it, t1=t1: e.tensor_scalar_mul(
                self.stats[:, tdim + it:tdim + t1], self.ps1[:, 0:t1 - it],
                1.0 / nd))
        q.op("vector", lambda e: e.tensor_tensor(
            self.stats[:, 2 * tdim:3 * tdim], self.stats[:, 0:tdim],
            self.stats[:, 0:tdim], op=ALU.mult))
        q.op("vector", lambda e: e.tensor_sub(
            self.stats[:, tdim:2 * tdim], self.stats[:, tdim:2 * tdim],
            self.stats[:, 2 * tdim:3 * tdim]))
        q.op("scalar", lambda e: e.activation(
            self.stats[:, tdim:2 * tdim], self.stats[:, tdim:2 * tdim],
            AF.Ln, bias=float(eps)))
        q.op("scalar", lambda e: e.activation(
            self.stats[:, tdim:2 * tdim], self.stats[:, tdim:2 * tdim],
            AF.Exp, scale=-0.5))
        for (src_off, dst) in ((0, self.mub), (tdim, self.rstdb)):
            for it in range(0, tdim, 512):
                t1 = min(it + 512, tdim)
                q.op("tensor", lambda e, src_off=src_off, it=it, t1=t1:
                     e.matmul(self.psA[:, 0:t1 - it], self.ones1[:],
                              self.stats[:, src_off + it:src_off + t1],
                              start=True, stop=True))
                q.op("vector", lambda e, dst=dst, it=it, t1=t1:
                     e.tensor_copy(dst[:, it:t1], self.psA[:, 0:t1 - it]))
        for kc in range(DC):
            q.op("vector", lambda e, kc=kc: e.tensor_sub(
                out_sb[:, kc, 0:tdim], x_sb[:, kc, 0:tdim], self.mub[:, 0:tdim]))
            q.op("vector", lambda e, kc=kc: e.tensor_mul(
                out_sb[:, kc, 0:tdim], out_sb[:, kc, 0:tdim], self.rstdb[:, 0:tdim]))


def store_fm(p, dram, sb, DC, tdim, eng="sync"):
    p.q.dma(eng, lambda e: e.dma_start(
        out=dram.rearrange("(kc p) m -> p kc m", p=128), in_=sb[:]))



def build_A():
    p = Prog("A")
    x_d = p.din("xA", [D, W])
    wuv_d = p.din("W_uv", [D, 2 * D])
    uv_d = p.dout("uvA", [2 * D, W])

    xb = load_fm_f32(p, "xb", x_d, D, W, cast=BF16)
    wsb = load_w_bf16(p, "wuv", wuv_d, D, 2 * D)
    stage = p.sb("uv_stage", [128, 2, W], F32)
    psA = p.ps("psA", [128, 512], F32)
    psB = p.ps("psB", [128, 512], F32)

    def evac(mc, ts, psum, waits):
        t0, t1 = ts
        sg = mc % 2
        if t0 == 0 and mc >= 2:
            waits = dict(waits)
            waits["sync"] = p.q.cnt["sync"]
        p.q.op("scalar", lambda e: e.activation(
            stage[:, sg, t0:t1], psum[:, 0:t1 - t0], AF.Silu), waits=waits)
        mark = ("scalar", p.q.cnt["scalar"])
        if t1 == W:
            p.q.dma("sync", lambda e, mc=mc, sg=sg: e.dma_start(
                out=uv_d[mc * 128:(mc + 1) * 128, :], in_=stage[:, sg, :]),
                waits={"scalar": p.q.cnt["scalar"]})
        return mark
    emit_matmul_fm(p, None, wsb, xb, 8, 16, W, [psA, psB], evac=evac)
    return p.finish()


N1 = 128
N2 = 128
NFFT = N1 * N2


def build_fft_consts(p):
    q = p.q
    p.const(-PI)
    iota_row = p.sb("fc_iota_row", [1, 128], F32)
    q.op("gpsimd", lambda e: e.iota(iota_row[:], pattern=[[1, 128]], base=0,
                                    channel_multiplier=0,
                                    allow_small_or_imprecise_dtypes=True))
    iota_rowb = p.sb("fc_iota_rowb", [1, 128], BF16)
    q.op("vector", lambda e: e.tensor_copy(iota_rowb[:], iota_row[:]))
    prod = p.fb["ps1r"]
    grid = p.sb("fc_grid", [128, 128], F32)
    tmp = p.sb("fc_tmp", [128, 128], F32)
    tmpi = p.sb("fc_tmpi", [128, 128], I32)
    out = {}
    q.op("tensor", lambda e: e.matmul(prod[:, 0:128], iota_rowb[:], iota_rowb[:],
                                      start=True, stop=True))
    q.op("vector", lambda e: e.tensor_copy(grid[:], prod[:, 0:128]))

    def make_sin(dst_name, shift, modn, scale_sign, dtype, scale_extra=1.0):
        dst = p.sb(dst_name, [128, 128], dtype)
        q.op("vector", lambda e: e.tensor_scalar(
            tmp[:], grid[:], float(scale_sign), float(shift),
            op0=ALU.mult, op1=ALU.add))
        q.op("vector", lambda e: e.tensor_copy(tmpi[:], tmp[:]))
        q.op("vector", lambda e: e.tensor_single_scalar(
            tmpi[:], tmpi[:], int(modn - 1), op=ALU.bitwise_and))
        q.op("vector", lambda e: e.tensor_copy(tmp[:], tmpi[:]))
        q.op("scalar", lambda e: e.activation(
            tmp[:], tmp[:], AF.Sin, scale=2 * PI / modn, bias=-PI))
        if scale_extra != 1.0:
            q.op("vector", lambda e: e.tensor_scalar_mul(dst[:], tmp[:],
                                                         float(scale_extra)))
        else:
            q.op("vector", lambda e: e.tensor_copy(dst[:], tmp[:]))
        return dst

    out["F_r"] = make_sin("fc_F_r", 32, 128, 1, BF16, scale_extra=-1.0)
    out["F_i"] = make_sin("fc_F_i", 0, 128, 1, BF16, scale_extra=1.0)
    out["G_r"] = make_sin("fc_G_r", 32, 128, 1, BF16, scale_extra=-1.0)
    out["G_i"] = make_sin("fc_G_i", 0, 128, 1, BF16, scale_extra=-1.0)
    out["Gn_r"] = make_sin("fc_Gn_r", 32, 128, 1, BF16, scale_extra=-1.0 / NFFT)
    out["Gn_i"] = make_sin("fc_Gn_i", 0, 128, 1, BF16, scale_extra=-1.0 / NFFT)
    out["TWf_r"] = make_sin("fc_TWf_r", NFFT // 4, NFFT, 1, F32, scale_extra=-1.0)
    out["TWf_i"] = make_sin("fc_TWf_i", 0, NFFT, 1, F32, scale_extra=1.0)
    out["TWi_r"] = make_sin("fc_TWi_r", NFFT // 4, NFFT, 1, F32, scale_extra=-1.0)
    out["TWi_i"] = make_sin("fc_TWi_i", 0, NFFT, 1, F32, scale_extra=-1.0)
    out["NF_i"] = make_sin("fc_NF_i", 0, 128, 1, BF16, scale_extra=-1.0)
    out["NG_i"] = make_sin("fc_NG_i", 0, 128, 1, BF16, scale_extra=1.0)
    out["NGn_i"] = make_sin("fc_NGn_i", 0, 128, 1, BF16, scale_extra=1.0 / NFFT)
    return out


CG = 4


def emit_fft_fwd(p, C, src_sb, spec_r, spec_i, ident, tag):
    q = p.q
    fb = p.fb
    ps1r, ps1i, pst, ps2 = fb["ps1r"], fb["ps1i"], fb["pst"], fb["ps2"]
    y_r, y_i, yt_r, yt_i, scr = fb["y_r"], fb["y_i"], fb["yt_r"], fb["yt_i"], fb["scr"]
    for g0 in range(0, C, CG):
        g1 = min(g0 + CG, C)
        ng = g1 - g0
        q.op("tensor", lambda e, g0=g0, ng=ng: e.matmul(
            ps1r[:, 0:ng * 128],
            p.fc["F_r"][:], src_sb[:, g0:g0 + ng, :].rearrange("p c n -> p (c n)"),
            start=True, stop=True))
        q.op("tensor", lambda e, g0=g0, ng=ng: e.matmul(
            ps1i[:, 0:ng * 128],
            p.fc["F_i"][:], src_sb[:, g0:g0 + ng, :].rearrange("p c n -> p (c n)"),
            start=True, stop=True))
        twr = p.fc["TWf_r"][:].unsqueeze(1).broadcast_to([128, ng, 128])
        twi = p.fc["TWf_i"][:].unsqueeze(1).broadcast_to([128, ng, 128])
        yrv = y_r[:, 0:ng, :]
        yiv = y_i[:, 0:ng, :]
        psr3 = ps1r[:, 0:ng * 128].rearrange("p (c n) -> p c n", n=128)
        psi3 = ps1i[:, 0:ng * 128].rearrange("p (c n) -> p c n", n=128)
        scr3 = scr[:, 0:ng * 128].rearrange("p (c n) -> p c n", n=128)
        q.op("vector", lambda e: e.tensor_tensor(scr3, psr3, twr, op=ALU.mult))
        q.op("vector", lambda e: e.tensor_tensor(yiv, psr3, twi, op=ALU.mult))
        q.op("vector", lambda e: e.tensor_tensor(yrv, psi3, twi, op=ALU.mult))
        q.op("vector", lambda e: e.tensor_sub(yrv, scr3, yrv))
        q.op("vector", lambda e: e.tensor_tensor(scr3, psi3, twr, op=ALU.mult))
        q.op("vector", lambda e: e.tensor_add(yiv, yiv, scr3))
        for c in range(ng):
            for (ysrc, ydst) in ((y_r, yt_r), (y_i, yt_i)):
                q.op("tensor", lambda e, c=c, ysrc=ysrc: e.transpose(
                    pst[:], ysrc[:, c, :], ident[:]))
                q.op("vector", lambda e, c=c, ydst=ydst: e.tensor_copy(
                    ydst[:, c, :], pst[:]))
        ytrv = yt_r[:, 0:ng, :].rearrange("p c n -> p (c n)")
        ytiv = yt_i[:, 0:ng, :].rearrange("p c n -> p (c n)")

        def mmXr(e, ng=ng):
            e.matmul(ps2[:, 0:ng * 128], p.fc["F_r"][:], ytrv, start=True, stop=False)
            return e.matmul(ps2[:, 0:ng * 128], p.fc["NF_i"][:], ytiv, start=False, stop=True)
        q.op("tensor", mmXr)
        q.op("vector", lambda e, g0=g0, ng=ng: e.tensor_copy(
            spec_r[:, g0:g0 + ng, :].rearrange("p c n -> p (c n)"), ps2[:, 0:ng * 128]))

        def mmXi(e, ng=ng):
            e.matmul(ps2[:, 0:ng * 128], p.fc["F_i"][:], ytrv, start=True, stop=False)
            return e.matmul(ps2[:, 0:ng * 128], p.fc["F_r"][:], ytiv, start=False, stop=True)
        q.op("tensor", mmXi)
        q.op("vector", lambda e, g0=g0, ng=ng: e.tensor_copy(
            spec_i[:, g0:g0 + ng, :].rearrange("p c n -> p (c n)"), ps2[:, 0:ng * 128]))


def emit_fft_inv_real(p, C, spec_r, spec_i, out_cb, ident, tag):
    q = p.q
    fb = p.fb
    psZr, psZi, pst = fb["ps1r"], fb["ps1i"], fb["pst"]
    z_r, z_i, zt_r, zt_i, scr = fb["y_r"], fb["y_i"], fb["yt_r"], fb["yt_i"], fb["scr"]
    psX = fb["ps2"][0:64, :]
    for g0 in range(0, C, CG):
        g1 = min(g0 + CG, C)
        ng = g1 - g0
        xrv = spec_r[:, g0:g1, :].rearrange("p c n -> p (c n)")
        xiv = spec_i[:, g0:g1, :].rearrange("p c n -> p (c n)")
        def mmZr(e, ng=ng, xrv=xrv, xiv=xiv):
            e.matmul(psZr[:, 0:ng * 128], p.fc["G_r"][:], xrv, start=True, stop=False)
            return e.matmul(psZr[:, 0:ng * 128], p.fc["NG_i"][:], xiv, start=False, stop=True)
        q.op("tensor", mmZr)

        def mmZi(e, ng=ng, xrv=xrv, xiv=xiv):
            e.matmul(psZi[:, 0:ng * 128], p.fc["G_i"][:], xrv, start=True, stop=False)
            return e.matmul(psZi[:, 0:ng * 128], p.fc["G_r"][:], xiv, start=False, stop=True)
        q.op("tensor", mmZi)
        twr = p.fc["TWi_r"][:].unsqueeze(1).broadcast_to([128, ng, 128])
        twi = p.fc["TWi_i"][:].unsqueeze(1).broadcast_to([128, ng, 128])
        zrv = z_r[:, 0:ng, :]
        ziv = z_i[:, 0:ng, :]
        psr3 = psZr[:, 0:ng * 128].rearrange("p (c n) -> p c n", n=128)
        psi3 = psZi[:, 0:ng * 128].rearrange("p (c n) -> p c n", n=128)
        scr3 = scr[:, 0:ng * 128].rearrange("p (c n) -> p c n", n=128)
        q.op("vector", lambda e: e.tensor_tensor(scr3, psr3, twr, op=ALU.mult))
        q.op("vector", lambda e: e.tensor_tensor(zrv, psi3, twi, op=ALU.mult))
        q.op("vector", lambda e: e.tensor_sub(zrv, scr3, zrv))
        q.op("vector", lambda e: e.tensor_tensor(scr3, psr3, twi, op=ALU.mult))
        q.op("vector", lambda e: e.tensor_tensor(ziv, psi3, twr, op=ALU.mult))
        q.op("vector", lambda e: e.tensor_add(ziv, ziv, scr3))
        for c in range(ng):
            for (zsrc, zdst) in ((z_r, zt_r), (z_i, zt_i)):
                q.op("tensor", lambda e, c=c, zsrc=zsrc: e.transpose(
                    pst[:], zsrc[:, c, :], ident[:]))
                q.op("vector", lambda e, c=c, zdst=zdst: e.tensor_copy(
                    zdst[:, c, :], pst[:]))
        ztrv = zt_r[:, 0:ng, :].rearrange("p c n -> p (c n)")
        ztiv = zt_i[:, 0:ng, :].rearrange("p c n -> p (c n)")

        def mmX(e, ng=ng, ztrv=ztrv, ztiv=ztiv):
            e.matmul(psX[:, 0:ng * 128], p.fc["Gn_r"][:, 0:64], ztrv, start=True, stop=False)
            return e.matmul(psX[:, 0:ng * 128], p.fc["NGn_i"][:, 0:64], ztiv, start=False, stop=True)
        q.op("tensor", mmX)
        out_cb(psX, g0, ng)


def build_B():
    p = Prog("B")
    q = p.q
    u_d = p.din("uB", [B, 128, T])
    v_d = p.din("vB", [B, 128, T])
    rpe_in_w = p.din("rpe_in_w", [1, 32])
    rpe_hid_w = p.din("rpe_hid_w", [3, 32, 32])
    rpe_ln_g = p.din("rpe_ln_g", [3, 32])
    rpe_ln_b = p.din("rpe_ln_b", [3, 32])
    rpe_out_w_sl = p.din("rpe_out_w_sl", [32, 128])
    gamma_sl = p.din("decay_gamma_sl", [128, 1])
    uo_d = p.dout("uoB", [B, 128, T])
    t_dram = p.dtmp("t_dramB", [128, T], F32)

    p.fb = {
        "ps1r": p.ps("B_ps1r", [128, CG * 128], F32),
        "ps1i": p.ps("B_ps1i", [128, CG * 128], F32),
        "pst": p.ps("B_pst", [128, 128], BF16),
        "ps2": p.ps("B_ps2", [128, CG * 128], F32),
    }
    p.fb["y_r"] = p.sb("B_yr", [128, CG, 128], BF16)
    p.fb["y_i"] = p.sb("B_yi", [128, CG, 128], BF16)
    p.fb["yt_r"] = p.sb("B_ytr", [128, CG, 128], BF16)
    p.fb["yt_i"] = p.sb("B_yti", [128, CG, 128], BF16)
    p.fb["scr"] = p.sb("B_scr", [128, CG * 128], F32)
    p.fb["scr2"] = p.sb("B_scr2", [128, CG * 128], F32)
    p.const(1e-5)
    p.fc = build_fft_consts(p)
    ident = p.sb("identB", [128, 128], BF16)
    ones_t = p.sb("onesB", [128, 128], BF16)
    q.op("gpsimd", lambda e: e.memset(ones_t[:], 1.0))
    q.op("gpsimd", lambda e: e.affine_select(
        ident[:], ones_t[:], pattern=[[1, 128]], compare_op=ALU.is_equal,
        fill=0.0, base=0, channel_multiplier=-1))

    p.push()
    riw = p.sb("riw", [1, 32], F32)
    q.dma("sync", lambda e: e.dma_start(out=riw[:], in_=rpe_in_w[:]))
    lng = p.sb("lngB", [32, 3], F32)
    lnb = p.sb("lnbB", [32, 3], F32)
    q.dma("sync", lambda e: e.dma_start(out=lng[:], in_=rpe_ln_g.rearrange("l f -> f l"), allow_slow_non_contiguous=True))
    q.dma("sync", lambda e: e.dma_start(out=lnb[:], in_=rpe_ln_b.rearrange("l f -> f l"), allow_slow_non_contiguous=True))
    whid = p.sb("whidB", [32, 3, 32], BF16)
    q.dma("gpsimd", lambda e: e.dma_start(
        out=whid[:], in_=rpe_hid_w.rearrange("l k m -> k l m")))
    row = p.sb("rowB", [32, 128], BF16)
    q.dma("gpsimd", lambda e: e.dma_start(out=row[:], in_=rpe_out_w_sl[:]))
    gam = p.sb("gamB", [128, 1], F32)
    q.dma("sync", lambda e: e.dma_start(out=gam[:], in_=gamma_sl[:]))
    q.op("scalar", lambda e: e.activation(gam[:], gam[:], AF.Sigmoid))
    q.op("vector", lambda e: e.tensor_scalar(
        gam[:], gam[:], 0.01, 0.99, op0=ALU.mult, op1=ALU.add))
    q.op("scalar", lambda e: e.activation(gam[:], gam[:], AF.Ln))
    ones32 = p.sb("ones32B", [32, 1], F32)
    q.op("gpsimd", lambda e: e.memset(ones32[:], 1.0))
    ones1_32 = p.sb("ones1_32B", [1, 32], F32)
    q.op("gpsimd", lambda e: e.memset(ones1_32[:], 1.0))
    TJ = 2048
    idx_row = p.sb("idx_row", [1, TJ], F32)
    h = p.sb("hB", [32, TJ], F32)
    hb = p.sb("hBb", [32, TJ], BF16)
    hsq = p.sb("hsqB", [32, TJ], F32)
    hn = p.sb("hnB", [32, TJ], F32)
    mub = p.sb("mubB", [32, TJ], F32)
    rstdb = p.sb("rstdbB", [32, TJ], F32)
    stats = p.sb("statsB", [1, 3 * TJ], F32)
    t_ch = p.sb("t_chB", [128, TJ], F32)
    iota_t = p.sb("iota_tB", [128, TJ], F32)
    ps32 = p.fb["ps1r"][0:32, 0:512]
    ps1 = p.fb["ps1r"][0:1, 0:512]
    psb32 = p.fb["ps1i"][0:32, 0:512]
    ps128 = p.fb["ps1r"]
    for j0 in range(0, T, TJ):
        q.op("gpsimd", lambda e, j0=j0: e.iota(
            idx_row[:], pattern=[[1, TJ]], base=j0, channel_multiplier=0,
            allow_small_or_imprecise_dtypes=True))
        for it in range(0, TJ, 512):
            q.op("tensor", lambda e, it=it: e.matmul(
                ps32[:, 0:512], riw[:], idx_row[:, it:it + 512], start=True, stop=True))
            q.op("scalar", lambda e, it=it: e.activation(
                h[:, it:it + 512], ps32[:, 0:512], AF.Silu))
        for layer in range(3):
            for it in range(0, TJ, 512):
                q.op("tensor", lambda e, it=it: e.matmul(
                    ps1[:, 0:512], ones32[:], h[:, it:it + 512], start=True, stop=True))
                q.op("vector", lambda e, it=it: e.tensor_scalar_mul(
                    stats[:, it:it + 512], ps1[:, 0:512], 1.0 / 32))
            q.op("scalar", lambda e: e.activation(hsq[:], h[:], AF.Square))
            for it in range(0, TJ, 512):
                q.op("tensor", lambda e, it=it: e.matmul(
                    ps1[:, 0:512], ones32[:], hsq[:, it:it + 512], start=True, stop=True))
                q.op("vector", lambda e, it=it: e.tensor_scalar_mul(
                    stats[:, TJ + it:TJ + it + 512], ps1[:, 0:512], 1.0 / 32))
            q.op("vector", lambda e: e.tensor_tensor(
                stats[:, 2 * TJ:3 * TJ], stats[:, 0:TJ], stats[:, 0:TJ], op=ALU.mult))
            q.op("vector", lambda e: e.tensor_sub(
                stats[:, TJ:2 * TJ], stats[:, TJ:2 * TJ], stats[:, 2 * TJ:3 * TJ]))
            q.op("scalar", lambda e: e.activation(
                stats[:, TJ:2 * TJ], stats[:, TJ:2 * TJ], AF.Ln, bias=1e-5))
            q.op("scalar", lambda e: e.activation(
                stats[:, TJ:2 * TJ], stats[:, TJ:2 * TJ], AF.Exp, scale=-0.5))
            for it in range(0, TJ, 512):
                q.op("tensor", lambda e, it=it: e.matmul(
                    psb32[:, 0:512], ones1_32[:], stats[:, it:it + 512],
                    start=True, stop=True))
                q.op("vector", lambda e, it=it: e.tensor_copy(
                    mub[:, it:it + 512], psb32[:, 0:512]))
                q.op("tensor", lambda e, it=it: e.matmul(
                    psb32[:, 0:512], ones1_32[:], stats[:, TJ + it:TJ + it + 512],
                    start=True, stop=True))
                q.op("vector", lambda e, it=it: e.tensor_copy(
                    rstdb[:, it:it + 512], psb32[:, 0:512]))
            q.op("vector", lambda e: e.tensor_sub(hn[:], h[:], mub[:]))
            q.op("vector", lambda e: e.tensor_mul(hn[:], hn[:], rstdb[:]))
            q.op("vector", lambda e, layer=layer: e.tensor_scalar(
                hn[:], hn[:], lng[:, layer:layer + 1], lnb[:, layer:layer + 1],
                op0=ALU.mult, op1=ALU.add))
            q.op("vector", lambda e: e.tensor_copy(hb[:], hn[:]))
            for it in range(0, TJ, 512):
                q.op("tensor", lambda e, it=it, layer=layer: e.matmul(
                    ps32[:, 0:512], whid[:, layer, :], hb[:, it:it + 512],
                    start=True, stop=True))
                q.op("scalar", lambda e, it=it: e.activation(
                    h[:, it:it + 512], ps32[:, 0:512], AF.Silu))
        q.op("vector", lambda e: e.tensor_copy(hb[:], h[:]))
        for it in range(0, TJ, 512):
            q.op("tensor", lambda e, it=it: e.matmul(
                ps128[:, 0:512], row[:], hb[:, it:it + 512], start=True, stop=True))
            q.op("vector", lambda e, it=it: e.tensor_copy(
                t_ch[:, it:it + 512], ps128[:, 0:512]))
        q.op("gpsimd", lambda e, j0=j0: e.iota(
            iota_t[:], pattern=[[1, TJ]], base=j0, channel_multiplier=0,
            allow_small_or_imprecise_dtypes=True))
        q.op("vector", lambda e: e.tensor_scalar_mul(iota_t[:], iota_t[:], gam[:, 0:1]))
        q.op("scalar", lambda e: e.activation(iota_t[:], iota_t[:], AF.Exp))
        q.op("vector", lambda e: e.tensor_mul(t_ch[:], t_ch[:], iota_t[:]))
        q.dma("sync", lambda e, j0=j0: e.dma_start(
            out=t_dram[:, j0:j0 + TJ], in_=t_ch[:]))
    p.pop()

    fftin = p.sb("fftinB", [128, 128, 128], BF16)
    tspec_r = p.sb("tspec_r", [128, 128, 128], BF16)
    tspec_i = p.sb("tspec_i", [128, 128, 128], BF16)
    q.op("vector", lambda e: e.memset(fftin[:], 0.0))
    q.dma("gpsimd", lambda e: e.dma_start(
        out=fftin[0:64, :, :], in_=t_dram.rearrange("c (a b) -> a c b", b=128)))
    emit_fft_fwd(p, 128, fftin, tspec_r, tspec_i, ident, "t")

    vspec_r = p.sb("vspec_r", [128, 128, 128], BF16)
    vspec_i = p.sb("vspec_i", [128, 128, 128], BF16)
    u_g = p.sb("u_gB", [64, CG, 128], F32)
    uo_g = p.sb("uo_gB", [64, CG, 128], F32)
    scrp = p.fb["scr"]
    for b in range(B):
        q.op("vector", lambda e: e.memset(fftin[:], 0.0))
        q.dma("gpsimd", lambda e, b=b: e.dma_start(
            out=fftin[0:64, :, :], in_=v_d[b].rearrange("c (a b2) -> a c b2", b2=128)))
        emit_fft_fwd(p, 128, fftin, vspec_r, vspec_i, ident, f"v{b}")
        for g0 in range(0, 128, CG):
            ng = CG
            trv = tspec_r[:, g0:g0 + ng, :].rearrange("p c n -> p (c n)")
            tiv = tspec_i[:, g0:g0 + ng, :].rearrange("p c n -> p (c n)")
            vrv = vspec_r[:, g0:g0 + ng, :].rearrange("p c n -> p (c n)")
            viv = vspec_i[:, g0:g0 + ng, :].rearrange("p c n -> p (c n)")
            q.op("vector", lambda e, trv=trv, vrv=vrv: e.tensor_tensor(
                scrp[:], trv, vrv, op=ALU.mult))
            q.op("vector", lambda e, tiv=tiv, viv=viv: e.tensor_tensor(
                p.fb["scr2"][:], tiv, viv, op=ALU.mult))
            q.op("vector", lambda e, trv=trv, viv=viv: e.tensor_tensor(
                viv, trv, viv, op=ALU.mult))
            q.op("vector", lambda e, tiv=tiv, vrv=vrv: e.tensor_tensor(
                vrv, tiv, vrv, op=ALU.mult))
            q.op("vector", lambda e, viv=viv, vrv=vrv: e.tensor_add(viv, viv, vrv))
            q.op("vector", lambda e, vrv=vrv: e.tensor_sub(
                vrv, scrp[:], p.fb["scr2"][:]))

        def out_cb(psX, g0, ng, b=b):
            q.dma("sync", lambda e, g0=g0, ng=ng, b=b: e.dma_start(
                out=u_g[:, 0:ng, :],
                in_=u_d[b, g0:g0 + ng, :].rearrange("c (a b2) -> a c b2", b2=128)))
            uv_ = u_g[:, 0:ng, :].rearrange("p c n -> p (c n)")
            uov = uo_g[:, 0:ng, :].rearrange("p c n -> p (c n)")
            q.op("vector", lambda e, uv_=uv_, uov=uov, ng=ng: e.tensor_tensor(
                uov, psX[:, 0:ng * 128], uv_, op=ALU.mult))
            q.dma("sync", lambda e, g0=g0, ng=ng, b=b: e.dma_start(
                out=uo_d[b, g0:g0 + ng, :].rearrange("c (a b2) -> a c b2", b2=128),
                in_=uo_g[:, 0:ng, :]))
        emit_fft_inv_real(p, 128, vspec_r, vspec_i, out_cb, ident, f"i{b}")
    return p.finish()



def build_C1():
    p = Prog("C1")
    q = p.q
    uo_d = p.din("uoC", [D, W])
    x_d = p.din("xC", [D, W])
    wo_d = p.din("W_o", [D, D])
    wg_d = p.din("Wg", [D, 8])
    bg_d = p.din("bg", [8, 1])
    wt_d = p.din("Wt", [D, 1])
    bt_d = p.din("bt", [1, 1])
    x1_d = p.dout("x1C", [D, W])
    w_out_d = p.dout("wC", [8, W])
    dbg_g = p.dout("dbgG", [8, W])
    dbg_t = p.dout("dbgT", [1, W])

    psA = p.ps("C1_psA", [128, 512], F32)
    psB = p.ps("C1_psB", [128, 512], F32)
    ps1 = psB[0:1, :]
    ps8 = psB[0:8, :]
    pst = p.ps("C1_pst", [128, 128], F32)

    uo = load_fm_f32(p, "uo_sb", uo_d, D, W, cast=BF16)
    x = load_fm_f32(p, "x_sb", x_d, D, W)
    wo = load_w_bf16(p, "wo_sb", wo_d, D, D)

    def evac_add(mc, ts, psum, waits):
        t0, t1 = ts
        q.op("vector", lambda e: e.tensor_add(
            x[:, mc, t0:t1], x[:, mc, t0:t1], psum[:, 0:t1 - t0]), waits=waits)
        return ("vector", q.cnt["vector"])
    emit_matmul_fm(p, None, wo, uo, 8, 8, W, [psA, psB[0:128, :]], evac=evac_add)
    ln = LNHelper(p, W, "c1", psA, ps1)
    ln.emit(x, x, 8)
    store_fm(p, x1_d, x, 8, W)
    xb = uo
    emit_cast(p, xb, x, 8, W)
    wg = load_w_bf16(p, "wg_sb", wg_d, D, 8)
    wt = load_w_bf16(p, "wt_sb", wt_d, D, 1)
    bg = p.sb("C1_bg", [8, 1], F32)
    q.dma("sync", lambda e: e.dma_start(out=bg[:], in_=bg_d[:]))
    bt = p.sb("C1_bt", [1, 1], F32)
    q.dma("sync", lambda e: e.dma_start(out=bt[:], in_=bt_d[:]))
    gsc = p.sb("C1_gsc", [8, W], F32)
    thr = p.sb("C1_thr", [1, W], F32)
    for it in range(0, W, 512):
        def mmg(e, it=it):
            for kc in range(8):
                inst = e.matmul(ps8[:, 0:512], wg[:, kc, :], xb[:, kc, it:it + 512],
                                start=(kc == 0), stop=(kc == 7))
            return inst
        q.op("tensor", mmg)
        q.op("scalar", lambda e, it=it: e.activation(
            gsc[:, it:it + 512], ps8[:, 0:512], AF.Identity, bias=bg[:, 0:1]))

        def mmt(e, it=it):
            for kc in range(8):
                inst = e.matmul(ps1[:, 0:512], wt[:, kc, :], xb[:, kc, it:it + 512],
                                start=(kc == 0), stop=(kc == 7))
            return inst
        q.op("tensor", mmt)
        q.op("scalar", lambda e, it=it: e.activation(
            thr[:, it:it + 512], ps1[:, 0:512], AF.Sigmoid, bias=bt[:, 0:1]))
    q.op("vector", lambda e: e.tensor_scalar_mul(thr[:], thr[:], 0.125))
    q.dma("sync", lambda e: e.dma_start(out=dbg_g[:], in_=gsc[:]))
    q.dma("sync", lambda e: e.dma_start(out=dbg_t[:], in_=thr[:]))
    gt = p.sb("C1_gt", [128, 16, 8], F32)
    tt_ = p.sb("C1_tt", [128, 16, 1], F32)
    identF = p.sb("C1_identF", [128, 128], F32)
    q.op("gpsimd", lambda e: e.memset(identF[:], 1.0))
    q.op("gpsimd", lambda e: e.affine_select(
        identF[:], identF[:], pattern=[[1, 128]], compare_op=ALU.is_equal,
        fill=0.0, base=0, channel_multiplier=-1))
    for ti in range(16):
        q.op("tensor", lambda e, ti=ti: e.transpose(
            pst[:, 0:8], gsc[:, ti * 128:(ti + 1) * 128], identF[0:8, 0:8]))
        q.op("vector", lambda e, ti=ti: e.tensor_copy(gt[:, ti, :], pst[:, 0:8]))
        q.op("tensor", lambda e, ti=ti: e.transpose(
            pst[:, 0:1], thr[:, ti * 128:(ti + 1) * 128], identF[0:1, 0:1]))
        q.op("vector", lambda e, ti=ti: e.tensor_copy(tt_[:, ti, :], pst[:, 0:1]))
    red = p.sb("C1_red", [128, 16, 1], F32)
    q.op("vector", lambda e: e.tensor_reduce(red[:], gt[:], axis=AX.X, op=ALU.max))
    q.op("vector", lambda e: e.tensor_sub(gt[:], gt[:], red[:].broadcast_to([128, 16, 8])))
    q.op("scalar", lambda e: e.activation(gt[:], gt[:], AF.Exp))
    q.op("vector", lambda e: e.tensor_reduce(red[:], gt[:], axis=AX.X, op=ALU.add))
    q.op("vector", lambda e: e.reciprocal(red[:], red[:]))
    q.op("vector", lambda e: e.tensor_mul(gt[:], gt[:], red[:].broadcast_to([128, 16, 8])))
    q.op("vector", lambda e: e.tensor_sub(gt[:], gt[:], tt_[:].broadcast_to([128, 16, 8])))
    q.op("vector", lambda e: e.tensor_relu(gt[:], gt[:]))
    q.op("vector", lambda e: e.tensor_reduce(red[:], gt[:], axis=AX.X, op=ALU.add))
    zmask = p.sb("C1_zmask", [128, 16, 1], F32)
    q.op("vector", lambda e: e.tensor_single_scalar(zmask[:], red[:], 0.0, op=ALU.is_equal))
    q.op("vector", lambda e: e.tensor_add(red[:], red[:], zmask[:]))
    q.op("vector", lambda e: e.reciprocal(red[:], red[:]))
    q.op("vector", lambda e: e.tensor_mul(gt[:], gt[:], red[:].broadcast_to([128, 16, 8])))
    wfm = p.sb("C1_wfm", [8, W], F32)
    for ti in range(16):
        q.op("tensor", lambda e, ti=ti: e.transpose(
            pst[0:8, :], gt[:, ti, :], identF[:]))
        q.op("vector", lambda e, ti=ti: e.tensor_copy(
            wfm[:, ti * 128:(ti + 1) * 128], pst[0:8, 0:128]))
    q.dma("sync", lambda e: e.dma_start(out=w_out_d[:], in_=wfm[:]))
    return p.finish()



def build_C2():
    p = Prog("C2")
    q = p.q
    x1_d = p.din("x1C2", [D, W])
    w_d = p.din("wC2", [8, W])
    W1_d = p.din("ffW1", [8, D, 4096])
    b1_d = p.din("ffb1", [128, 8, 32])
    W2_d = p.din("ffW2", [8, 4096, D])
    b2_d = p.din("ffb2", [128, 8, 8])
    x15_d = p.dout("x15C", [D, W])

    psA = p.ps("C2_psA", [128, 512], F32)
    psB = p.ps("C2_psB", [128, 512], F32)
    psC = p.ps("C2_psC", [128, 512], F32)
    psD = p.ps("C2_psD", [128, 512], F32)
    ps1 = psC[0:1, :]

    TC = 512
    ones1 = p.sb("C2_ones1", [1, 128], BF16)
    q.op("gpsimd", lambda e: e.memset(ones1[:], 1.0))
    wrow1 = p.sb("C2_wrow1", [1, 512], F32)
    wrow1b = p.sb("C2_wrow1b", [1, 512], BF16)
    b1 = p.sb("C2_b1", [128, 8, 32], F32)
    q.dma("sync", lambda e: e.dma_start(out=b1[:], in_=b1_d[:]))
    b2 = p.sb("C2_b2", [128, 8, 8], F32)
    q.dma("sync", lambda e: e.dma_start(out=b2[:], in_=b2_d[:]))
    ln = LNHelper(p, TC, "c2", psA, ps1)

    x1bf = p.sb("C2_x1bf", [128, 8, TC], BF16)
    h = p.sb("C2_h", [128, 32, TC], BF16)
    acc = p.sb("C2_acc", [128, 8, TC], F32)
    wb = p.sb("C2_wb", [128, TC], F32)
    scr = p.sb("C2_scr", [128, 2, 512], F32)
    w1b = [p.sb(f"C2_w1_{i}", [128, 8, 2048], BF16) for i in range(2)]
    w2b2 = [p.sb(f"C2_w2_{i}", [128, 32, 512], BF16) for i in range(2)]

    w1_rel = {0: {}, 1: {}}
    w2_rel = [{}, {}]
    nexp = 0
    for tc0 in range(0, W, TC):
        e0 = q.dma("gpsimd", lambda e, tc0=tc0: e.dma_start(
            out=x1bf[:], in_=x1_d[:, tc0:tc0 + TC].rearrange("(kc p) m -> p kc m", p=128)))
        q.op("vector", lambda e: e.memset(acc[:], 0.0),
             waits={"tensor": q.cnt["tensor"], "sync": q.cnt["sync"],
                    "scalar": q.cnt["scalar"]})
        memset_mark = q.cnt["vector"]
        for ex in range(8):
            w1_loaded = {}
            for hf in range(2):
                q.dma("gpsimd", lambda e, ex=ex, hf=hf: e.dma_start(
                    out=w1b[hf][:],
                    in_=W1_d[ex, :, hf * 2048:(hf + 1) * 2048].rearrange(
                        "(kc p) m -> p kc m", p=128)),
                    waits=dict(w1_rel[hf]))
                w1_loaded[hf] = q.cnt["gpsimd"]
            marks = {}
            for i in range(32):
                mc = i
                hf = mc // 16
                waits = {"gpsimd": w1_loaded[hf]}
                if i >= 2:
                    waits["scalar"] = marks[i - 2]
                def mmh(e, mc=mc, hf=hf):
                    for kc in range(8):
                        inst = e.matmul(
                            (psA if mc % 2 == 0 else psB)[:, 0:512],
                            w1b[hf][:, kc, (mc % 16) * 128:(mc % 16 + 1) * 128],
                            x1bf[:, kc, :], start=(kc == 0), stop=(kc == 7))
                    return inst
                q.op("tensor", mmh, waits=waits)
                q.op("scalar", lambda e, mc=mc, ex=ex: e.activation(
                    h[:, mc, :], (psA if mc % 2 == 0 else psB)[:, 0:512],
                    AF.Gelu_apprx_tanh, bias=b1[:, ex, mc:mc + 1]),
                    waits={"tensor": q.cnt["tensor"]})
                marks[i] = q.cnt["scalar"]
                if i == 15:
                    w1_rel[0] = {"tensor": q.cnt["tensor"]}
            h_done = q.cnt["scalar"]
            w1_rel[1] = {"tensor": q.cnt["tensor"]}
            q.dma("sync", lambda e, ex=ex, tc0=tc0: e.dma_start(
                out=wrow1[:], in_=w_d[ex:ex + 1, tc0:tc0 + TC]),
                waits={"vector": q.cnt["vector"]})
            q.op("vector", lambda e: e.tensor_copy(wrow1b[:], wrow1[:]),
                 waits={"sync": q.cnt["sync"]})
            q.op("tensor", lambda e: e.matmul(
                psD[:, 0:512], ones1[:], wrow1b[:], start=True, stop=True),
                waits={"vector": q.cnt["vector"]})
            q.op("vector", lambda e: e.tensor_copy(wb[:], psD[:, 0:512]),
                 waits={"tensor": q.cnt["tensor"]})
            wb_done = q.cnt["vector"]
            w2_loaded = {}
            for hf in range(2):
                q.dma("gpsimd", lambda e, ex=ex, hf=hf: e.dma_start(
                    out=w2b2[hf][:],
                    in_=W2_d[ex, :, hf * 512:(hf + 1) * 512].rearrange(
                        "(kc p) m -> p kc m", p=128)),
                    waits=dict(w2_rel[hf]))
                w2_loaded[hf] = q.cnt["gpsimd"]
            vmarks = {}
            for i in range(8):
                dc = i
                hf = dc // 4
                waits = {"gpsimd": w2_loaded[hf], "scalar": h_done}
                if i >= 2:
                    waits["scalar"] = max(h_done, vmarks[i - 2][0])
                def mmo(e, dc=dc, hf=hf):
                    for kc in range(32):
                        inst = e.matmul(
                            (psA if dc % 2 == 0 else psB)[:, 0:512],
                            w2b2[hf][:, kc, (dc % 4) * 128:(dc % 4 + 1) * 128],
                            h[:, kc, :], start=(kc == 0), stop=(kc == 31))
                    return inst
                q.op("tensor", mmo, waits=waits)
                sw = {"tensor": q.cnt["tensor"]}
                if i >= 2:
                    sw["vector"] = vmarks[i - 2][1]
                q.op("scalar", lambda e, dc=dc, ex=ex: e.activation(
                    scr[:, dc % 2, :], (psA if dc % 2 == 0 else psB)[:, 0:512],
                    AF.Identity, bias=b2[:, ex, dc:dc + 1]), waits=sw)
                smark = q.cnt["scalar"]
                vw = {"scalar": smark, "vector": max(wb_done, memset_mark)}
                q.op("vector", lambda e, dc=dc: e.tensor_mul(
                    scr[:, dc % 2, :], scr[:, dc % 2, :], wb[:]), waits=vw)
                q.op("vector", lambda e, dc=dc: e.tensor_add(
                    acc[:, dc, :], acc[:, dc, :], scr[:, dc % 2, :]), waits={})
                vmarks[i] = (smark, q.cnt["vector"])
                if i == 3:
                    w2_rel[0] = {"tensor": q.cnt["tensor"]}
            w2_rel[1] = {"tensor": q.cnt["tensor"]}
            nexp += 1
        for kc in range(8):
            q.dma("sync", lambda e, kc=kc, tc0=tc0: e.dma_start(
                out=scr[:, 0, :],
                in_=x1_d[kc * 128:(kc + 1) * 128, tc0:tc0 + TC]))
            q.op("vector", lambda e, kc=kc: e.tensor_add(
                acc[:, kc, :], acc[:, kc, :], scr[:, 0, :]))
        ln.emit(acc, acc, 8)
        q.dma("sync", lambda e, tc0=tc0: e.dma_start(
            out=x15_d[:, tc0:tc0 + TC].rearrange("(kc p) m -> p kc m", p=128),
            in_=acc[:]))
    return p.finish()


EXT = 2688
QOFF = 637
NQ = 2051


def build_D():
    p = Prog("D")
    q = p.q
    x_d = p.din("x15D", [D, EXT])
    valid_d = p.din("validD", [1, EXT])
    wq_d = p.din("WqT", [D, D])
    wk_d = p.din("WkT", [D, D])
    wv_d = p.din("WvT", [D, D])
    wo_d = p.din("WoT", [D, D])
    w1blk_d = p.din("W1blk", [4, 8, 128, 128])
    w2blk_d = p.din("W2blk", [4, 8, 128, 512])
    wcp_d = p.din("WcpT", [16384, D])
    x25_d = p.dout("x25D", [D, NQ])
    x2_dram = p.dtmp("D_x2dram", [D, NQ], F32)

    psS = p.ps("D_psS", [128, 512], F32)
    psC = p.ps("D_psC", [128, 512], F32)
    psY = p.ps("D_psY", [128, 512], F32)
    psZ = p.ps("D_psZ", [128, 512], F32)
    psO = [p.ps(f"D_psO{i}", [128, 512], F32) for i in range(4)]
    ps1 = psY[0:1, :]
    psT = psZ[:, 0:64].bitcast(BF16)

    QT = [(QOFF, 3)] + [(QOFF + 3 + 512 * k, 512) for k in range(4)]
    TT5 = [(0, 3)] + [(3 + 512 * k, 512) for k in range(4)]
    NKT = EXT // 128

    ident = p.sb("D_ident", [128, 128], BF16)
    onesb = p.sb("D_onesb", [128, 128], BF16)
    q.op("gpsimd", lambda e: e.memset(onesb[:], 1.0))
    q.op("gpsimd", lambda e: e.affine_select(
        ident[:], onesb[:], pattern=[[1, 128]], compare_op=ALU.is_equal,
        fill=0.0, base=0, channel_multiplier=-1))
    ones1 = p.sb("D_ones1", [1, 128], F32)
    q.op("gpsimd", lambda e: e.memset(ones1[:], 1.0))
    onescF = p.sb("D_onescF", [128, 1], F32)
    q.op("gpsimd", lambda e: e.memset(onescF[:], 1.0))
    zeros = p.sb("D_zeros", [128, 512], F32)
    q.op("gpsimd", lambda e: e.memset(zeros[:], 0.0))
    ln = LNHelper(p, 512, "d", psS, ps1)
    attnout = p.sb("D_attnout", [128, 8, NQ], BF16)

    wcp_bf = p.dtmp("D_wcpbf", [16384, D], BF16)
    p.push()
    wstage = p.sb("D_wstage", [128, 8, 1024], BF16)
    for r in range(16):
        q.dma("gpsimd", lambda e, r=r: e.dma_start(
            out=wstage[:],
            in_=wcp_d[r * 1024:(r + 1) * 1024, :].rearrange(
                "(kc p) m -> p kc m", p=128)),
            waits={"sync": q.cnt["sync"]})
        q.dma("sync", lambda e, r=r: e.dma_start(
            out=wcp_bf[r * 1024:(r + 1) * 1024, :].rearrange(
                "(kc p) m -> p kc m", p=128),
            in_=wstage[:]), waits={"gpsimd": q.cnt["gpsimd"]})
    p.pop()

    p.push()
    xb = p.sb("D_xb", [128, 8, EXT], BF16)
    q.dma("gpsimd", lambda e: e.dma_start(
        out=xb[:], in_=x_d.rearrange("(kc p) m -> p kc m", p=128)))
    kvc = p.sb("D_kvc", [128, NKT], F32)
    vrow = p.sb("D_vrow", [1, EXT], F32)
    q.dma("sync", lambda e: e.dma_start(out=vrow[:], in_=valid_d[:]))
    q.op("vector", lambda e: e.tensor_scalar(
        vrow[:], vrow[:], 8e9, -8e9, op0=ALU.mult, op1=ALU.add))
    for kt in range(NKT):
        q.op("tensor", lambda e, kt=kt: e.transpose(
            psS[:, 0:1], vrow[:, kt * 128:(kt + 1) * 128], ones1[0:1, 0:1]))
        q.op("vector", lambda e, kt=kt: e.tensor_copy(kvc[:, kt:kt + 1], psS[:, 0:1]))

    whq = p.sb("D_whq", [128, 8, 64], BF16)
    whk = p.sb("D_whk", [128, 8, 64], BF16)
    whv = p.sb("D_whv", [128, 8, 64], BF16)
    qh = p.sb("D_qh", [64, EXT], BF16)
    kh = p.sb("D_kh", [64, EXT], BF16)
    vh = p.sb("D_vh", [64, EXT], BF16)
    vaug = p.sb("D_vaug", [128, NKT, 65], BF16)
    sm = p.sb("D_sm", [128, 512], F32)
    pt = p.sb("D_pt", [128, 512], BF16)
    msk = p.sb("D_msk", [128, 512], F32)
    msk2 = p.sb("D_msk2", [128, 512], F32)
    smB = p.sb("D_smB", [128, 512], F32)
    ptB = p.sb("D_ptB", [128, 512], BF16)
    mskB = p.sb("D_mskB", [128, 512], F32)
    msk2B = p.sb("D_msk2B", [128, 512], F32)
    pt_rel = {}
    ctx = p.sb("D_ctx", [65, 512], F32)
    rb = p.sb("D_rb", [64, 512], F32)
    rrow = p.sb("D_rrow", [1, 512], F32)
    negb = p.sb("D_negb", [128, 1], F32)
    nrm = p.sb("D_nrm", [1, 8], F32)
    sq64 = p.sb("D_sq64", [64, 512], F32)
    ETT = [(512 * k, 512) for k in range(5)] + [(2560, 128)]

    msk512 = p.sb("D_msk512", [128, 8, 512], F32)
    for m in range(-3, 5):
        q.op("gpsimd", lambda e, m=m: e.iota(
            msk[:, 0:512], pattern=[[1, 512]], base=128 * m,
            channel_multiplier=-1, allow_small_or_imprecise_dtypes=True))
        q.op("vector", lambda e: e.tensor_single_scalar(
            msk2[:, 0:512], msk[:, 0:512], 0.0, op=ALU.is_ge))
        q.op("vector", lambda e: e.tensor_single_scalar(
            msk[:, 0:512], msk[:, 0:512], 512.0, op=ALU.is_lt))
        q.op("vector", lambda e: e.tensor_mul(
            msk[:, 0:512], msk[:, 0:512], msk2[:, 0:512]))
        q.op("vector", lambda e, m=m: e.tensor_scalar(
            msk512[:, m + 3, :], msk[:, 0:512], 8e9, -8e9,
            op0=ALU.mult, op1=ALU.add))

    for h in range(16):
        for (wd, wh) in ((wq_d, whq), (wk_d, whk), (wv_d, whv)):
            q.dma("gpsimd", lambda e, wd=wd, wh=wh, h=h: e.dma_start(
                out=wh[:], in_=wd[:, 64 * h:64 * (h + 1)].rearrange(
                    "(kc p) m -> p kc m", p=128)))
        for (wh, dst) in ((whq, qh), (whk, kh), (whv, vh)):
            for (t0, w) in ETT:
                def mmp(e, wh=wh, t0=t0, w=w):
                    for kc in range(8):
                        inst = e.matmul(psS[0:64, 0:w], wh[:, kc, :],
                                        xb[:, kc, t0:t0 + w],
                                        start=(kc == 0), stop=(kc == 7))
                    return inst
                q.op("tensor", mmp)
                q.op("vector", lambda e, dst=dst, t0=t0, w=w: e.tensor_copy(
                    dst[:, t0:t0 + w], psS[0:64, 0:w]))
        for (src, col) in ((qh, 0), (kh, 1)):
            for i, (t0, w) in enumerate(ETT):
                q.op("scalar", lambda e, src=src, t0=t0, w=w: e.activation(
                    sq64[:, 0:w], src[:, t0:t0 + w], AF.Square))
                q.op("tensor", lambda e, w=w: e.matmul(
                    ps1[:, 0:w], onescF[0:64, :], sq64[:, 0:w],
                    start=True, stop=True))
                tgt = col if i == 0 else 2
                q.op("vector", lambda e, tgt=tgt, w=w: e.reduce_max(
                    nrm[:, tgt:tgt + 1], ps1[:, 0:w], axis=AX.X))
                if i > 0:
                    q.op("vector", lambda e, col=col: e.tensor_max(
                        nrm[:, col:col + 1], nrm[:, col:col + 1], nrm[:, 2:3]))
        q.op("scalar", lambda e: e.activation(nrm[:, 0:1], nrm[:, 0:1], AF.Sqrt))
        q.op("scalar", lambda e: e.activation(nrm[:, 1:2], nrm[:, 1:2], AF.Sqrt))
        q.op("vector", lambda e: e.tensor_mul(nrm[:, 3:4], nrm[:, 0:1], nrm[:, 1:2]))
        q.op("vector", lambda e: e.tensor_scalar_mul(nrm[:, 3:4], nrm[:, 3:4], -0.125))
        q.op("tensor", lambda e: e.matmul(
            psY[:, 0:1], ones1[:], nrm[:, 3:4], start=True, stop=True))
        q.op("vector", lambda e: e.tensor_copy(negb[:], psY[:, 0:1]))
        for kt in range(NKT):
            q.op("tensor", lambda e, kt=kt: e.transpose(
                psT[:, 0:64], vh[:, kt * 128:(kt + 1) * 128], ident[0:64, 0:64]))
            q.op("vector", lambda e, kt=kt: e.tensor_copy(vaug[:, kt, 0:64], psT[:, 0:64]))
        q.op("gpsimd", lambda e: e.memset(vaug[:, :, 64:65], 1.0))
        hc, hb = h // 2, (h % 2) * 64
        entry = q.snap()
        for (t0, w) in QT:
            kt0 = max((t0 - 512) // 128, 0)
            kt1 = (t0 + w + 127) // 128
            nkt = kt1 - kt0
            vm = {}
            sm_marks = {}
            for ki, kt in enumerate(range(kt0, kt1)):
                bank = psS if ki % 2 == 0 else psZ
                tw = dict(entry)
                if ki >= 2:
                    tw["vector"] = vm[ki - 2]
                q.op("tensor", lambda e, kt=kt, t0=t0, w=w, bank=bank: e.matmul(
                    bank[:, 0:w], kh[:, kt * 128:(kt + 1) * 128],
                    qh[:, t0:t0 + w], start=True, stop=True), waits=tw)
                smm = q.cnt["tensor"]
                if w == 512:
                    mi = (t0 - 128 * kt) // 128 + 3
                    assert 0 <= mi < 8 and t0 - 128 * kt == 128 * (mi - 3)
                    mk = msk512[:, mi, :]
                else:
                    gw = dict(entry)
                    if ki >= 2:
                        gw["vector"] = max(gw.get("vector", 0), vm[ki - 2])
                    q.op("gpsimd", lambda e, t0=t0, kt=kt, w=w, bank=bank: e.iota(
                        (msk if bank is psS else mskB)[:, 0:w], pattern=[[1, w]],
                        base=t0 - 128 * kt,
                        channel_multiplier=-1, allow_small_or_imprecise_dtypes=True),
                        waits=gw)
                    gm = q.cnt["gpsimd"]
                    mk0 = msk if ki % 2 == 0 else mskB
                    mk2 = msk2 if ki % 2 == 0 else msk2B
                    q.op("vector", lambda e, w=w, mk0=mk0, mk2=mk2: e.tensor_single_scalar(
                        mk2[:, 0:w], mk0[:, 0:w], 0.0, op=ALU.is_ge),
                        waits={"gpsimd": gm})
                    q.op("vector", lambda e, w=w, mk0=mk0: e.tensor_single_scalar(
                        mk0[:, 0:w], mk0[:, 0:w], 512.0, op=ALU.is_lt), waits={})
                    q.op("vector", lambda e, w=w, mk0=mk0, mk2=mk2: e.tensor_mul(
                        mk0[:, 0:w], mk0[:, 0:w], mk2[:, 0:w]), waits={})
                    q.op("vector", lambda e, w=w, mk0=mk0: e.tensor_scalar(
                        mk0[:, 0:w], mk0[:, 0:w], 8e9, -8e9, op0=ALU.mult, op1=ALU.add),
                        waits={})
                    mk = mk0[:, 0:w]
                sw = {"tensor": smm}
                if ki >= 2:
                    sw["scalar"] = sm_marks[ki - 2]
                q.op("vector", lambda e, kt=kt, w=w, bank=bank, mk=mk: e.scalar_tensor_tensor(
                    (sm if bank is psS else smB)[:, 0:w], bank[:, 0:w],
                    kvc[:, kt:kt + 1], mk,
                    op0=ALU.add, op1=ALU.add), waits=sw)
                vm[ki] = q.cnt["vector"]
                ptw = dict(entry)
                ptw["vector"] = vm[ki]
                if ki >= 2:
                    ptw["tensor"] = max(ptw.get("tensor", 0), pt_rel[ki - 2])
                q.op("scalar", lambda e, w=w, bank=bank: e.activation(
                    (pt if bank is psS else ptB)[:, 0:w],
                    (sm if bank is psS else smB)[:, 0:w],
                    AF.Exp, bias=negb[:, 0:1], scale=0.125), waits=ptw)
                sm_marks[ki] = q.cnt["scalar"]
                q.op("tensor", lambda e, kt=kt, ki=ki, nkt=nkt, w=w, bank=bank: e.matmul(
                    psC[0:65, 0:w], vaug[:, kt, :],
                    (pt if bank is psS else ptB)[:, 0:w],
                    start=(ki == 0), stop=(ki == nkt - 1)),
                    waits={"scalar": sm_marks[ki]})
                pt_rel[ki] = q.cnt["tensor"]
            q.op("vector", lambda e, w=w: e.tensor_copy(ctx[:, 0:w], psC[0:65, 0:w]),
                 waits={"tensor": q.cnt["tensor"]})
            q.op("vector", lambda e, w=w: e.reciprocal(rrow[:, 0:w], ctx[64:65, 0:w]),
                 waits={})
            q.op("tensor", lambda e, w=w: e.matmul(
                psY[0:64, 0:w], ones1[0:1, 0:64], rrow[:, 0:w],
                start=True, stop=True), waits={"vector": q.cnt["vector"]})
            q.op("vector", lambda e, w=w: e.tensor_copy(rb[:, 0:w], psY[0:64, 0:w]),
                 waits={"tensor": q.cnt["tensor"]})
            q.op("vector", lambda e, w=w: e.tensor_mul(
                ctx[0:64, 0:w], ctx[0:64, 0:w], rb[:, 0:w]), waits={})
            q.op("vector", lambda e, t0=t0, w=w, hc=hc, hb=hb: e.tensor_copy(
                attnout[hb:hb + 64, hc, t0 - QOFF:t0 - QOFF + w], ctx[0:64, 0:w]),
                waits={})
            entry = q.snap()
    p.pop()

    p.push()
    wsb = load_w_bf16(p, "D_wo", wo_d, D, D)
    x2t = p.sb("D_x2t", [128, 8, 512], F32)
    for (t0, w) in TT5:
        q.dma("sync", lambda e, t0=t0, w=w: e.dma_start(
            out=x2t[:, :, 0:w],
            in_=x_d[:, QOFF + t0:QOFF + t0 + w].rearrange("(kc p) m -> p kc m", p=128)))
        for mc in range(8):
            def mmwo(e, mc=mc, t0=t0, w=w):
                for kc in range(8):
                    inst = e.matmul(psS[:, 0:w],
                                    wsb[:, kc, mc * 128:(mc + 1) * 128],
                                    attnout[:, kc, t0:t0 + w],
                                    start=(kc == 0), stop=(kc == 7))
                return inst
            q.op("tensor", mmwo)
            q.op("vector", lambda e, mc=mc, w=w: e.tensor_add(
                x2t[:, mc, 0:w], x2t[:, mc, 0:w], psS[:, 0:w]))
        ln.emit_w(x2t, x2t, 8, w)
        q.dma("sync", lambda e, t0=t0, w=w: e.dma_start(
            out=x2_dram[:, t0:t0 + w].rearrange("(kc p) m -> p kc m", p=128),
            in_=x2t[:, :, 0:w]))
    p.pop()

    p.push()
    w1blk = p.sb("D_w1blk", [128, 4, 8, 128], BF16)
    q.dma("gpsimd", lambda e: e.dma_start(
        out=w1blk[:], in_=w1blk_d.rearrange("h g p m -> p h g m")))
    w2blk = p.sb("D_w2blk", [128, 4, 8, 512], BF16)
    q.dma("gpsimd", lambda e: e.dma_start(
        out=w2blk[:], in_=w2blk_d.rearrange("h g p m -> p h g m")))
    wcpb = p.sb("D_wcpb", [128, 2, 4, 512], BF16)
    x2bt = p.sb("D_x2bt", [128, 8, 512], BF16)
    x2rt = p.sb("D_x2rt", [128, 8, 512], F32)
    ybt = p.sb("D_ybt", [128, 2, 512], BF16)
    hgt = p.sb("D_hgt", [128, 4, 512], BF16)
    for (t0, w) in TT5:
        q.dma("gpsimd", lambda e, t0=t0, w=w: e.dma_start(
            out=x2bt[:, :, 0:w],
            in_=x2_dram[:, t0:t0 + w].rearrange("(kc p) m -> p kc m", p=128)))
        q.dma("sync", lambda e, t0=t0, w=w: e.dma_start(
            out=x2rt[:, :, 0:w],
            in_=x2_dram[:, t0:t0 + w].rearrange("(kc p) m -> p kc m", p=128)))
        for dh in range(2):
            entry = q.snap()
            ycopy = {}
            gelu_m = {}
            cp_rel = {}
            ld_m = {}
            first = True
            u = 0
            for h4 in range(4):
                for g8 in range(8):
                    fc0 = h4 * 32 + g8 * 4
                    lw = dict(entry)
                    if u >= 2:
                        lw["tensor"] = max(lw.get("tensor", 0), cp_rel[(u - 2, 3)])
                    q.dma("gpsimd", lambda e, fc0=fc0, dh=dh, u=u: e.dma_start(
                        out=wcpb[:, u % 2],
                        in_=wcp_bf[fc0 * 128:(fc0 + 4) * 128,
                                   dh * 512:(dh + 1) * 512].rearrange(
                            "(m p) c -> p m c", p=128)),
                        waits=lw)
                    ld_m[u] = q.cnt["gpsimd"]
                    ybank = psY if u % 2 == 0 else psS
                    yslot = u % 2
                    tw = dict(entry)
                    if u >= 2:
                        tw["vector"] = max(tw.get("vector", 0), ycopy[u - 2])
                    q.op("tensor", lambda e, h4=h4, g8=g8, w=w, ybank=ybank: e.matmul(
                        ybank[:, 0:w], w1blk[:, h4, g8, :], x2bt[:, g8, 0:w],
                        start=True, stop=True), waits=tw)
                    q.op("vector", lambda e, w=w, ybank=ybank, yslot=yslot:
                         e.tensor_copy(ybt[:, yslot, 0:w], ybank[:, 0:w]),
                         waits={"tensor": q.cnt["tensor"]})
                    ycopy[u] = q.cnt["vector"]
                    for mi in range(4):
                        fc = h4 * 32 + g8 * 4 + mi
                        zbank = psZ if mi % 2 == 0 else psC
                        zw = dict(entry)
                        zw["vector"] = ycopy[u]
                        if (u, mi) in gelu_m or u >= 1:
                            prev = gelu_m.get((u, mi - 2)) or gelu_m.get((u - 1, mi + 2))
                            if prev:
                                zw["scalar"] = max(zw.get("scalar", 0), prev)
                        q.op("tensor", lambda e, h4=h4, g8=g8, mi=mi, w=w,
                             zbank=zbank, yslot=yslot: e.matmul(
                            zbank[:, 0:w], w2blk[:, h4, g8, mi * 128:(mi + 1) * 128],
                            ybt[:, yslot, 0:w], start=True, stop=True), waits=zw)
                        gw = {"tensor": q.cnt["tensor"]}
                        if (u - 1, mi) in cp_rel:
                            gw["tensor"] = q.cnt["tensor"]
                        q.op("scalar", lambda e, w=w, mi=mi, zbank=zbank: e.activation(
                            hgt[:, mi, 0:w], zbank[:, 0:w], AF.Gelu), waits=gw)
                        gelu_m[(u, mi)] = q.cnt["scalar"]
                        cpw = dict(entry)
                        cpw["scalar"] = gelu_m[(u, mi)]
                        cpw["gpsimd"] = max(cpw.get("gpsimd", 0), ld_m[u])
                        for oc in range(4):
                            q.op("tensor", lambda e, oc=oc, w=w, mi=mi, u=u,
                                 first=first: e.matmul(
                                psO[oc][:, 0:w],
                                wcpb[:, u % 2, mi, oc * 128:(oc + 1) * 128],
                                hgt[:, mi, 0:w], start=first,
                                stop=(fc == 127)),
                                waits=cpw)
                        cp_rel[(u, mi)] = q.cnt["tensor"]
                        first = False
                    u += 1
            for oc in range(4):
                dc = dh * 4 + oc
                q.op("vector", lambda e, oc=oc, dc=dc, w=w: e.tensor_add(
                    x2rt[:, dc, 0:w], x2rt[:, dc, 0:w], psO[oc][:, 0:w]),
                    waits={"tensor": q.cnt["tensor"], "sync": q.cnt["sync"]})
        ln.emit_w(x2rt, x2rt, 8, w)
        q.dma("sync", lambda e, t0=t0, w=w: e.dma_start(
            out=x25_d[:, t0:t0 + w].rearrange("(kc p) m -> p kc m", p=128),
            in_=x2rt[:, :, 0:w]))
    p.pop()
    return p.finish()


DI = 2048


def build_E():
    p = Prog("E")
    q = p.q
    x_d = p.din("x25E", [D, 3 + W])
    win_d = p.din("W_in", [D, 2 * DI])
    cw_d = p.din("conv_w_h", [128, 16, 4])
    cb_d = p.din("conv_b_h", [128, 16])
    alog_d = p.din("A_log_h", [128, 16, 16])
    uc_d = p.dout("ucE", [DI, W])
    zs_d = p.dout("zsE", [DI, W])
    S_d = p.dout("SE", [128, 16, 16])

    psA = p.ps("E_psA", [128, 512], F32)
    psB = p.ps("E_psB", [128, 512], F32)

    xb = load_fm_f32(p, "E_xb", x_d, D, 3 + W, cast=BF16)
    wsb = load_w_bf16(p, "E_win", win_d, D, 2 * DI)
    cw = p.sb("E_cw", [128, 16, 4], F32)
    q.dma("sync", lambda e: e.dma_start(out=cw[:], in_=cw_d[:]))
    cb = p.sb("E_cb", [128, 16], F32)
    q.dma("sync", lambda e: e.dma_start(out=cb[:], in_=cb_d[:]))
    alog = p.sb("E_alog", [128, 16, 16], F32)
    q.dma("sync", lambda e: e.dma_start(out=alog[:], in_=alog_d[:]))
    A = p.sb("E_A", [128, 16, 16], F32)
    q.op("scalar", lambda e: e.activation(A[:], alog[:], AF.Exp))
    q.op("vector", lambda e: e.tensor_scalar_mul(A[:], A[:], -1.0))

    uraw = p.sb("E_uraw", [128, 3 + W], F32)
    ucv = p.sb("E_ucv", [128, W], F32)
    ucvb = p.sb("E_ucvb", [128, W], F32)
    scr = p.sb("E_scr", [128, W], F32)
    S = p.sb("E_S", [128, 16, 16], F32)
    zbuf = p.sb("E_zbuf", [128, 2, W], F32)

    pre = q.snap()
    gi = 0
    copymark = {}
    for mc in range(16):
        for t0 in range(0, 3 + W, 512):
            t1 = min(t0 + 512, 3 + W)
            bank = psA if gi % 2 == 0 else psB
            tw = dict(pre)
            if gi >= 2:
                tw["vector"] = max(tw.get("vector", 0), copymark[gi - 2])

            def mmu(e, mc=mc, t0=t0, t1=t1, bank=bank):
                for kc in range(8):
                    inst = e.matmul(bank[:, 0:t1 - t0],
                                    wsb[:, kc, mc * 128:(mc + 1) * 128],
                                    xb[:, kc, t0:t1], start=(kc == 0), stop=(kc == 7))
                return inst
            q.op("tensor", mmu, waits=tw)
            q.op("vector", lambda e, t0=t0, t1=t1, bank=bank: e.tensor_copy(
                uraw[:, t0:t1], bank[:, 0:t1 - t0]),
                waits={"tensor": q.cnt["tensor"]})
            copymark[gi] = q.cnt["vector"]
            gi += 1
        q.op("vector", lambda e, mc=mc: e.tensor_scalar_mul(
            ucv[:], uraw[:, 0:W], cw[:, mc, 0:1]))
        for k in range(1, 4):
            q.op("vector", lambda e, mc=mc, k=k: e.scalar_tensor_tensor(
                ucv[:], uraw[:, k:k + W], cw[:, mc, k:k + 1], ucv[:],
                op0=ALU.mult, op1=ALU.add))
        q.op("vector", lambda e, mc=mc: e.tensor_scalar_add(
            ucv[:], ucv[:], cb[:, mc:mc + 1]))
        q.dma("sync", lambda e, mc=mc: e.dma_start(
            out=uc_d[mc * 128:(mc + 1) * 128, :], in_=ucv[:]))
        for s in range(16):
            q.op("vector", lambda e, mc=mc, s=s: e.tensor_tensor_scan(
                scr[:], A[:, mc, s:s + 1].broadcast_to([128, W]), ucv[:],
                initial=0.0, op0=ALU.mult, op1=ALU.add))
            q.op("vector", lambda e, mc=mc, s=s: e.tensor_copy(
                S[:, mc, s:s + 1], scr[:, W - 1:W]))
    q.dma("sync", lambda e: e.dma_start(out=S_d[:], in_=S[:]))
    ucopy_done = copymark[gi - 1]
    gz = 0
    silumark = {}
    zdma = {}
    for mc in range(16):
        for t0 in range(0, W, 512):
            bank = psA if gz % 2 == 0 else psB
            tw = dict(pre)
            tw["vector"] = max(tw.get("vector", 0), ucopy_done)
            if gz >= 2:
                tw["scalar"] = max(tw.get("scalar", 0), silumark[gz - 2])

            def mmz(e, mc=mc, t0=t0, bank=bank):
                for kc in range(8):
                    inst = e.matmul(bank[:, 0:512],
                                    wsb[:, kc, 2048 + mc * 128:2048 + (mc + 1) * 128],
                                    xb[:, kc, 3 + t0:3 + t0 + 512],
                                    start=(kc == 0), stop=(kc == 7))
                return inst
            q.op("tensor", mmz, waits=tw)
            sw = {"tensor": q.cnt["tensor"]}
            if mc >= 2 and t0 == 0:
                sw["sync"] = zdma[mc - 2]
            q.op("scalar", lambda e, t0=t0, mc=mc, bank=bank: e.activation(
                zbuf[:, mc % 2, t0:t0 + 512], bank[:, 0:512], AF.Silu), waits=sw)
            silumark[gz] = q.cnt["scalar"]
            gz += 1
        q.dma("sync", lambda e, mc=mc: e.dma_start(
            out=zs_d[mc * 128:(mc + 1) * 128, :], in_=zbuf[:, mc % 2, :]),
            waits={"scalar": q.cnt["scalar"]})
        zdma[mc] = q.cnt["sync"]
    return p.finish()



def build_F():
    p = Prog("F")
    q = p.q
    uc_d = p.din("ucF", [DI, W])
    zs_d = p.din("zsF", [DI, W])
    x25_d = p.din("x25F", [D, W])
    alog_d = p.din("A_log_h", [128, 16, 16])
    dprm_d = p.din("D_param_h", [128, 16])
    wout_d = p.din("W_out", [DI, D])
    sprev_d = p.din("SprevF", [3, 128, 16, 16])
    wg_d = p.din("wgF", [D, 8])
    x3_d = p.dout("x3F", [D, W])
    g_d = p.dout("gF", [8, W])

    psA = p.ps("F_psA", [128, 512], F32)
    psB = p.ps("F_psB", [128, 512], F32)
    ps1 = psB[0:1, :]
    ps8 = psB[0:8, :]
    pst = p.ps("F_pst", [128, 128], F32)

    alog = p.sb("F_alog", [128, 16, 16], F32)
    q.dma("sync", lambda e: e.dma_start(out=alog[:], in_=alog_d[:]))
    A = p.sb("F_A", [128, 16, 16], F32)
    q.op("scalar", lambda e: e.activation(A[:], alog[:], AF.Exp))
    q.op("vector", lambda e: e.tensor_scalar_mul(A[:], A[:], -1.0))
    dprm = p.sb("F_dprm", [128, 16], F32)
    q.dma("sync", lambda e: e.dma_start(out=dprm[:], in_=dprm_d[:]))
    carry = p.sb("F_carry", [128, 16, 16], F32)
    mid = p.sb("F_mid", [128, 16, 16], F32)
    x3 = p.sb("F_x3", [128, 8, W], F32)
    q.dma("sync", lambda e: e.dma_start(
        out=x3[:], in_=x25_d.rearrange("(kc p) m -> p kc m", p=128)))
    ln = LNHelper(p, 512, "f", psA, ps1)

    p.push()
    sprev = p.sb("F_sprev", [128, 3, 16, 16], F32)
    q.dma("sync", lambda e: e.dma_start(
        out=sprev[:], in_=sprev_d.rearrange("k p c s -> p k c s")))
    dec = p.sb("F_dec", [128, 16, 16], F32)
    q.op("vector", lambda e: e.memset(carry[:], 0.0))
    for k in range(1, 4):
        q.op("scalar", lambda e, k=k: e.activation(
            dec[:], alog[:], AF.Exp, scale=float(2048.0 * (k - 1))))
        q.op("vector", lambda e, k=k: e.tensor_mul(
            dec[:], dec[:], sprev[:, k - 1, :, :]))
        q.op("vector", lambda e: e.tensor_add(carry[:], carry[:], dec[:]))
    p.pop()

    TH = 1024
    p.push()
    wsb = load_w_bf16(p, "F_wout", wout_d, DI, D)
    ucv = p.sb("F_ucv", [128, TH], F32)
    zsv = p.sb("F_zsv", [128, TH], F32)
    scr = p.sb("F_scr", [128, TH], F32)
    yv = p.sb("F_yv", [128, TH], F32)
    ybf = p.sb("F_ybf", [128, 16, TH], BF16)
    for th in range(2):
        h0 = th * TH
        for mc in range(16):
            q.dma("sync", lambda e, mc=mc, h0=h0: e.dma_start(
                out=ucv[:], in_=uc_d[mc * 128:(mc + 1) * 128, h0:h0 + TH]))
            q.dma("sync", lambda e, mc=mc, h0=h0: e.dma_start(
                out=zsv[:], in_=zs_d[mc * 128:(mc + 1) * 128, h0:h0 + TH]))
            for s in range(16):
                init = carry[:, mc, s:s + 1] if th == 0 else mid[:, mc, s:s + 1]
                q.op("vector", lambda e, mc=mc, s=s, init=init: e.tensor_tensor_scan(
                    scr[:], A[:, mc, s:s + 1].broadcast_to([128, TH]), ucv[:],
                    initial=init, op0=ALU.mult, op1=ALU.add))
                if th == 0:
                    q.op("vector", lambda e, mc=mc, s=s: e.tensor_copy(
                        mid[:, mc, s:s + 1], scr[:, TH - 1:TH]))
                if s == 0:
                    q.op("vector", lambda e: e.tensor_copy(yv[:], scr[:]))
                else:
                    q.op("vector", lambda e: e.tensor_add(yv[:], yv[:], scr[:]))
            q.op("vector", lambda e, mc=mc: e.scalar_tensor_tensor(
                yv[:], ucv[:], dprm[:, mc:mc + 1], yv[:], op0=ALU.mult, op1=ALU.add))
            q.op("vector", lambda e: e.tensor_mul(yv[:], yv[:], zsv[:]))
            q.op("vector", lambda e, mc=mc: e.tensor_copy(ybf[:, mc, :], yv[:]))
        for mc in range(8):
            for t0 in range(0, TH, 512):
                def mmwo(e, mc=mc, t0=t0):
                    for kc in range(16):
                        inst = e.matmul(psA[:, 0:512],
                                        wsb[:, kc, mc * 128:(mc + 1) * 128],
                                        ybf[:, kc, t0:t0 + 512],
                                        start=(kc == 0), stop=(kc == 15))
                    return inst
                q.op("tensor", mmwo)
                q.op("vector", lambda e, mc=mc, t0=t0, h0=h0: e.tensor_add(
                    x3[:, mc, h0 + t0:h0 + t0 + 512], x3[:, mc, h0 + t0:h0 + t0 + 512],
                    psA[:, 0:512]))
    p.pop()
    x3v = x3[:].rearrange("p kc (n t) -> p kc n t", t=512)
    for ti in range(4):
        ln.emit(x3v[:, :, ti, :], x3v[:, :, ti, :], 8)
    store_fm(p, x3_d, x3, 8, W)
    p.push()
    x3b = p.sb("F_x3b", [128, 8, W], BF16)
    emit_cast(p, x3b, x3, 8, W)
    wg = load_w_bf16(p, "F_wg", wg_d, D, 8)
    gsc = p.sb("F_gsc", [8, W], F32)
    for it in range(0, W, 512):
        def mmg(e, it=it):
            for kc in range(8):
                inst = e.matmul(ps8[:, 0:512], wg[:, kc, :], x3b[:, kc, it:it + 512],
                                start=(kc == 0), stop=(kc == 7))
            return inst
        q.op("tensor", mmg)
        q.op("vector", lambda e, it=it: e.tensor_copy(
            gsc[:, it:it + 512], ps8[:, 0:512]))
    identF = p.sb("F_identF", [128, 128], F32)
    q.op("gpsimd", lambda e: e.memset(identF[:], 1.0))
    q.op("gpsimd", lambda e: e.affine_select(
        identF[:], identF[:], pattern=[[1, 128]], compare_op=ALU.is_equal,
        fill=0.0, base=0, channel_multiplier=-1))
    gt = p.sb("F_gt", [128, 16, 8], F32)
    red = p.sb("F_red", [128, 16, 1], F32)
    for ti in range(16):
        q.op("tensor", lambda e, ti=ti: e.transpose(
            pst[:, 0:8], gsc[:, ti * 128:(ti + 1) * 128], identF[0:8, 0:8]))
        q.op("vector", lambda e, ti=ti: e.tensor_copy(gt[:, ti, :], pst[:, 0:8]))
    q.op("vector", lambda e: e.tensor_reduce(red[:], gt[:], axis=AX.X, op=ALU.max))
    q.op("vector", lambda e: e.tensor_sub(gt[:], gt[:], red[:].broadcast_to([128, 16, 8])))
    q.op("scalar", lambda e: e.activation(gt[:], gt[:], AF.Exp))
    q.op("vector", lambda e: e.tensor_reduce(red[:], gt[:], axis=AX.X, op=ALU.add))
    q.op("vector", lambda e: e.reciprocal(red[:], red[:]))
    q.op("vector", lambda e: e.tensor_mul(gt[:], gt[:], red[:].broadcast_to([128, 16, 8])))
    for ti in range(16):
        q.op("tensor", lambda e, ti=ti: e.transpose(
            pst[0:8, :], gt[:, ti, :], identF[:]))
        q.op("vector", lambda e, ti=ti: e.tensor_copy(
            gsc[:, ti * 128:(ti + 1) * 128], pst[0:8, 0:128]))
    q.dma("sync", lambda e: e.dma_start(out=g_d[:], in_=gsc[:]))
    p.pop()
    return p.finish()



def build_G():
    p = Prog("G")
    q = p.q
    x3_d = p.din("x3G", [D, W])
    gown_d = p.din("gownG", [8, W])
    gall_d = p.din("gallG", [8, 8 * W])
    w1_d = p.din("moW1", [8, D, DI])
    w3_d = p.din("moW3", [8, D, DI])
    w2_d = p.din("moW2", [8, DI, D])
    wdqt_d = p.din("WDQT", [64, 512])
    wuqt_d = p.din("WUQT", [512, 64])
    wdkvt_d = p.din("WDKVT", [64, 1024])
    wukvt_d = p.din("WUKVT", [1024, 128])
    wqrt_d = p.din("WQRT", [64, 64])
    wkrt_d = p.din("WKRT", [64, 64])
    qnw_d = p.din("qnw", [64, 1])
    knw_d = p.din("knw", [64, 1])
    x35_d = p.dout("x35G", [D, W])
    qH_d = p.dout("qHG", [16, 128, W], BF16)
    kH_d = p.dout("kHG", [16, 128, W], BF16)
    vH_d = p.dout("vHG", [16, 64, W], BF16)
    wsel_dram = p.dtmp("G_wsel", [8, W], F32)

    psA = p.ps("G_psA", [128, 512], F32)
    psB = p.ps("G_psB", [128, 512], F32)
    ps1 = psB[0:1, :]

    ones1 = p.sb("G_ones1", [1, 128], BF16)
    q.op("gpsimd", lambda e: e.memset(ones1[:], 1.0))
    ones1f = p.sb("G_ones1f", [1, 128], F32)
    q.op("gpsimd", lambda e: e.memset(ones1f[:], 1.0))
    onescF = p.sb("G_onescF", [128, 1], F32)
    q.op("gpsimd", lambda e: e.memset(onescF[:], 1.0))
    ln = LNHelper(p, 512, "g", psA, ps1)

    p.push()
    gall = p.sb("G_gall", [8, 8 * W], F32)
    q.dma("sync", lambda e: e.dma_start(out=gall[:], in_=gall_d[:]))
    cmp_ = p.sb("G_cmp", [8, 8 * W], F32)
    sc = p.sb("G_sc", [8, 8], F32)
    q.op("gpsimd", lambda e: e.memset(sc[:, 0:1], 0.0))
    q.op("gpsimd", lambda e: e.memset(sc[:, 1:2], 1.1))
    for it in range(48):
        q.op("vector", lambda e: e.tensor_add(sc[:, 2:3], sc[:, 0:1], sc[:, 1:2]))
        q.op("vector", lambda e: e.tensor_scalar_mul(sc[:, 2:3], sc[:, 2:3], 0.5))
        q.op("vector", lambda e: e.tensor_scalar(
            cmp_[:], gall[:], sc[:, 2:3], None, op0=ALU.is_ge))
        q.op("vector", lambda e: e.reduce_sum(sc[:, 3:4], cmp_[:], axis=AX.X))
        q.op("vector", lambda e: e.tensor_single_scalar(
            sc[:, 4:5], sc[:, 3:4], 4096.0, op=ALU.is_ge))
        q.op("vector", lambda e: e.tensor_scalar(
            sc[:, 5:6], sc[:, 4:5], -1.0, 1.0, op0=ALU.mult, op1=ALU.add))
        q.op("vector", lambda e: e.tensor_sub(sc[:, 6:7], sc[:, 2:3], sc[:, 0:1]))
        q.op("vector", lambda e: e.tensor_sub(sc[:, 7:8], sc[:, 2:3], sc[:, 1:2]))
        q.op("vector", lambda e: e.scalar_tensor_tensor(
            sc[:, 0:1], sc[:, 6:7], sc[:, 4:5], sc[:, 0:1], op0=ALU.mult, op1=ALU.add))
        q.op("vector", lambda e: e.scalar_tensor_tensor(
            sc[:, 1:2], sc[:, 7:8], sc[:, 5:6], sc[:, 1:2], op0=ALU.mult, op1=ALU.add))
    gown = p.sb("G_gown", [8, W], F32)
    q.dma("sync", lambda e: e.dma_start(out=gown[:], in_=gown_d[:]))
    q.op("vector", lambda e: e.tensor_scalar(
        cmp_[:, 0:W], gown[:], sc[:, 0:1], None, op0=ALU.is_ge))
    q.op("vector", lambda e: e.tensor_mul(gown[:], gown[:], cmp_[:, 0:W]))
    q.dma("sync", lambda e: e.dma_start(out=wsel_dram[:], in_=gown[:]))
    wsel_done = q.cnt["sync"]
    p.pop()

    x35bf = p.sb("G_x35bf", [128, 8, W], BF16)
    p.push()
    TC = 512
    x3bf = p.sb("G_x3bf", [128, 8, TC], BF16)
    acc = p.sb("G_acc", [128, 8, TC], F32)
    hsb = p.sb("G_h", [128, 16, TC], BF16)
    wb = p.sb("G_wb", [128, TC], F32)
    scr = p.sb("G_scr", [128, 2, TC], F32)
    wrow1 = p.sb("G_wrow1", [1, TC], F32)
    wrow1b = p.sb("G_wrow1b", [1, TC], BF16)
    xres = p.sb("G_xres", [128, 8, TC], F32)
    w1sb = p.sb("G_w1", [128, 8, DI], BF16)
    w3sb = p.sb("G_w3", [128, 8, DI], BF16)
    w2sb = [p.sb(f"G_w2_{i}", [128, 16, 512], BF16) for i in range(2)]
    psD = p.ps("G_psD", [128, 512], F32)
    w1_rel = {}
    w3_rel = {}
    w2_rel = [{}, {}]
    for tc0 in range(0, W, TC):
        q.dma("sync", lambda e, tc0=tc0: e.dma_start(
            out=xres[:], in_=x3_d[:, tc0:tc0 + TC].rearrange("(kc p) m -> p kc m", p=128)))
        q.op("vector", lambda e: e.memset(acc[:], 0.0),
             waits={"tensor": q.cnt["tensor"], "scalar": q.cnt["scalar"],
                    "sync": q.cnt["sync"]})
        memset_mark = q.cnt["vector"]
        for ex in range(8):
            q.dma("sync", lambda e, ex=ex, tc0=tc0: e.dma_start(
                out=wrow1[:], in_=wsel_dram[ex:ex + 1, tc0:tc0 + TC]),
                waits={"vector": q.cnt["vector"], "sync": wsel_done})
            q.op("vector", lambda e: e.tensor_copy(wrow1b[:], wrow1[:]),
                 waits={"sync": q.cnt["sync"]})
            q.op("tensor", lambda e: e.matmul(
                psD[:, 0:TC], ones1[:], wrow1b[:], start=True, stop=True),
                waits={"vector": q.cnt["vector"]})
            q.op("vector", lambda e: e.tensor_copy(wb[:], psD[:, 0:TC]),
                 waits={"tensor": q.cnt["tensor"]})
            for kc in range(8):
                q.op("vector", lambda e, kc=kc: e.tensor_tensor(
                    x3bf[:, kc, :], xres[:, kc, :], wb[:], op=ALU.mult),
                    waits=({"tensor": q.cnt["tensor"]} if kc == 0 else {}))
            route_done = q.cnt["vector"]
            q.dma("gpsimd", lambda e, ex=ex: e.dma_start(
                out=w1sb[:], in_=w1_d[ex].rearrange("(kc p) m -> p kc m", p=128)),
                waits=dict(w1_rel))
            w1_loaded = q.cnt["gpsimd"]
            q.dma("gpsimd", lambda e, ex=ex: e.dma_start(
                out=w3sb[:], in_=w3_d[ex].rearrange("(kc p) m -> p kc m", p=128)),
                waits=dict(w3_rel))
            w3_loaded = q.cnt["gpsimd"]
            vmarks = {}
            smarks = {}
            for i in range(16):
                mc = i
                w_ = {"gpsimd": w1_loaded, "vector": route_done}
                if i >= 1:
                    w_["scalar"] = smarks[i - 1]
                def mm1(e, mc=mc):
                    for kc in range(8):
                        inst = e.matmul(psA[:, 0:TC], w1sb[:, kc, mc * 128:(mc + 1) * 128],
                                        x3bf[:, kc, :], start=(kc == 0), stop=(kc == 7))
                    return inst
                q.op("tensor", mm1, waits=w_)
                mm1_done = q.cnt["tensor"]
                w_ = {"gpsimd": w3_loaded}
                if i >= 1:
                    w_["vector"] = vmarks[i - 1]
                def mm3(e, mc=mc):
                    for kc in range(8):
                        inst = e.matmul(psB[:, 0:TC], w3sb[:, kc, mc * 128:(mc + 1) * 128],
                                        x3bf[:, kc, :], start=(kc == 0), stop=(kc == 7))
                    return inst
                q.op("tensor", mm3, waits=w_)
                sw = {"tensor": mm1_done}
                if i >= 2:
                    sw["vector"] = vmarks[i - 2]
                q.op("scalar", lambda e, mc=mc: e.activation(
                    scr[:, mc % 2, :], psA[:, 0:TC], AF.Silu), waits=sw)
                smarks[i] = q.cnt["scalar"]
                q.op("vector", lambda e, mc=mc: e.tensor_tensor(
                    hsb[:, mc, :], scr[:, mc % 2, :], psB[:, 0:TC], op=ALU.mult),
                    waits={"tensor": q.cnt["tensor"], "scalar": smarks[i]})
                vmarks[i] = q.cnt["vector"]
            h_done = q.cnt["vector"]
            w1_rel = {"tensor": q.cnt["tensor"]}
            w3_rel = {"tensor": q.cnt["tensor"]}
            eomarks = {}
            for i in range(8):
                dc = i
                hf = dc // 4
                if i < 2:
                    q.dma("gpsimd", lambda e, ex=ex, hf2=i: e.dma_start(
                        out=w2sb[hf2][:],
                        in_=w2_d[ex, :, hf2 * 512:(hf2 + 1) * 512].rearrange(
                            "(kc p) m -> p kc m", p=128)),
                        waits=dict(w2_rel[i]))
                    w2_rel[i] = {}
                    w2_rel[i]["loaded"] = q.cnt["gpsimd"]
                w_ = {"gpsimd": w2_rel[hf]["loaded"], "vector": h_done,
                      "scalar": smarks[15]}
                if i >= 2:
                    w_["vector"] = max(h_done, eomarks[i - 2])
                def mmo(e, dc=dc, hf=hf):
                    for kc in range(16):
                        inst = e.matmul(
                            (psA if dc % 2 == 0 else psB)[:, 0:TC],
                            w2sb[hf][:, kc, (dc % 4) * 128:(dc % 4 + 1) * 128],
                            hsb[:, kc, :], start=(kc == 0), stop=(kc == 15))
                    return inst
                q.op("tensor", mmo, waits=w_)
                q.op("vector", lambda e, dc=dc: e.tensor_add(
                    acc[:, dc, :], acc[:, dc, :], (psA if dc % 2 == 0 else psB)[:, 0:TC]),
                    waits={"tensor": q.cnt["tensor"], "vector": memset_mark})
                eomarks[i] = q.cnt["vector"]
            w2_rel = [{"tensor": q.cnt["tensor"], "loaded": w2_rel[0]["loaded"]},
                      {"tensor": q.cnt["tensor"], "loaded": w2_rel[1]["loaded"]}]
            for d_ in w2_rel:
                d_.pop("loaded", None)
        for kc in range(8):
            q.op("vector", lambda e, kc=kc: e.tensor_add(
                acc[:, kc, :], acc[:, kc, :], xres[:, kc, :]))
        ln.emit(acc, acc, 8)
        q.dma("sync", lambda e, tc0=tc0: e.dma_start(
            out=x35_d[:, tc0:tc0 + TC].rearrange("(kc p) m -> p kc m", p=128),
            in_=acc[:]))
        for kc in range(8):
            q.op("vector", lambda e, kc=kc, tc0=tc0: e.tensor_copy(
                x35bf[:, kc, tc0:tc0 + TC], acc[:, kc, :]))
    p.pop()

    p.push()
    EPS3 = 1.1920929e-07
    p.const(EPS3)
    wdqt = p.sb("G_wdqt", [64, 512], BF16)
    q.dma("gpsimd", lambda e: e.dma_start(out=wdqt[:], in_=wdqt_d[:]))
    wuqt = p.sb("G_wuqt", [128, 4, 64], BF16)
    q.dma("gpsimd", lambda e: e.dma_start(
        out=wuqt[:], in_=wuqt_d.rearrange("(kc p) m -> p kc m", p=128)))
    wdkvt = p.sb("G_wdkvt", [64, 1024], BF16)
    q.dma("gpsimd", lambda e: e.dma_start(out=wdkvt[:], in_=wdkvt_d[:]))
    wukvt = p.sb("G_wukvt", [128, 8, 128], BF16)
    q.dma("gpsimd", lambda e: e.dma_start(
        out=wukvt[:], in_=wukvt_d.rearrange("(kc p) m -> p kc m", p=128)))
    wqrt = p.sb("G_wqrt", [64, 64], BF16)
    q.dma("gpsimd", lambda e: e.dma_start(out=wqrt[:], in_=wqrt_d[:]))
    wkrt = p.sb("G_wkrt", [64, 64], BF16)
    q.dma("gpsimd", lambda e: e.dma_start(out=wkrt[:], in_=wkrt_d[:]))
    qnw = p.sb("G_qnw", [64, 1], F32)
    q.dma("sync", lambda e: e.dma_start(out=qnw[:], in_=qnw_d[:]))
    knw = p.sb("G_knw", [64, 1], F32)
    q.dma("sync", lambda e: e.dma_start(out=knw[:], in_=knw_d[:]))

    xh0 = p.sb("G_xh0", [64, W], BF16)
    cq = p.sb("G_cq", [128, 4, W], BF16)
    ckv = p.sb("G_ckv", [128, 8, W], BF16)
    qc = p.sb("G_qc", [64, W], F32)
    kvf = p.sb("G_kvf", [128, W], F32)
    sq64 = p.sb("G_sq64", [64, W], F32)
    row = p.sb("G_row", [1, W], F32)
    rb64 = p.sb("G_rb64", [64, W], F32)
    outb = p.sb("G_outb", [64, W], BF16)

    def rmsnorm64(src_ap, w_col, dst_bf):
        q.op("scalar", lambda e: e.activation(sq64[:], src_ap, AF.Square))
        for it in range(0, W, 512):
            q.op("tensor", lambda e, it=it: e.matmul(
                ps1[:, 0:512], onescF[0:64, :], sq64[:, it:it + 512],
                start=True, stop=True))
            q.op("vector", lambda e, it=it: e.tensor_scalar_mul(
                row[:, it:it + 512], ps1[:, 0:512], 1.0 / 64))
        q.op("scalar", lambda e: e.activation(row[:], row[:], AF.Ln, bias=EPS3))
        q.op("scalar", lambda e: e.activation(row[:], row[:], AF.Exp, scale=-0.5))
        for it in range(0, W, 512):
            q.op("tensor", lambda e, it=it: e.matmul(
                psB[0:64, 0:512], ones1f[0:1, 0:64], row[:, it:it + 512],
                start=True, stop=True))
            q.op("vector", lambda e, it=it: e.tensor_copy(
                rb64[:, it:it + 512], psB[0:64, 0:512]))
        q.op("vector", lambda e: e.tensor_mul(rb64[:], src_ap, rb64[:]))
        q.op("vector", lambda e: e.tensor_scalar_mul(rb64[:], rb64[:], w_col))
        q.op("vector", lambda e: e.tensor_copy(dst_bf, rb64[:]))

    def pgroup(units):
        entry = q.snap()
        marks = {}
        for i, (mmf, dstf) in enumerate(units):
            bank = psA if i % 2 == 0 else psB
            tw = dict(entry)
            if i >= 2:
                tw["vector"] = max(tw.get("vector", 0), marks[i - 2])
            q.op("tensor", lambda e, mmf=mmf, bank=bank: mmf(e, bank), waits=tw)
            q.op("vector", lambda e, dstf=dstf, bank=bank: dstf(e, bank),
                 waits={"tensor": q.cnt["tensor"]})
            marks[i] = q.cnt["vector"]

    for h in range(16):
        hc, hb = h // 2, (h % 2) * 64
        q.op("vector", lambda e, hc=hc, hb=hb: e.tensor_copy(
            xh0[:], x35bf[hb:hb + 64, hc, :]))
        pgroup([
            (lambda e, bank, mc=mc, it=it: e.matmul(
                bank[:, 0:512], wdqt[:, mc * 128:(mc + 1) * 128],
                xh0[:, it:it + 512], start=True, stop=True),
             lambda e, bank, mc=mc, it=it: e.tensor_copy(
                cq[:, mc, it:it + 512], bank[:, 0:512]))
            for mc in range(4) for it in range(0, W, 512)])

        def mmq_u(e, bank, it):
            for kc in range(4):
                inst = e.matmul(bank[0:64, 0:512], wuqt[:, kc, :],
                                cq[:, kc, it:it + 512],
                                start=(kc == 0), stop=(kc == 3))
            return inst
        pgroup([
            (lambda e, bank, it=it: mmq_u(e, bank, it),
             lambda e, bank, it=it: e.tensor_copy(
                qc[:, it:it + 512], bank[0:64, 0:512]))
            for it in range(0, W, 512)])
        rmsnorm64(qc[:], qnw[:, 0:1], outb[:])
        q.dma("sync", lambda e, h=h: e.dma_start(out=qH_d[h, 0:64, :], in_=outb[:]))
        pgroup([
            (lambda e, bank, it=it: e.matmul(
                bank[0:64, 0:512], wqrt[:], outb[:, it:it + 512],
                start=True, stop=True),
             lambda e, bank, it=it: e.tensor_copy(
                sq64[:, it:it + 512], bank[0:64, 0:512]))
            for it in range(0, W, 512)])
        q.op("vector", lambda e: e.tensor_copy(outb[:], sq64[:]))
        q.dma("sync", lambda e, h=h: e.dma_start(out=qH_d[h, 64:128, :], in_=outb[:]))
        pgroup([
            (lambda e, bank, mc=mc, it=it: e.matmul(
                bank[:, 0:512], wdkvt[:, mc * 128:(mc + 1) * 128],
                xh0[:, it:it + 512], start=True, stop=True),
             lambda e, bank, mc=mc, it=it: e.tensor_copy(
                ckv[:, mc, it:it + 512], bank[:, 0:512]))
            for mc in range(8) for it in range(0, W, 512)])

        def mmkv_u(e, bank, it):
            for kc in range(8):
                inst = e.matmul(bank[:, 0:512], wukvt[:, kc, :],
                                ckv[:, kc, it:it + 512],
                                start=(kc == 0), stop=(kc == 7))
            return inst
        pgroup([
            (lambda e, bank, it=it: mmkv_u(e, bank, it),
             lambda e, bank, it=it: e.tensor_copy(
                kvf[:, it:it + 512], bank[:, 0:512]))
            for it in range(0, W, 512)])
        rmsnorm64(kvf[0:64, :], knw[:, 0:1], outb[:])
        q.dma("sync", lambda e, h=h: e.dma_start(out=kH_d[h, 0:64, :], in_=outb[:]))
        q.op("vector", lambda e: e.tensor_copy(outb[:], kvf[64:128, :]))
        q.dma("sync", lambda e, h=h: e.dma_start(out=vH_d[h, :, :], in_=outb[:]))
        pgroup([
            (lambda e, bank, it=it: e.matmul(
                bank[0:64, 0:512], wkrt[:], xh0[:, it:it + 512],
                start=True, stop=True),
             lambda e, bank, it=it: e.tensor_copy(
                sq64[:, it:it + 512], bank[0:64, 0:512]))
            for it in range(0, W, 512)])
        q.op("vector", lambda e: e.tensor_copy(outb[:], sq64[:]))
        q.dma("sync", lambda e, h=h: e.dma_start(out=kH_d[h, 64:128, :], in_=outb[:]))
    p.pop()
    return p.finish()



def build_H1():
    p = Prog("H1")
    q = p.q
    q_d = p.din("qH1", [4, 128, T], BF16)
    k_d = p.din("kH1", [4, 128, T], BF16)
    v_d = p.din("vH1", [4, 64, T], BF16)
    ctx_d = p.dout("ctxH1", [4, 64, T], BF16)

    psS = p.ps("H1_psS", [128, 512], F32)
    psC = p.ps("H1_psC", [128, 512], F32)
    psB_ = p.ps("H1_psB", [128, 512], F32)
    psD1 = p.ps("H1_psD", [128, 512], F32)
    psE = p.ps("H1_psE", [128, 512], F32)
    psF = p.ps("H1_psF", [128, 512], F32)
    ps1 = psB_[0:1, :]
    psT = p.ps("H1_psT", [128, 128], BF16)

    ident = p.sb("H1_ident", [128, 128], BF16)
    q.op("gpsimd", lambda e: e.memset(ident[:], 1.0))
    q.op("gpsimd", lambda e: e.affine_select(
        ident[:], ident[:], pattern=[[1, 128]], compare_op=ALU.is_equal,
        fill=0.0, base=0, channel_multiplier=-1))
    ones1 = p.sb("H1_ones1", [1, 128], F32)
    q.op("gpsimd", lambda e: e.memset(ones1[:], 1.0))
    onescF = p.sb("H1_onescF", [128, 1], F32)
    q.op("gpsimd", lambda e: e.memset(onescF[:], 1.0))
    m4 = p.sb("H1_m4", [128, 4, 512], F32)
    for dlt in range(4):
        q.op("gpsimd", lambda e, dlt=dlt: e.iota(
            m4[:, dlt, :], pattern=[[1, 512]], base=-128 * dlt,
            channel_multiplier=-1, allow_small_or_imprecise_dtypes=True))
    q.op("vector", lambda e: e.tensor_single_scalar(m4[:], m4[:], 0.0, op=ALU.is_ge))
    q.op("vector", lambda e: e.tensor_scalar(
        m4[:], m4[:], 8e9, -8e9, op0=ALU.mult, op1=ALU.add))

    qsb = p.sb("H1_q", [128, T], BF16)
    ksb = p.sb("H1_k", [128, T], BF16)
    vsb = p.sb("H1_v", [64, T], BF16)
    vaug = p.sb("H1_vaug", [128, 64, 65], BF16)
    ctxout = p.sb("H1_ctx", [64, T], BF16)
    sm = p.sb("H1_sm", [128, 512], F32)
    pt = p.sb("H1_pt", [128, 512], BF16)
    smB = p.sb("H1_smB", [128, 512], F32)
    ptB = p.sb("H1_ptB", [128, 512], BF16)
    sm4 = p.sb("H1_sm4", [128, 4, 512], F32)
    pt4 = p.sb("H1_pt4", [128, 4, 512], BF16)
    ctx = p.sb("H1_ctxf", [65, 512], F32)
    rb = p.sb("H1_rb", [64, 512], F32)
    rrow = p.sb("H1_rrow", [1, 512], F32)
    negb = p.sb("H1_negb", [128, 1], F32)
    nrm = p.sb("H1_nrm", [1, 8], F32)
    sqs = p.sb("H1_sqs", [128, 512], F32)

    for pr in range(4):
        q.dma("sync", lambda e, pr=pr: e.dma_start(out=qsb[:], in_=q_d[pr]))
        q.dma("sync", lambda e, pr=pr: e.dma_start(out=ksb[:], in_=k_d[pr]))
        q.dma("sync", lambda e, pr=pr: e.dma_start(out=vsb[:], in_=v_d[pr]))
        q.op("gpsimd", lambda e: e.memset(vaug[:], 0.0))
        for kt in range(64):
            q.op("tensor", lambda e, kt=kt: e.transpose(
                psT[:, 0:64], vsb[:, kt * 128:(kt + 1) * 128], ident[0:64, 0:64]))
            q.op("vector", lambda e, kt=kt: e.tensor_copy(
                vaug[:, kt, 0:64], psT[:, 0:64]))
        q.op("gpsimd", lambda e: e.memset(vaug[:, :, 64:65], 1.0))
        for (src_, col) in ((qsb, 0), (ksb, 1)):
            for i, t0 in enumerate(range(0, T, 512)):
                q.op("scalar", lambda e, src_=src_, t0=t0: e.activation(
                    sqs[:], src_[:, t0:t0 + 512], AF.Square))
                q.op("tensor", lambda e: e.matmul(
                    ps1[:, 0:512], onescF[:], sqs[:], start=True, stop=True))
                tgt = col if i == 0 else 2
                q.op("vector", lambda e, tgt=tgt: e.reduce_max(
                    nrm[:, tgt:tgt + 1], ps1[:, 0:512], axis=AX.X))
                if i > 0:
                    q.op("vector", lambda e, col=col: e.tensor_max(
                        nrm[:, col:col + 1], nrm[:, col:col + 1], nrm[:, 2:3]))
        q.op("scalar", lambda e: e.activation(nrm[:, 0:1], nrm[:, 0:1], AF.Sqrt))
        q.op("scalar", lambda e: e.activation(nrm[:, 1:2], nrm[:, 1:2], AF.Sqrt))
        q.op("vector", lambda e: e.tensor_mul(nrm[:, 3:4], nrm[:, 0:1], nrm[:, 1:2]))
        q.op("vector", lambda e: e.tensor_scalar_mul(nrm[:, 3:4], nrm[:, 3:4], -0.125))
        q.op("tensor", lambda e: e.matmul(
            psB_[:, 0:1], ones1[:], nrm[:, 3:4], start=True, stop=True))
        q.op("vector", lambda e: e.tensor_copy(negb[:], psB_[:, 0:1]))
        entry = q.snap()
        for qt in range(16):
            qt0 = qt * 512
            nkt = 4 * (qt + 1)
            freemk = {}
            expp = {}
            pvp = {}
            for j in range(nkt // 2):
                kts = (2 * j, 2 * j + 1)
                BP = (psS, psD1) if j % 2 == 0 else (psE, psF)
                sl = (0, 1) if j % 2 == 0 else (2, 3)
                tw = dict(entry)
                if j >= 2:
                    for eng_r, cnt_r in freemk[j - 2].items():
                        tw[eng_r] = max(tw.get(eng_r, 0), cnt_r)

                def mmqk(e, kts=kts, qt0=qt0, BP=BP):
                    for m in range(2):
                        inst = e.matmul(
                            BP[m][:, 0:512],
                            ksb[:, kts[m] * 128:(kts[m] + 1) * 128],
                            qsb[:, qt0:qt0 + 512], start=True, stop=True)
                    return inst
                q.op("tensor", mmqk, waits=tw)
                qk_cnt = q.cnt["tensor"]
                dlts = [kts[m] - 4 * qt for m in range(2)]
                diag = [m for m in range(2) if dlts[m] >= 0]
                vcnt = None
                if diag:
                    vw = {"tensor": qk_cnt}
                    if j >= 2:
                        vw["scalar"] = expp[j - 2]

                    def mmadd(e, diag=diag, dlts=dlts, BP=BP, sl=sl):
                        for m in diag:
                            inst = e.tensor_add(
                                sm4[:, sl[m], :], BP[m][:, 0:512], m4[:, dlts[m], :])
                        return inst
                    q.op("vector", mmadd, waits=vw)
                    vcnt = q.cnt["vector"]
                ew = {"tensor": qk_cnt}
                if j >= 2:
                    ew["tensor"] = max(ew["tensor"], pvp[j - 2])
                if vcnt is not None:
                    ew["vector"] = vcnt

                def mmexp(e, diag=diag, BP=BP, sl=sl):
                    for m in range(2):
                        src = sm4[:, sl[m], :] if m in diag else BP[m][:, 0:512]
                        inst = e.activation(
                            pt4[:, sl[m], :], src, AF.Exp,
                            bias=negb[:, 0:1], scale=0.125)
                    return inst
                q.op("scalar", mmexp, waits=ew)
                expp[j] = q.cnt["scalar"]
                freemk[j] = {"scalar": expp[j]}
                if vcnt is not None:
                    freemk[j]["vector"] = vcnt

                def mmpv(e, kts=kts, nkt=nkt, sl=sl):
                    for m in range(2):
                        inst = e.matmul(
                            psC[0:65, 0:512], vaug[:, kts[m], :], pt4[:, sl[m], :],
                            start=(kts[m] == 0), stop=(kts[m] == nkt - 1))
                    return inst
                q.op("tensor", mmpv, waits={"scalar": expp[j]})
                pvp[j] = q.cnt["tensor"]
            q.op("vector", lambda e: e.tensor_copy(ctx[:], psC[0:65, 0:512]),
                 waits={"tensor": q.cnt["tensor"]})
            q.op("vector", lambda e: e.reciprocal(rrow[:], ctx[64:65, :]), waits={})
            q.op("tensor", lambda e: e.matmul(
                psB_[0:64, 0:512], ones1[0:1, 0:64], rrow[:],
                start=True, stop=True), waits={"vector": q.cnt["vector"]})
            q.op("vector", lambda e: e.tensor_copy(rb[:], psB_[0:64, 0:512]),
                 waits={"tensor": q.cnt["tensor"]})
            q.op("vector", lambda e: e.tensor_mul(ctx[0:64, :], ctx[0:64, :], rb[:]),
                 waits={})
            q.op("vector", lambda e, qt0=qt0: e.tensor_copy(
                ctxout[:, qt0:qt0 + 512], ctx[0:64, :]), waits={})
            entry = q.snap()
        q.dma("sync", lambda e, pr=pr: e.dma_start(out=ctx_d[pr], in_=ctxout[:]))
    return p.finish()



def build_H2():
    p = Prog("H2")
    q = p.q
    x35_d = p.din("x35H", [D, W])
    ctx_d = p.din("ctxH2", [D, W], BF16)
    wot_d = p.din("WOT", [64, 64])
    rwt_d = p.din("routerW", [2, D, 6])
    rb_d = p.din("routerB", [2, 6])
    gb_d = p.din("gateB", [2, 6])
    wup_d = p.din("expWup", [6, D, D])
    bup_d = p.din("expBup", [128, 6, 8])
    wgg_d = p.din("expWg", [6, D, D])
    bgg_d = p.din("expBg", [128, 6, 8])
    wdn_d = p.din("expWdown", [6, D, D])
    bdn_d = p.din("expBdown", [128, 6, 8])
    swup_d = p.din("shWup", [D, 2 * D])
    sbup_d = p.din("shBup", [128, 16])
    swg_d = p.din("shWg", [D, D])
    sbg_d = p.din("shBg", [128, 8])
    swdn_d = p.din("shWdown", [D, D])
    sbdn_d = p.din("shBdown", [128, 8])
    nw_d = p.din("normW", [128, 8, 2])
    out_d = p.dout("outH2", [D, W])
    x4_dram = p.dtmp("H2_x4", [D, W], F32)
    xf_dram = p.dtmp("H2_xf", [D, W], F32)
    ws_dram = p.dtmp("H2_ws", [6, W], F32)

    psA = p.ps("H2_psA", [128, 512], F32)
    psB = p.ps("H2_psB", [128, 512], F32)
    psD = p.ps("H2_psD", [128, 512], F32)
    ps1 = psD[0:1, :]
    ps6 = psD[0:6, :]
    pst = p.ps("H2_pst", [128, 128], F32)
    ws_done = {}

    ones1 = p.sb("H2_ones1", [1, 128], BF16)
    q.op("gpsimd", lambda e: e.memset(ones1[:], 1.0))
    ones1f = p.sb("H2_ones1f", [1, 128], F32)
    q.op("gpsimd", lambda e: e.memset(ones1f[:], 1.0))
    onescF = p.sb("H2_onescF", [128, 1], F32)
    q.op("gpsimd", lambda e: e.memset(onescF[:], 1.0))
    identF = p.sb("H2_identF", [128, 128], F32)
    q.op("gpsimd", lambda e: e.memset(identF[:], 1.0))
    q.op("gpsimd", lambda e: e.affine_select(
        identF[:], identF[:], pattern=[[1, 128]], compare_op=ALU.is_equal,
        fill=0.0, base=0, channel_multiplier=-1))
    ln = LNHelper(p, 512, "h2", psA, ps1)
    p.const(1e-6)

    p.push()
    wot = p.sb("H2_wot", [64, 64], BF16)
    q.dma("gpsimd", lambda e: e.dma_start(out=wot[:], in_=wot_d[:]))
    ctxb = p.sb("H2_ctxb", [128, 8, W], BF16)
    q.dma("sync", lambda e: e.dma_start(
        out=ctxb[:], in_=ctx_d.rearrange("(kc p) m -> p kc m", p=128)))
    ctx0 = p.sb("H2_ctx0", [64, W], BF16)
    x4t = p.sb("H2_x4t", [128, 8, 512], F32)
    att = p.sb("H2_att", [128, 8, W], F32)
    for h in range(16):
        hc, hb = h // 2, (h % 2) * 64
        q.op("vector", lambda e, hc=hc, hb=hb: e.tensor_copy(
            ctx0[:], ctxb[hb:hb + 64, hc, :]))
        for it in range(0, W, 512):
            q.op("tensor", lambda e, it=it: e.matmul(
                psA[0:64, 0:512], wot[:], ctx0[:, it:it + 512],
                start=True, stop=True))
            q.op("vector", lambda e, it=it, hc=hc, hb=hb: e.tensor_copy(
                att[hb:hb + 64, hc, it:it + 512], psA[0:64, 0:512]))
    for (t0, w) in [(i * 512, 512) for i in range(4)]:
        q.dma("sync", lambda e, t0=t0: e.dma_start(
            out=x4t[:], in_=x35_d[:, t0:t0 + 512].rearrange("(kc p) m -> p kc m", p=128)))
        for kc in range(8):
            q.op("vector", lambda e, kc=kc, t0=t0: e.tensor_add(
                x4t[:, kc, :], x4t[:, kc, :], att[:, kc, t0:t0 + 512]))
        ln.emit(x4t, x4t, 8)
        q.dma("sync", lambda e, t0=t0: e.dma_start(
            out=x4_dram[:, t0:t0 + 512].rearrange("(kc p) m -> p kc m", p=128),
            in_=x4t[:]))
        q.dma("sync", lambda e, t0=t0: e.dma_start(
            out=xf_dram[:, t0:t0 + 512].rearrange("(kc p) m -> p kc m", p=128),
            in_=x4t[:]))
    p.pop()

    rbv = p.sb("H2_rbv", [6, 2], F32)
    q.dma("sync", lambda e: e.dma_start(out=rbv[:], in_=rb_d.rearrange("c e -> e c"),
                                        allow_slow_non_contiguous=True))
    gbv = p.sb("H2_gbv", [6, 2], F32)
    q.dma("sync", lambda e: e.dma_start(out=gbv[:], in_=gb_d.rearrange("c e -> e c"),
                                        allow_slow_non_contiguous=True))
    bup = p.sb("H2_bup", [128, 6, 8], F32)
    q.dma("sync", lambda e: e.dma_start(out=bup[:], in_=bup_d[:]))
    bgg = p.sb("H2_bgg", [128, 6, 8], F32)
    q.dma("sync", lambda e: e.dma_start(out=bgg[:], in_=bgg_d[:]))
    bdn = p.sb("H2_bdn", [128, 6, 8], F32)
    q.dma("sync", lambda e: e.dma_start(out=bdn[:], in_=bdn_d[:]))
    sbup = p.sb("H2_sbup", [128, 16], F32)
    q.dma("sync", lambda e: e.dma_start(out=sbup[:], in_=sbup_d[:]))
    sbg = p.sb("H2_sbg", [128, 8], F32)
    q.dma("sync", lambda e: e.dma_start(out=sbg[:], in_=sbg_d[:]))
    sbdn = p.sb("H2_sbdn", [128, 8], F32)
    q.dma("sync", lambda e: e.dma_start(out=sbdn[:], in_=sbdn_d[:]))
    nw = p.sb("H2_nw", [128, 8, 2], F32)
    q.dma("sync", lambda e: e.dma_start(out=nw[:], in_=nw_d[:]))

    TC = 512
    for cyc in range(2):
        p.push()
        xfbr = p.sb(f"H2_xfbr_{cyc}", [128, 8, W], BF16)
        q.dma("gpsimd", lambda e, xfbr=xfbr: e.dma_start(
            out=xfbr[:], in_=xf_dram.rearrange("(kc p) m -> p kc m", p=128)))
        rwt = p.sb(f"H2_rwt_{cyc}", [128, 8, 6], BF16)
        q.dma("gpsimd", lambda e, cyc=cyc: e.dma_start(
            out=rwt[:], in_=rwt_d[cyc].rearrange("(kc p) m -> p kc m", p=128),
            allow_slow_non_contiguous=True))
        gate = p.sb(f"H2_gate_{cyc}", [6, W], F32)
        for it in range(0, W, 512):
            def mmr(e, it=it, rwt=rwt, xfbr=xfbr):
                for kc in range(8):
                    inst = e.matmul(ps6[:, 0:512], rwt[:, kc, :],
                                    xfbr[:, kc, it:it + 512], start=(kc == 0), stop=(kc == 7))
                return inst
            q.op("tensor", mmr)
            q.op("scalar", lambda e, it=it, cyc=cyc: e.activation(
                gate[:, it:it + 512], ps6[:, 0:512], AF.Identity,
                bias=rbv[:, cyc:cyc + 1]))
        gt = p.sb(f"H2_gt_{cyc}", [128, 16, 6], F32)
        bt = p.sb(f"H2_bt_{cyc}", [128, 16, 6], F32)
        red = p.sb(f"H2_red_{cyc}", [128, 16, 1], F32)
        red2 = p.sb(f"H2_red2_{cyc}", [128, 16, 1], F32)
        eqm = p.sb(f"H2_eqm_{cyc}", [128, 16, 6], F32)
        for ti in range(16):
            q.op("tensor", lambda e, ti=ti: e.transpose(
                pst[:, 0:6], gate[:, ti * 128:(ti + 1) * 128], identF[0:6, 0:6]))
            q.op("vector", lambda e, ti=ti: e.tensor_copy(gt[:, ti, :], pst[:, 0:6]))
        q.op("vector", lambda e, cyc=cyc: e.tensor_scalar_add(
            gate[:], gate[:], gbv[:, cyc:cyc + 1]))
        for ti in range(16):
            q.op("tensor", lambda e, ti=ti: e.transpose(
                pst[:, 0:6], gate[:, ti * 128:(ti + 1) * 128], identF[0:6, 0:6]))
            q.op("vector", lambda e, ti=ti: e.tensor_copy(bt[:, ti, :], pst[:, 0:6]))
        q.op("vector", lambda e: e.tensor_reduce(red[:], bt[:], axis=AX.X, op=ALU.max))
        q.op("vector", lambda e: e.tensor_tensor(
            eqm[:], bt[:], red[:].broadcast_to([128, 16, 6]), op=ALU.is_ge))
        q.op("vector", lambda e: e.scalar_tensor_tensor(
            eqm[:], eqm[:], -1e30, bt[:], op0=ALU.mult, op1=ALU.add))
        q.op("vector", lambda e: e.tensor_reduce(red2[:], eqm[:], axis=AX.X, op=ALU.max))
        q.op("vector", lambda e: e.tensor_tensor(
            eqm[:], bt[:], red2[:].broadcast_to([128, 16, 6]), op=ALU.is_ge))
        q.op("vector", lambda e: e.tensor_mul(gt[:], gt[:], eqm[:]))
        q.op("vector", lambda e: e.tensor_scalar(
            eqm[:], eqm[:], 1e30, -1e30, op0=ALU.mult, op1=ALU.add))
        q.op("vector", lambda e: e.tensor_add(gt[:], gt[:], eqm[:]))
        q.op("vector", lambda e: e.tensor_reduce(red[:], gt[:], axis=AX.X, op=ALU.max))
        q.op("vector", lambda e: e.tensor_sub(gt[:], gt[:], red[:].broadcast_to([128, 16, 6])))
        q.op("scalar", lambda e: e.activation(gt[:], gt[:], AF.Exp))
        q.op("vector", lambda e: e.tensor_reduce(red[:], gt[:], axis=AX.X, op=ALU.add))
        q.op("vector", lambda e: e.reciprocal(red[:], red[:]))
        q.op("vector", lambda e: e.tensor_mul(gt[:], gt[:], red[:].broadcast_to([128, 16, 6])))
        for ti in range(16):
            q.op("tensor", lambda e, ti=ti: e.transpose(
                pst[0:6, :], gt[:, ti, :], identF[:]))
            q.op("vector", lambda e, ti=ti: e.tensor_copy(
                gate[:, ti * 128:(ti + 1) * 128], pst[0:6, 0:128]))
        q.dma("sync", lambda e: e.dma_start(out=ws_dram[:], in_=gate[:]))
        ws_done[cyc] = q.cnt["sync"]
        p.pop()

        p.push()
        TCH = 512
        xfb = p.sb(f"H2_xfb2_{cyc}", [128, 8, TCH], BF16)
        xfres = p.sb(f"H2_xfres_{cyc}", [128, 8, TCH], F32)
        acc = p.sb(f"H2_acc_{cyc}", [128, 8, TCH], F32)
        h1 = p.sb(f"H2_h1_{cyc}", [128, 8, TCH], BF16)
        h2t = p.sb(f"H2_h2t_{cyc}", [128, 8, TCH], BF16)
        wb = p.sb(f"H2_wb_{cyc}", [128, TCH], F32)
        scr = p.sb(f"H2_scr_{cyc}", [128, 2, TCH], F32)
        wrow1 = p.sb(f"H2_wrow1_{cyc}", [1, TCH], F32)
        wrow1b = p.sb(f"H2_wrow1b_{cyc}", [1, TCH], BF16)
        sgrp = p.sb(f"H2_sgrp_{cyc}", [128, 16, TCH], BF16)
        sg2 = p.sb(f"H2_sg2_{cyc}", [128, 16, TCH], BF16)
        poolA = p.sb(f"H2_poolA_{cyc}", [128, 8, 2048], BF16)
        poolB = p.sb(f"H2_poolB_{cyc}", [128, 8, 2048], BF16)
        relA = {}
        relB = {}
        for tc0 in range(0, W, TCH):
            q.dma("gpsimd", lambda e, tc0=tc0: e.dma_start(
                out=xfb[:], in_=xf_dram[:, tc0:tc0 + TCH].rearrange("(kc p) m -> p kc m", p=128)))
            q.dma("sync", lambda e, tc0=tc0: e.dma_start(
                out=xfres[:], in_=xf_dram[:, tc0:tc0 + TCH].rearrange("(kc p) m -> p kc m", p=128)))
            q.op("vector", lambda e: e.memset(acc[:], 0.0))
            entry = q.snap()
            accbase = q.cnt["vector"]

            def units8(wview, xview, evacs, loaded, entry, extra=None):
                marks = {}
                for i in range(8):
                    bank = psA if i % 2 == 0 else psB
                    tw = dict(entry)
                    tw["gpsimd"] = max(tw.get("gpsimd", 0), loaded)
                    if extra:
                        for k2, v2 in extra.items():
                            tw[k2] = max(tw.get(k2, 0), v2)
                    if i >= 2:
                        eng_r, cnt_r = marks[i - 2]
                        tw[eng_r] = max(tw.get(eng_r, 0), cnt_r)

                    def mmx(e, i=i, bank=bank, wview=wview, xview=xview):
                        for kc in range(8):
                            inst = e.matmul(bank[:, 0:TCH],
                                            wview(kc, i), xview(kc),
                                            start=(kc == 0), stop=(kc == 7))
                        return inst
                    q.op("tensor", mmx, waits=tw)
                    marks[i] = evacs(i, bank, {"tensor": q.cnt["tensor"]})
                return marks

            for ex in range(6):
                q.dma("gpsimd", lambda e, ex=ex: e.dma_start(
                    out=poolA[:, :, 0:1024],
                    in_=wup_d[ex].rearrange("(kc p) m -> p kc m", p=128)),
                    waits=dict(relA))
                ldA = q.cnt["gpsimd"]

                def ev1(i, bank, tw, ex=ex):
                    q.op("scalar", lambda e, i=i, bank=bank: e.activation(
                        h1[:, i, :], bank[:, 0:TCH], AF.Identity,
                        bias=bup[:, ex, i:i + 1]), waits=tw)
                    return ("scalar", q.cnt["scalar"])
                units8(lambda kc, i: poolA[:, kc, i * 128:(i + 1) * 128],
                       lambda kc: xfb[:, kc, :], ev1, ldA, entry)
                relA = {"tensor": q.cnt["tensor"]}
                h1_done = q.cnt["scalar"]
                q.dma("gpsimd", lambda e, ex=ex: e.dma_start(
                    out=poolB[:, :, 0:1024],
                    in_=wgg_d[ex].rearrange("(kc p) m -> p kc m", p=128)),
                    waits=dict(relB))
                ldB = q.cnt["gpsimd"]

                def ev2(i, bank, tw, ex=ex):
                    q.op("scalar", lambda e, i=i, bank=bank: e.activation(
                        scr[:, i % 2, :], bank[:, 0:TCH], AF.Silu,
                        bias=bgg[:, ex, i:i + 1]), waits=tw)
                    q.op("vector", lambda e, i=i: e.tensor_tensor(
                        h2t[:, i, :], scr[:, i % 2, :], h1[:, i, :], op=ALU.mult),
                        waits={"scalar": q.cnt["scalar"]})
                    return ("vector", q.cnt["vector"])
                units8(lambda kc, i: poolB[:, kc, i * 128:(i + 1) * 128],
                       lambda kc: h1[:, kc, :], ev2, ldB, entry,
                       extra={"scalar": h1_done})
                relB = {"tensor": q.cnt["tensor"]}
                h2_done = q.cnt["vector"]
                q.dma("sync", lambda e, ex=ex, tc0=tc0: e.dma_start(
                    out=wrow1[:], in_=ws_dram[ex:ex + 1, tc0:tc0 + TCH]),
                    waits={"vector": q.cnt["vector"], "sync": ws_done[cyc]})
                q.op("vector", lambda e: e.tensor_copy(wrow1b[:], wrow1[:]),
                     waits={"sync": q.cnt["sync"]})
                q.op("tensor", lambda e: e.matmul(
                    psD[:, 0:TCH], ones1[:], wrow1b[:], start=True, stop=True),
                    waits={"vector": q.cnt["vector"]})
                q.op("vector", lambda e: e.tensor_copy(wb[:], psD[:, 0:TCH]),
                     waits={"tensor": q.cnt["tensor"]})
                wb_done = q.cnt["vector"]
                q.dma("gpsimd", lambda e, ex=ex: e.dma_start(
                    out=poolA[:, :, 0:1024],
                    in_=wdn_d[ex].rearrange("(kc p) m -> p kc m", p=128)),
                    waits=dict(relA))
                ldA = q.cnt["gpsimd"]

                def ev3(i, bank, tw, ex=ex, wb_done=wb_done, accbase=accbase):
                    q.op("scalar", lambda e, i=i, bank=bank: e.activation(
                        scr[:, i % 2, :], bank[:, 0:TCH], AF.Identity,
                        bias=bdn[:, ex, i:i + 1]), waits=tw)
                    sm_ = q.cnt["scalar"]
                    q.op("vector", lambda e, i=i: e.tensor_mul(
                        scr[:, i % 2, :], scr[:, i % 2, :], wb[:]),
                        waits={"scalar": sm_, "vector": max(wb_done, accbase)})
                    q.op("vector", lambda e, i=i: e.tensor_add(
                        acc[:, i, :], acc[:, i, :], scr[:, i % 2, :]), waits={})
                    return ("scalar", sm_)
                units8(lambda kc, i: poolA[:, kc, i * 128:(i + 1) * 128],
                       lambda kc: h2t[:, kc, :], ev3, ldA, entry,
                       extra={"vector": h2_done})
                relA = {"tensor": q.cnt["tensor"]}
            q.dma("gpsimd", lambda e: e.dma_start(
                out=poolB[:],
                in_=swup_d.rearrange("(kc p) m -> p kc m", p=128)),
                waits=dict(relB))
            ldB = q.cnt["gpsimd"]
            smarks = {}
            for i in range(16):
                bank = psA if i % 2 == 0 else psB
                tw = dict(entry)
                tw["gpsimd"] = max(tw.get("gpsimd", 0), ldB)
                if i >= 2:
                    tw["scalar"] = max(tw.get("scalar", 0), smarks[i - 2])

                def mms(e, i=i, bank=bank):
                    for kc in range(8):
                        inst = e.matmul(bank[:, 0:TCH],
                                        poolB[:, kc, i * 128:(i + 1) * 128],
                                        xfb[:, kc, :], start=(kc == 0), stop=(kc == 7))
                    return inst
                q.op("tensor", mms, waits=tw)
                q.op("scalar", lambda e, i=i, bank=bank: e.activation(
                    sgrp[:, i, :], bank[:, 0:TCH], AF.Identity,
                    bias=sbup[:, i:i + 1]), waits={"tensor": q.cnt["tensor"]})
                smarks[i] = q.cnt["scalar"]
            relB = {"tensor": q.cnt["tensor"]}
            sgrp_done = q.cnt["scalar"]
            q.dma("gpsimd", lambda e: e.dma_start(
                out=poolA[:, :, 0:1024],
                in_=swg_d.rearrange("(kc p) m -> p kc m", p=128)),
                waits=dict(relA))
            ldA = q.cnt["gpsimd"]
            vmk = {}
            for i in range(16):
                g = i // 8
                mc = i % 8
                bank = psA if i % 2 == 0 else psB
                tw = dict(entry)
                tw["gpsimd"] = max(tw.get("gpsimd", 0), ldA)
                tw["scalar"] = max(tw.get("scalar", 0), sgrp_done)
                if i >= 2:
                    tw["vector"] = max(tw.get("vector", 0), vmk[i - 2])

                def mmsg(e, g=g, mc=mc, bank=bank):
                    for kc in range(8):
                        inst = e.matmul(bank[:, 0:TCH],
                                        poolA[:, kc, mc * 128:(mc + 1) * 128],
                                        sgrp[:, g * 8 + kc, :],
                                        start=(kc == 0), stop=(kc == 7))
                    return inst
                q.op("tensor", mmsg, waits=tw)
                q.op("scalar", lambda e, i=i, mc=mc, bank=bank: e.activation(
                    scr[:, i % 2, :], bank[:, 0:TCH], AF.Silu,
                    bias=sbg[:, mc:mc + 1]), waits={"tensor": q.cnt["tensor"]})
                q.op("vector", lambda e, i=i: e.tensor_tensor(
                    sg2[:, i, :], scr[:, i % 2, :], sgrp[:, i, :], op=ALU.mult),
                    waits={"scalar": q.cnt["scalar"]})
                vmk[i] = q.cnt["vector"]
            relA = {"tensor": q.cnt["tensor"]}
            sg2_done = q.cnt["vector"]
            q.dma("gpsimd", lambda e: e.dma_start(
                out=poolB[:, :, 0:1024],
                in_=swdn_d.rearrange("(kc p) m -> p kc m", p=128)),
                waits=dict(relB))
            ldB = q.cnt["gpsimd"]
            admk = {}
            for i in range(16):
                g = i // 8
                mc = i % 8
                bank = psA if i % 2 == 0 else psB
                tw = dict(entry)
                tw["gpsimd"] = max(tw.get("gpsimd", 0), ldB)
                tw["vector"] = max(tw.get("vector", 0), sg2_done)
                if i >= 2:
                    tw["scalar"] = max(tw.get("scalar", 0), admk[i - 2])

                def mmsd(e, g=g, mc=mc, bank=bank):
                    for kc in range(8):
                        inst = e.matmul(bank[:, 0:TCH],
                                        poolB[:, kc, mc * 128:(mc + 1) * 128],
                                        sg2[:, g * 8 + kc, :],
                                        start=(kc == 0), stop=(kc == 7))
                    return inst
                q.op("tensor", mmsd, waits=tw)
                q.op("scalar", lambda e, i=i, mc=mc, bank=bank: e.activation(
                    scr[:, i % 2, :], bank[:, 0:TCH], AF.Identity,
                    bias=sbdn[:, mc:mc + 1]), waits={"tensor": q.cnt["tensor"]})
                admk[i] = q.cnt["scalar"]
                q.op("vector", lambda e, i=i, mc=mc: e.tensor_add(
                    acc[:, mc, :], acc[:, mc, :], scr[:, i % 2, :]),
                    waits={"scalar": admk[i]})
            relB = {"tensor": q.cnt["tensor"]}
            for kc in range(8):
                q.op("vector", lambda e, kc=kc: e.tensor_add(
                    xfres[:, kc, :], xfres[:, kc, :], acc[:, kc, :]))
            for kc in range(8):
                q.op("scalar", lambda e, kc=kc: e.activation(
                    scr[:, 0, :], xfres[:, kc, :], AF.Square))

                def mmss(e, kc=kc):
                    return e.matmul(ps1[:, 0:TCH], onescF[:], scr[:, 0, :],
                                    start=(kc == 0), stop=(kc == 7))
                q.op("tensor", mmss)
            q.op("vector", lambda e: e.tensor_scalar_mul(
                ln.stats[:, 0:TCH], ps1[:, 0:TCH], 1.0 / 1024))
            q.op("scalar", lambda e: e.activation(
                ln.stats[:, 0:TCH], ln.stats[:, 0:TCH], AF.Ln, bias=1e-6))
            q.op("scalar", lambda e: e.activation(
                ln.stats[:, 0:TCH], ln.stats[:, 0:TCH], AF.Exp, scale=-0.5))
            q.op("tensor", lambda e: e.matmul(
                psA[:, 0:TCH], ones1f[:], ln.stats[:, 0:TCH], start=True, stop=True))
            q.op("vector", lambda e: e.tensor_copy(ln.mub[:, 0:TCH], psA[:, 0:TCH]))
            for kc in range(8):
                q.op("vector", lambda e, kc=kc: e.tensor_mul(
                    xfres[:, kc, :], xfres[:, kc, :], ln.mub[:, 0:TCH]))
                q.op("vector", lambda e, kc=kc, cyc=cyc: e.tensor_scalar_mul(
                    xfres[:, kc, :], xfres[:, kc, :], nw[:, kc, cyc:cyc + 1]))
            q.dma("sync", lambda e, tc0=tc0: e.dma_start(
                out=xf_dram[:, tc0:tc0 + TCH].rearrange("(kc p) m -> p kc m", p=128),
                in_=xfres[:]))
        p.pop()

    p.push()
    x4t = p.sb("H2_fx4", [128, 8, 512], F32)
    xft = p.sb("H2_fxf", [128, 8, 512], F32)
    for t0 in range(0, W, 512):
        q.dma("sync", lambda e, t0=t0: e.dma_start(
            out=x4t[:], in_=x4_dram[:, t0:t0 + 512].rearrange("(kc p) m -> p kc m", p=128)))
        q.dma("sync", lambda e, t0=t0: e.dma_start(
            out=xft[:], in_=xf_dram[:, t0:t0 + 512].rearrange("(kc p) m -> p kc m", p=128)))
        for kc in range(8):
            q.op("vector", lambda e, kc=kc: e.tensor_add(
                x4t[:, kc, :], x4t[:, kc, :], xft[:, kc, :]))
        ln.emit(x4t, x4t, 8)
        q.dma("sync", lambda e, t0=t0: e.dma_start(
            out=out_d[:, t0:t0 + 512].rearrange("(kc p) m -> p kc m", p=128),
            in_=x4t[:]))
    p.pop()
    return p.finish()



def _fm(a):
    return np.ascontiguousarray(a.T)


def kernel(**inputs):
    import ml_dtypes
    inp = inputs
    x0 = np.asarray(inp["L0_attn_x"], np.float32)

    def run(nc, in_maps):
        res = run_bass_kernel_spmd(nc, in_maps, core_ids=list(range(NCORES)))
        return res.results

    def win(c):
        b, q_ = divmod(c, 4)
        return b, q_ * W

    maps = []
    for c in range(NCORES):
        b, w0 = win(c)
        maps.append({"xA": _fm(x0[b, w0:w0 + W, :]),
                     "W_uv": np.asarray(inp["L0_attn_W_uv"], np.float32)})
    rA = run(build_A(), maps)
    uv = np.zeros((2, T, 2 * D), np.float32)
    for c in range(NCORES):
        b, w0 = win(c)
        uv[b, w0:w0 + W, :] = rA[c]["uvA"].T
    u_full, v_full = uv[..., :D], uv[..., D:]

    maps = []
    for c in range(NCORES):
        sl = slice(c * 128, (c + 1) * 128)
        maps.append({
            "uB": np.ascontiguousarray(u_full[:, :, sl].transpose(0, 2, 1)),
            "vB": np.ascontiguousarray(v_full[:, :, sl].transpose(0, 2, 1)),
            "rpe_in_w": np.asarray(inp["L0_attn_rpe_in_w"], np.float32),
            "rpe_hid_w": np.asarray(inp["L0_attn_rpe_hid_w"], np.float32),
            "rpe_ln_g": np.asarray(inp["L0_attn_rpe_ln_g"], np.float32),
            "rpe_ln_b": np.asarray(inp["L0_attn_rpe_ln_b"], np.float32),
            "rpe_out_w_sl": np.ascontiguousarray(np.asarray(inp["L0_attn_rpe_out_w"], np.float32)[:, sl]),
            "decay_gamma_sl": np.ascontiguousarray(np.asarray(inp["L0_attn_decay_gamma"], np.float32)[sl, None]),
        })
    rB = run(build_B(), maps)
    uo = np.zeros((2, T, D), np.float32)
    for c in range(NCORES):
        sl = slice(c * 128, (c + 1) * 128)
        uo[:, :, sl] = rB[c]["uoB"].transpose(0, 2, 1)

    maps = []
    for c in range(NCORES):
        b, w0 = win(c)
        maps.append({
            "uoC": _fm(uo[b, w0:w0 + W, :]),
            "xC": _fm(x0[b, w0:w0 + W, :]),
            "W_o": np.asarray(inp["L0_attn_W_o"], np.float32),
            "Wg": np.asarray(inp["L0_ffn_Wg"], np.float32),
            "bg": np.asarray(inp["L0_ffn_bg"], np.float32)[:, None],
            "Wt": np.asarray(inp["L0_ffn_Wt"], np.float32),
            "bt": np.asarray(inp["L0_ffn_bt"], np.float32)[:, None],
        })
    rC1 = run(build_C1(), maps)
    x1 = np.zeros((2, T, D), np.float32)
    wgt = np.zeros((2, T, 8), np.float32)
    for c in range(NCORES):
        b, w0 = win(c)
        x1[b, w0:w0 + W, :] = rC1[c]["x1C"].T
        wgt[b, w0:w0 + W, :] = rC1[c]["wC"].T

    b1h = np.ascontiguousarray(np.asarray(inp["L0_ffn_b1"], np.float32).reshape(8, 32, 128).transpose(2, 0, 1))
    b2h = np.ascontiguousarray(np.asarray(inp["L0_ffn_b2"], np.float32).reshape(8, 8, 128).transpose(2, 0, 1))
    maps = []
    for c in range(NCORES):
        b, w0 = win(c)
        maps.append({
            "x1C2": _fm(x1[b, w0:w0 + W, :]),
            "wC2": _fm(wgt[b, w0:w0 + W, :]),
            "ffW1": np.asarray(inp["L0_ffn_W1"], np.float32),
            "ffb1": b1h,
            "ffW2": np.asarray(inp["L0_ffn_W2"], np.float32),
            "ffb2": b2h,
        })
    rC2 = run(build_C2(), maps)
    x15 = np.zeros((2, T, D), np.float32)
    for c in range(NCORES):
        b, w0 = win(c)
        x15[b, w0:w0 + W, :] = rC2[c]["x15C"].T

    W1blk = np.zeros((4, 8, 128, 128), np.float32)
    W2blk = np.zeros((4, 8, 128, 512), np.float32)
    rm32 = np.asarray(inp["L1_ffn_rail_maker"], np.float32)
    rup32 = np.asarray(inp["L1_ffn_rail_up_proj"], np.float32)
    for h in range(4):
        for g in range(8):
            for j in range(8):
                W1blk[h, g, 16 * j:16 * j + 16, 16 * j:16 * j + 16] = rm32[h, 8 * g + j]
                W2blk[h, g, 16 * j:16 * j + 16, 64 * j:64 * j + 64] = rup32[h, 8 * g + j].T
    maps = []
    for c in range(NCORES):
        b, w0 = win(c)
        lo = w0 - (EXT - W)
        xe = np.zeros((D, EXT), np.float32)
        ve = np.zeros((1, EXT), np.float32)
        s0 = max(lo, 0)
        xe[:, s0 - lo:EXT] = x15[b, s0:w0 + W, :].T
        ve[0, s0 - lo:EXT] = 1.0
        maps.append({
            "x15D": xe, "validD": ve,
            "WqT": np.ascontiguousarray(np.asarray(inp["L1_attn_Wq"], np.float32).T),
            "WkT": np.ascontiguousarray(np.asarray(inp["L1_attn_Wk"], np.float32).T),
            "WvT": np.ascontiguousarray(np.asarray(inp["L1_attn_Wv"], np.float32).T),
            "WoT": np.ascontiguousarray(np.asarray(inp["L1_attn_Wo"], np.float32).T),
            "W1blk": W1blk, "W2blk": W2blk,
            "WcpT": np.ascontiguousarray(np.asarray(inp["L1_ffn_w_cproj"], np.float32).T),
        })
    rD = run(build_D(), maps)
    x25 = np.zeros((2, T, D), np.float32)
    for c in range(NCORES):
        b, w0 = win(c)
        x25[b, w0:w0 + W, :] = rD[c]["x25D"][:, 3:].T

    cwh = np.ascontiguousarray(np.asarray(inp["L2_attn_conv_w"], np.float32)[:, 0, :].reshape(16, 128, 4).transpose(1, 0, 2))
    cbh = np.ascontiguousarray(np.asarray(inp["L2_attn_conv_b"], np.float32).reshape(16, 128).T)
    alogh = np.ascontiguousarray(np.asarray(inp["L2_attn_A_log"], np.float32).reshape(16, 16, 128).transpose(2, 1, 0))
    dprmh = np.ascontiguousarray(np.asarray(inp["L2_attn_D_param"], np.float32).reshape(16, 128).T)
    maps = []
    for c in range(NCORES):
        b, w0 = win(c)
        xe = np.zeros((D, 3 + W), np.float32)
        lo = w0 - 3
        s0 = max(lo, 0)
        xe[:, s0 - lo:] = x25[b, s0:w0 + W, :].T
        maps.append({"x25E": xe, "W_in": np.asarray(inp["L2_attn_W_in"], np.float32),
                     "conv_w_h": cwh, "conv_b_h": cbh, "A_log_h": alogh})
    rE = run(build_E(), maps)

    maps = []
    for c in range(NCORES):
        b, w0 = win(c)
        q_ = c % 4
        sprev = np.zeros((3, 128, 16, 16), np.float32)
        for k in range(1, 4):
            if q_ - k >= 0:
                sprev[k - 1] = rE[c - k]["SE"]
        maps.append({
            "ucF": rE[c]["ucE"], "zsF": rE[c]["zsE"],
            "x25F": _fm(x25[b, w0:w0 + W, :]),
            "A_log_h": alogh, "D_param_h": dprmh,
            "W_out": np.asarray(inp["L2_attn_W_out"], np.float32),
            "SprevF": sprev,
            "wgF": np.asarray(inp["L2_ffn_wg"], np.float32),
        })
    rF = run(build_F(), maps)
    x3 = np.zeros((2, T, D), np.float32)
    gsc = np.zeros((2, T, 8), np.float32)
    for c in range(NCORES):
        b, w0 = win(c)
        x3[b, w0:w0 + W, :] = rF[c]["x3F"].T
        gsc[b, w0:w0 + W, :] = rF[c]["gF"].T

    gall = np.ascontiguousarray(gsc.reshape(-1, 8).T)
    maps = []
    for c in range(NCORES):
        b, w0 = win(c)
        maps.append({
            "x3G": _fm(x3[b, w0:w0 + W, :]),
            "gownG": _fm(gsc[b, w0:w0 + W, :]),
            "gallG": gall,
            "moW1": np.asarray(inp["L2_ffn_w1"], np.float32),
            "moW3": np.asarray(inp["L2_ffn_w3"], np.float32),
            "moW2": np.asarray(inp["L2_ffn_w2"], np.float32),
            "WDQT": np.ascontiguousarray(np.asarray(inp["L3_attn_W_DQ"], np.float32).T),
            "WUQT": np.ascontiguousarray(np.asarray(inp["L3_attn_W_UQ"], np.float32).T),
            "WDKVT": np.ascontiguousarray(np.asarray(inp["L3_attn_W_DKV"], np.float32).T),
            "WUKVT": np.ascontiguousarray(np.asarray(inp["L3_attn_W_UKV"], np.float32).T),
            "WQRT": np.ascontiguousarray(np.asarray(inp["L3_attn_W_QR"], np.float32).T),
            "WKRT": np.ascontiguousarray(np.asarray(inp["L3_attn_W_KR"], np.float32).T),
            "qnw": np.asarray(inp["L3_attn_q_norm_w"], np.float32)[:, None],
            "knw": np.asarray(inp["L3_attn_k_norm_w"], np.float32)[:, None],
        })
    rG = run(build_G(), maps)
    x35 = np.zeros((2, T, D), np.float32)
    for c in range(NCORES):
        b, w0 = win(c)
        x35[b, w0:w0 + W, :] = rG[c]["x35G"].T

    bf16 = ml_dtypes.bfloat16
    maps = []
    for c in range(NCORES):
        qs, ks, vs = [], [], []
        for j in range(4):
            pr = 4 * c + j
            b, h = pr // 16, pr % 16
            qs.append(np.concatenate([np.asarray(rG[4 * b + qq]["qHG"][h]) for qq in range(4)], axis=1))
            ks.append(np.concatenate([np.asarray(rG[4 * b + qq]["kHG"][h]) for qq in range(4)], axis=1))
            vs.append(np.concatenate([np.asarray(rG[4 * b + qq]["vHG"][h]) for qq in range(4)], axis=1))
        maps.append({"qH1": np.stack(qs), "kH1": np.stack(ks), "vH1": np.stack(vs)})
    rH1 = run(build_H1(), maps)
    ctx = np.zeros((2, 16, 64, T), bf16)
    for c in range(NCORES):
        for j in range(4):
            pr = 4 * c + j
            b, h = pr // 16, pr % 16
            ctx[b, h] = np.asarray(rH1[c]["ctxH1"][j])

    f32 = np.float32
    bup_h = np.ascontiguousarray(np.asarray(inp["L3_ffn_exp_bup"], f32).reshape(6, 8, 128).transpose(2, 0, 1))
    bgg_h = np.ascontiguousarray(np.asarray(inp["L3_ffn_exp_bg"], f32).reshape(6, 8, 128).transpose(2, 0, 1))
    bdn_h = np.ascontiguousarray(np.asarray(inp["L3_ffn_exp_bdown"], f32).reshape(6, 8, 128).transpose(2, 0, 1))
    sbup_h = np.ascontiguousarray(np.asarray(inp["L3_ffn_share_bup"], f32).reshape(16, 128).T)
    sbg_h = np.ascontiguousarray(np.asarray(inp["L3_ffn_share_bg"], f32).reshape(8, 128).T)
    sbdn_h = np.ascontiguousarray(np.asarray(inp["L3_ffn_share_bdown"], f32).reshape(8, 128).T)
    nw_h = np.ascontiguousarray(np.asarray(inp["L3_ffn_norm_w"], f32).reshape(2, 8, 128).transpose(2, 1, 0))
    maps = []
    for c in range(NCORES):
        b, w0 = win(c)
        ctxfm = np.ascontiguousarray(ctx[b, :, :, w0:w0 + W]).reshape(D, W)
        maps.append({
            "x35H": _fm(x35[b, w0:w0 + W, :]),
            "ctxH2": ctxfm,
            "WOT": np.ascontiguousarray(np.asarray(inp["L3_attn_W_O"], f32).T),
            "routerW": np.asarray(inp["L3_ffn_router_W"], f32),
            "routerB": np.asarray(inp["L3_ffn_router_b"], f32),
            "gateB": np.asarray(inp["L3_ffn_gate_bias"], f32),
            "expWup": np.asarray(inp["L3_ffn_exp_Wup"], f32),
            "expBup": bup_h,
            "expWg": np.asarray(inp["L3_ffn_exp_Wg"], f32),
            "expBg": bgg_h,
            "expWdown": np.asarray(inp["L3_ffn_exp_Wdown"], f32),
            "expBdown": bdn_h,
            "shWup": np.asarray(inp["L3_ffn_share_Wup"], f32),
            "shBup": sbup_h,
            "shWg": np.asarray(inp["L3_ffn_share_Wg"], f32),
            "shBg": sbg_h,
            "shWdown": np.asarray(inp["L3_ffn_share_Wdown"], f32),
            "shBdown": sbdn_h,
            "normW": nw_h,
        })
    rH2 = run(build_H2(), maps)
    out = np.zeros((2, T, D), np.float32)
    for c in range(NCORES):
        b, w0 = win(c)
        out[b, w0:w0 + W, :] = rH2[c]["outH2"].T
    return out
